# version 34
# baseline (speedup 1.0000x reference)
from contextlib import ExitStack

import numpy as np
import ml_dtypes

import concourse.bass as bass
import concourse.mybir as mybir
from concourse.bass_utils import run_bass_kernel_spmd

F32 = mybir.dt.float32
BF16 = mybir.dt.bfloat16
AF = mybir.ActivationFunctionType
ALU = mybir.AluOpType
AX = mybir.AxisListType

S_LEN = 4096
D = 1024
HD = 64
NT = 8
TT = 512
NCHUNK = 8
NG = 4
CLAMP_MAX = float(np.log(1.0 / 0.01))
EPS = 1e-5


class Ctr:
    def __init__(self, st, nc, name):
        self.h = st.enter_context(nc.semaphore(name))
        self.n = 0

    def tick(self, k=1):
        self.n += k
        return self.n


def build_program():
    nc = bass.Bass("TRN2", target_bir_lowering=False)

    xh = nc.dram_tensor("x", [S_LEN, D], F32, kind="ExternalInput")
    xTh = nc.dram_tensor("xT", [D, S_LEN], F32, kind="ExternalInput")
    wqh = nc.dram_tensor("wqT", [D, D], F32, kind="ExternalInput")
    wkh = nc.dram_tensor("wkT", [D, D], F32, kind="ExternalInput")
    wvh = nc.dram_tensor("wvT", [D, D], F32, kind="ExternalInput")
    woh = nc.dram_tensor("woT", [D, D], F32, kind="ExternalInput")
    Wh = nc.dram_tensor("WT", [D, D], F32, kind="ExternalInput")
    Wph = nc.dram_tensor("WpT", [D, 256], F32, kind="ExternalInput")
    Wdh = nc.dram_tensor("WdTp", [D, 32], F32, kind="ExternalInput")
    Wuh = nc.dram_tensor("WuTp", [32, D], F32, kind="ExternalInput")
    simh = nc.dram_tensor("simT", [4, 256], F32, kind="ExternalInput")
    temph = nc.dram_tensor("tempv", [1, 1], F32, kind="ExternalInput")
    bph = nc.dram_tensor("bp_bc", [128, 256], F32, kind="ExternalInput")
    cosmh = nc.dram_tensor("cosm", [128, S_LEN], F32, kind="ExternalInput")
    sinmh = nc.dram_tensor("sinm", [128, S_LEN], F32, kind="ExternalInput")
    identh = nc.dram_tensor("identb", [128, 128], BF16, kind="ExternalInput")
    rmath = nc.dram_tensor("rmatb", [128, 128], BF16, kind="ExternalInput")
    hsumh = nc.dram_tensor("headsum", [128, 128], BF16, kind="ExternalInput")
    maskh = nc.dram_tensor("masksb", [128, 896], BF16, kind="ExternalInput")
    onesbh = nc.dram_tensor("onesb", [1, 128], BF16, kind="ExternalInput")
    onesfh = nc.dram_tensor("onesf", [1, 128], F32, kind="ExternalInput")
    selh = nc.dram_tensor("selm", [128, 256], BF16, kind="ExternalInput")
    out_h = nc.dram_tensor("o", [S_LEN, D], F32, kind="ExternalOutput")

    QS = nc.dram_tensor("QS", [NT, 128, 4096], BF16, kind="Internal")
    KS = nc.dram_tensor("KS", [NT, NCHUNK, 128, 512], BF16, kind="Internal")
    VS = nc.dram_tensor("VS", [32, 128, 1040], BF16, kind="Internal")
    CHD = nc.dram_tensor("CHD", [1, 16], F32, kind="Internal")

    with ExitStack() as st:
        def sb(name, shape, dt):
            return st.enter_context(nc.sbuf_tensor(name, shape, dt))

        def mkpsum(name, shape, dt):
            return st.enter_context(nc.psum_tensor(name, shape, dt))

        id_sb = sb("id_sb", [128, 128], BF16)
        rm_sb = sb("rm_sb", [128, 128], BF16)
        hs_sb = sb("hs_sb", [128, 128], BF16)
        msk_sb = sb("msk_sb", [128, 896], BF16)
        ones_sb = sb("ones_sb", [1, 128], BF16)
        onesf_sb = sb("onesf_sb", [1, 128], F32)
        bpb_sb = sb("bpb_sb", [128, 256], F32)
        simT_sb = sb("simT_sb", [4, 256], F32)
        smq_sb = sb("smq_sb", [4, 256], F32)
        simbf_sb = sb("simbf_sb", [4, 256], BF16)
        temp_sb = sb("temp_sb", [1, 1], F32)
        simn_sb = sb("simn_sb", [128, 8], BF16)
        chrow_sb = sb("chrow_sb", [1, 16], F32)
        CB_sb = sb("CB_sb", [128, 16], F32)
        CBP_sb = sb("CBP_sb", [128, 8], F32)
        chp_sb = sb("chp_sb", [1, 8], F32)
        qmax2_sb = sb("qmax2_sb", [16, 1], F32)
        kmax2_sb = sb("kmax2_sb", [16, 1], F32)
        qmx_sb = sb("qmx_sb", [16, 1], F32)
        kmx_sb = sb("kmx_sb", [16, 1], F32)
        ch_sb = sb("ch_sb", [16, 1], F32)
        n2t_sb = sb("n2t_sb", [16, 2], F32)
        WpT_sb = sb("WpT_sb", [128, 2048], BF16)
        WdT_sb = sb("WdT_sb", [128, 256], BF16)
        WuT_sb = sb("WuT_sb", [32, 1024], BF16)
        lnp = sb("lnp", [128, 64], F32)
        rtp = sb("rtp", [128, 16], F32)
        lw_sb = sb("lw_sb", [128, 4], F32)
        g4_sb = sb("g4_sb", [128, 16], F32)
        acc_sb = sb("acc_sb", [128, 20], F32)
        spc_sb = sb("spc_sb", [128, 4], F32)
        rbc_sb = sb("rbc_sb", [64, 1024], BF16)
        dstf = sb("dstf", [128, 512], F32)
        rrv4b = sb("rrv4b", [128, 512], BF16)
        sel_sb = sb("sel_sb", [128, 256], BF16)
        projf = sb("projf", [128, 1024], F32)
        xs_sb = sb("xs_sb", [128, 1024], F32)
        VpT = sb("VpT", [128, 2080], BF16)
        sqA = sb("sqA", [128, 1024], BF16)
        t1_sb = sb("t1_sb", [128, 1024], BF16)
        t2_sb = sb("t2_sb", [128, 1024], BF16)

        w1 = sb("w1", [128, 8192], BF16)
        w2 = sb("w2", [128, 8192], BF16)
        w3 = sb("w3", [128, 8192], BF16)
        m1 = sb("m1", [128, 4096], BF16)
        m2 = sb("m2", [128, 4096], BF16)
        m3 = sb("m3", [128, 4096], BF16)
        m4 = sb("m4", [128, 4096], BF16)
        m5 = sb("m5", [128, 4096], BF16)
        m6 = sb("m6", [128, 4096], BF16)
        m7 = sb("m7", [128, 4096], BF16)
        m8 = sb("m8", [128, 4096], BF16)
        m9 = sb("m9", [128, 4096], BF16)
        B1 = sb("B1", [128, 4096], F32)
        B2 = sb("B2", [128, 4096], F32)

        pF = [mkpsum(f"pF{i}", [128, 512], F32) for i in range(4)]
        pSA = mkpsum("pSA", [128, 1024], F32)
        pSB = mkpsum("pSB", [128, 1024], F32)
        pS2 = [pSA, pSB]
        TRA = pSA[:, 0:512].bitcast(BF16)
        TRB = pSB[:, 512:1024].bitcast(BF16)

        c_pe = Ctr(st, nc, "c_pe")
        c_dve = Ctr(st, nc, "c_dve")
        c_act = Ctr(st, nc, "c_act")
        c_pool = Ctr(st, nc, "c_pool")
        d_ld = Ctr(st, nc, "d_ld")
        d_xt = Ctr(st, nc, "d_xt")
        d_scw = Ctr(st, nc, "d_scw")
        d_q = Ctr(st, nc, "d_q")
        d_k = Ctr(st, nc, "d_k")
        d_v = Ctr(st, nc, "d_v")
        d_x = Ctr(st, nc, "d_x")
        d_out = Ctr(st, nc, "d_out")
        c_ph = Ctr(st, nc, "c_ph")
        ENG = {"t": c_pe, "v": c_dve, "c": c_act, "g": c_pool}

        prog = {k: [] for k in "tvcgy"}

        def em(q, fn):
            prog[q].append(fn)

        def wt(q, ctr, val):
            assert val <= ctr.n, f"wait ahead of producer: {val} > {ctr.n} on {q}"
            if val > 0:
                em(q, lambda e, h=ctr.h, v=val: e.wait_ge(h, v))

        def inc(q, ctr, k=1):
            last = prog[q][-1]

            def fn(e, last=last, h=ctr.h, k=k):
                last(e).then_inc(h, k)
            prog[q][-1] = fn
            return ctr.tick(k)

        def vsp():
            em("v", lambda e: e.tensor_scalar_mul(
                spc_sb[:, 0:1], spc_sb[:, 1:2], 1.0))

        def vinc_safe():
            vsp()
            return inc("v", c_dve)

        def asp():
            em("c", lambda e: e.mul(spc_sb[:, 2:3], spc_sb[:, 3:4], 1.0))

        def ainc_safe():
            asp()
            return inc("c", c_act)

        def mm(out, lhsT, rhs, start=True, stop=True):
            em("t", lambda e, o=out, l=lhsT, r=rhs, s1=start, s2=stop:
               e.matmul(o, l, r, start=s1, stop=s2))

        def tr(out, in_):
            kp = in_.shape[0]
            em("t", lambda e, o=out, i=in_, k=kp: e.transpose(
                o, i, id_sb[0:k, 0:k]))

        for dst, src in [
            (id_sb[:, :], identh[:, :]), (rm_sb[:, :], rmath[:, :]),
            (hs_sb[:, :], hsumh[:, :]), (msk_sb[:, :], maskh[:, :]),
            (ones_sb[:, :], onesbh[:, :]), (onesf_sb[:, :], onesfh[:, :]),
            (sel_sb[:, :], selh[:, :]),
            (bpb_sb[:, :], bph[:, :]), (simT_sb[:, :], simh[:, :]),
            (temp_sb[:, :], temph[:, :]),
        ]:
            em("y", lambda e, d=dst, s=src: e.dma_start(out=d, in_=s))
            inc("y", d_ld, 16)
        ld_plain = d_ld.n

        for dst, src in [
            (w1[:, :].rearrange("p (c o) -> p c o", o=1024),
             wqh[:, :].rearrange("(c p) o -> p c o", p=128)),
            (w2[:, :].rearrange("p (c o) -> p c o", o=1024),
             wkh[:, :].rearrange("(c p) o -> p c o", p=128)),
            (w3[:, :].rearrange("p (c o) -> p c o", o=1024),
             wvh[:, :].rearrange("(c p) o -> p c o", p=128)),
            (m1[:, :], cosmh[:, :]),
            (m2[:, :], sinmh[:, :]),
            (WpT_sb[:, :].rearrange("p (c o) -> p c o", o=256),
             Wph[:, :].rearrange("(c p) o -> p c o", p=128)),
            (WdT_sb[:, :].rearrange("p (c o) -> p c o", o=32),
             Wdh[:, :].rearrange("(c p) o -> p c o", p=128)),
            (WuT_sb[:, :], Wuh[:, :]),
        ]:
            em("g", lambda e, d=dst, s=src: e.dma_start(out=d, in_=s))
            inc("g", d_ld, 16)
        ld_all = d_ld.n

        em("g", lambda e: e.memset(spc_sb[:, :], 0.0))
        em("g", lambda e: e.memset(qmax2_sb[:, :], 0.0))
        em("g", lambda e: e.memset(kmax2_sb[:, :], 0.0))
        em("g", lambda e: e.memset(
            bass.AP(VpT, 64, [[2080, 128], [65, 32], [1, 1]]), 1.0))
        em("g", lambda e: e.memset(dstf[:, :], 1.0))
        memset_done = inc("g", c_pool)

        em("g", lambda e: e.dma_start(
            out=m3[:, :].rearrange("p (c t) -> p c t", t=TT),
            in_=xTh[:, 0:TT].rearrange("(c p) t -> p c t", p=128)))
        inc("g", d_xt, 16)

        wt("c", d_ld, ld_all)
        em("c", lambda e: e.activation(smq_sb[:, :], simT_sb[:, :], AF.Square))
        sm1 = inc("c", c_act)
        wt("v", c_act, sm1)
        em("v", lambda e: e.reduce_sum(rtp[0:4, 0:1], smq_sb[:, :], AX.X))
        sm2 = vinc_safe()
        wt("c", c_dve, sm2)
        em("c", lambda e: e.activation(rtp[0:4, 1:2], rtp[0:4, 0:1], AF.Sqrt))
        sm3 = ainc_safe()
        wt("v", c_act, sm3)
        em("v", lambda e: e.tensor_scalar_max(rtp[0:4, 1:2], rtp[0:4, 1:2], 1e-12))
        vsp()
        em("v", lambda e: e.reciprocal(rtp[0:4, 2:3], rtp[0:4, 1:2]))
        em("v", lambda e: e.tensor_scalar_min(rtp[0:1, 3:4], temp_sb[:, :], CLAMP_MAX))
        sm4 = vinc_safe()
        wt("c", c_dve, sm4)
        em("c", lambda e: e.activation(rtp[0:1, 4:5], rtp[0:1, 3:4], AF.Exp))
        sm5 = ainc_safe()
        wt("t", c_act, sm5)
        mm(pSB[0:4, 0:1], onesf_sb[0:1, 0:4], rtp[0:1, 4:5])
        sm6 = inc("t", c_pe)
        wt("v", c_pe, sm6)
        em("v", lambda e: e.tensor_copy(rtp[0:4, 5:6], pSB[0:4, 0:1]))
        vsp()
        em("v", lambda e: e.tensor_mul(rtp[0:4, 6:7], rtp[0:4, 2:3], rtp[0:4, 5:6]))
        vsp()
        em("v", lambda e: e.tensor_scalar_mul(
            simbf_sb[:, :], simT_sb[:, :], rtp[0:4, 6:7]))
        sm7 = vinc_safe()
        wt("t", c_dve, sm7)
        wt("t", d_ld, ld_plain)
        tr(TRB[0:128, 0:4], simbf_sb[0:4, 0:128])
        tr(TRB[0:128, 4:8], simbf_sb[0:4, 128:256])
        sm8 = inc("t", c_pe)
        wt("v", c_pe, sm8)
        em("v", lambda e: e.tensor_copy(simn_sb[:, :], TRB[0:128, 0:8]))
        lastpTR_read = inc("v", c_dve)
        sm7x = lastpTR_read

        xt = [m3, m4]
        ck_qkv_last = {}
        ck_evac = []
        ck_qpl, ck_kpl, ck_vsb = {}, {}, {}
        ck_rope, ck_t2, ck_addidx, ck_sq, ck_t1 = {}, {}, {}, {}, {}
        ck_n2done = {}
        rec_vp = [0, 0]
        rec_qtb = 0
        rec_ktb = 0
        gidx = 0
        ridx = 0
        tidx = 0
        ck_hsq_last = 0
        ck_hsk_last = 0
        ck_n2done_prev = 0
        ck_vtr_last = {c: 0 for c in range(NCHUNK)}
        hs_byc = {}

        for tc in range(NT):
            par = tc % 2
            if tc + 1 < NT:
                if tc >= 1:
                    wt("g", c_pe, ck_qkv_last[tc - 1])
                em("g", lambda e, t=tc + 1: e.dma_start(
                    out=xt[t % 2][:, :].rearrange("p (c t) -> p c t", t=TT),
                    in_=xTh[:, t * TT:(t + 1) * TT].rearrange(
                        "(c p) t -> p c t", p=128)))
                inc("g", d_xt, 16)
            xt_ld = 16 * (tc + 1)

            first_mm_of_tile = True
            for w in range(3):
                wsb = [w1, w2, w3][w]
                for oc in range(NCHUNK):
                    bank = pF[gidx % 4]
                    if gidx >= 4:
                        engq, cnt = ck_evac[gidx - 4]
                        wt("t", ENG[engq], cnt)
                    if first_mm_of_tile:
                        wt("t", d_xt, xt_ld)
                        first_mm_of_tile = False
                    for c in range(NCHUNK):
                        mm(bank[:, :],
                           wsb[:, 1024 * c + 128 * oc:1024 * c + 128 * oc + 128],
                           xt[par][:, TT * c:TT * (c + 1)],
                           start=(c == 0), stop=(c == NCHUNK - 1))
                    g_ck = inc("t", c_pe)
                    if w == 2 and oc == NCHUNK - 1:
                        ck_qkv_last[tc] = g_ck
                    if w == 0:
                        wt("c", c_pe, g_ck)
                        if tc > 0:
                            wt("c", c_dve, ck_rope[("q", tc - 1, oc)])
                            wt("c", c_pe, ck_hsq_last)
                        em("c", lambda e, b=bank, o=oc: e.activation(
                            m5[:, TT * o:TT * (o + 1)], b[:, :], AF.Copy,
                            scale=0.125))
                        ck_qpl[(tc, oc)] = inc("c", c_act)
                        ck_evac.append(("c", ck_qpl[(tc, oc)]))
                    elif w == 1:
                        wt("c", c_pe, g_ck)
                        if tc > 0:
                            wt("c", c_dve, ck_rope[("k", tc - 1, oc)])
                            wt("c", c_pe, ck_hsk_last)
                        em("c", lambda e, b=bank, o=oc: e.activation(
                            m6[:, TT * o:TT * (o + 1)], b[:, :], AF.Copy,
                            scale=1.0))
                        ck_kpl[(tc, oc)] = inc("c", c_act)
                        ck_evac.append(("c", ck_kpl[(tc, oc)]))
                    else:
                        wt("v", c_pe, g_ck)
                        if tc > 0:
                            wt("v", c_pe, ck_vtr_last[oc])
                        em("v", lambda e, b=bank, o=oc: e.tensor_copy(
                            m9[:, TT * o:TT * (o + 1)], b[:, :]))
                        ck_vsb[(tc, oc)] = inc("v", c_dve)
                        ck_evac.append(("v", ck_vsb[(tc, oc)]))
                    gidx += 1

            for which, pl, tgt, n2bank, ckd in (
                    ("q", m5, m7, pSB[:, 0:512], ck_qpl),
                    ("k", m6, m8, pSB[:, 0:512], ck_kpl)):
                scw_gate = rec_qtb if which == "q" else rec_ktb
                for c in range(NCHUNK):
                    rb = pSA[:, 512 * (ridx % 2):512 * (ridx % 2 + 1)]
                    ep = ckd[(tc, c)]
                    wt("t", c_act, ep)
                    if ridx >= 2:
                        wt("t", c_dve, ck_t2[ridx - 2])
                    mm(rb, rm_sb[:, :], pl[:, TT * c:TT * (c + 1)])
                    rot_ck = inc("t", c_pe)
                    tslot = tidx % 2
                    wt("g", c_act, ep)
                    if tidx >= 2:
                        wt("g", c_dve, ck_addidx[tidx - 2])
                    em("g", lambda e, p=pl, c_=c, s=tslot, t=tc: e.tensor_tensor(
                        t1_sb[:, TT * s:TT * (s + 1)],
                        p[:, TT * c_:TT * (c_ + 1)],
                        m1[:, TT * t:TT * (t + 1)], ALU.mult))
                    ck_t1[tidx] = inc("g", c_pool)
                    wt("v", c_pe, rot_ck)
                    em("v", lambda e, b=rb, s=tslot, t=tc: e.tensor_tensor(
                        t2_sb[:, TT * s:TT * (s + 1)], b,
                        m2[:, TT * t:TT * (t + 1)], ALU.mult))
                    ck_t2[ridx] = inc("v", c_dve)
                    wt("v", c_pool, ck_t1[tidx])
                    wt("v", d_scw, scw_gate)
                    em("v", lambda e, t_=tgt, c_=c, s=tslot: e.tensor_add(
                        t_[:, TT * c_:TT * (c_ + 1)],
                        t1_sb[:, TT * s:TT * (s + 1)],
                        t2_sb[:, TT * s:TT * (s + 1)]))
                    ck_rope[(which, tc, c)] = inc("v", c_dve)
                    ck_addidx[tidx] = ck_rope[(which, tc, c)]
                    ridx += 1
                    tidx += 1
                    sslot = c % 2
                    wt("c", c_act, ep)
                    key2 = (which, c - 2) if c >= 2 else (
                        ("q", NCHUNK - 2 + c) if which == "k" else None)
                    if key2 is not None and (key2[1] >= 0):
                        prev = hs_byc.get(key2)
                        if prev:
                            wt("c", c_pe, prev)
                    em("c", lambda e, p=pl, c_=c, s=sslot: e.activation(
                        sqA[:, TT * s:TT * (s + 1)],
                        p[:, TT * c_:TT * (c_ + 1)], AF.Square))
                    ck_sq[(which, tc, c)] = inc("c", c_act)
                    wt("t", c_act, ck_sq[(which, tc, c)])
                    if c == 0:
                        wt("t", c_dve, ck_n2done_prev)
                        if tc == 0 and which == "q":
                            wt("t", c_dve, sm7x)
                    mm(n2bank[0:16, 0:512], hs_sb[:, 16 * c:16 * c + 16],
                       sqA[:, TT * sslot:TT * (sslot + 1)],
                       start=(c == 0), stop=(c == NCHUNK - 1))
                    hs_ck = inc("t", c_pe)
                    hs_byc[(which, c)] = hs_ck
                if which == "q":
                    ck_hsq_last = hs_ck
                else:
                    ck_hsk_last = hs_ck
                wt("v", c_pe, hs_ck)
                if tc == 0 and which == "q":
                    wt("v", c_pool, memset_done)
                tgt_max = qmax2_sb if which == "q" else kmax2_sb
                em("v", lambda e: e.reduce_max(
                    n2t_sb[0:16, 0:1], pSB[0:16, 0:512], AX.X))
                vsp()
                em("v", lambda e, t_=tgt_max: e.tensor_max(
                    t_[:, :], t_[:, :], n2t_sb[0:16, 0:1]))
                ck_n2done_prev = vinc_safe()
            ck_n2done[tc] = ck_n2done_prev

            for j in range(4):
                for c in range(NCHUNK):
                    wt("t", c_dve, ck_vsb[(tc, c)])
                    if c == 0:
                        wt("t", c_dve, lastpTR_read)
                        if tc == 0 and j == 0:
                            wt("t", c_pool, memset_done)
                    tr(TRB[:, 128 * c:128 * (c + 1)],
                       m9[:, TT * c + 128 * j:TT * c + 128 * (j + 1)])
                vtr_ck = inc("t", c_pe)
                for c in range(NCHUNK):
                    ck_vtr_last[c] = vtr_ck
                vslot = j % 2
                wt("v", c_pe, vtr_ck)
                wt("v", d_scw, rec_vp[vslot])
                em("v", lambda e, s=vslot: e.tensor_copy(
                    bass.AP(VpT, 1040 * s, [[2080, 128], [130, 8], [65, 2], [1, 64]]),
                    TRB.rearrange("p (c d i) -> p c d i", d=2, i=64)))
                vev = inc("v", c_dve)
                lastpTR_read = vev
                wt("y", c_dve, vev)
                em("y", lambda e, s=vslot, t=tc, j_=j: e.dma_start(
                    out=VS[4 * t + j_, :, :],
                    in_=VpT[:, 1040 * s:1040 * (s + 1)]))
                rec_vp[vslot] = inc("y", d_scw, 16)

            wt("y", c_dve, ck_rope[("q", tc, NCHUNK - 1)])
            em("y", lambda e, t=tc: e.dma_start(out=QS[t, :, :], in_=m7[:, :]))
            rec_qtb = inc("y", d_scw, 16)
            wt("y", c_dve, ck_rope[("k", tc, NCHUNK - 1)])
            em("y", lambda e, t=tc: e.dma_start(
                out=KS[t].rearrange("c p i -> p c i"),
                in_=m8[:, :].rearrange("p (c i) -> p c i", i=512)))
            rec_ktb = inc("y", d_scw, 16)

        wt("c", c_dve, ck_n2done[NT - 1])
        em("c", lambda e: e.activation(qmx_sb[:, :], qmax2_sb[:, :], AF.Sqrt))
        em("c", lambda e: e.activation(kmx_sb[:, :], kmax2_sb[:, :], AF.Sqrt))
        cbm = ainc_safe()
        wt("v", c_act, cbm)
        em("v", lambda e: e.tensor_mul(ch_sb[:, :], qmx_sb[:, :], kmx_sb[:, :]))
        cb2 = vinc_safe()
        wt("y", c_dve, cb2)
        em("y", lambda e: e.dma_start(
            out=CHD[:, :].rearrange("a b -> b a"), in_=ch_sb[:, :]))
        cb3 = inc("y", d_scw, 16)
        wt("y", d_scw, cb3)
        em("y", lambda e: e.dma_start(out=chrow_sb[:, :], in_=CHD[:, :]))
        cb4 = inc("y", d_ld, 16)
        wt("v", d_ld, cb4)
        em("v", lambda e: e.tensor_tensor(
            chp_sb[:, :],
            bass.AP(chrow_sb, 0, [[16, 1], [2, 8]]),
            bass.AP(chrow_sb, 1, [[16, 1], [2, 8]]), ALU.max))
        cbp1 = vinc_safe()
        wt("t", c_dve, cbp1)
        mm(pSB[0:128, 0:8], onesf_sb[:, :], chp_sb[:, :])
        cb5 = inc("t", c_pe)
        wt("v", c_pe, cb5)
        em("v", lambda e: e.tensor_scalar_mul(CBP_sb[:, :], pSB[0:128, 0:8], -1.0))
        ck_cb = vinc_safe()

        for q in "tvcgy":
            em(q, lambda e: e.nop())
            last = prog[q][-1]

            def fn(e, last=last):
                last(e).then_inc(c_ph.h, 1)
            prog[q][-1] = fn
        c_ph.tick(5)
        for q in "tvcgy":
            em(q, lambda e: e.wait_ge(c_ph.h, 5))

        em("g", lambda e: e.dma_start(
            out=w2[:, :].rearrange("p (c o) -> p c o", o=1024),
            in_=Wh[:, :].rearrange("(c p) o -> p c o", p=128)))
        inc("g", d_ld, 16)
        em("g", lambda e: e.dma_start(
            out=w3[:, :].rearrange("p (c o) -> p c o", o=1024),
            in_=woh[:, :].rearrange("(c p) o -> p c o", p=128)))
        ld_b = inc("g", d_ld, 16)

        def PT4(hi, par):
            return m5[:, 1024 * (2 * par + hi // 2) + 512 * (hi % 2):
                      1024 * (2 * par + hi // 2) + 512 * (hi % 2) + 512]

        rrv = [bass.AP(m7, 1024 + 512 * s, [[4096, 1], [1, 512]]) for s in range(2)]
        sqs_v = m7[:, 2048:2560]
        sqp_v = m7[:, 2560:2816]
        hT_off = 3072

        ck_pv = {}
        ck_mrg_last = {hi: 0 for hi in range(4)}
        ck_pre1 = {}
        ck_pre1_slot = {0: 0, 1: 0}
        ck_pre2_slot = {0: 0, 1: 0}
        ck_pre2_last = 0
        ck_out = 0
        ck_x1bf_rd = 0
        ck_base_last = 0
        ck_rbc_last = 0
        ck_hbf_last = 0
        ck_dve_wp_rd = 0
        ck_dve_lg_rd = 0
        ck_lw_last = 0
        ck_lnp78_rd = 0
        lnp_guard = {j: 0 for j in range(4)}
        pre2_cks = {}
        ck_wo_last = 0
        lastV_ck = [0, 0]
        kslotQK = [0, 0]
        pt_last_par = {0: {}, 1: {}}
        rr_last_bc = [0, 0]
        kg_ck = {}
        vw_ck = {}
        x_ck = {}
        spidx = 0
        exp_sp_cks = {}
        qk_cks = {}
        ck_state = {"last_rec": 0, "hbf_for_bc": 0}
        ck_exp_last = 0
        prev_qk_m1 = 0

        def kg_views(g):
            if g % 2 == 0:
                return (w1, 0, 8192, w1, 4096, 8192)
            return (m8, 0, 4096, m9, 0, 4096)

        def emit_kg_load(qc, g):
            tA, bA, rA, tB, bB, rB = kg_views(g)
            wt("y", c_pe, kslotQK[g % 2])
            for (tt, bb, c) in ((tA, bA, 2 * g), (tB, bB, 2 * g + 1)):
                em("y", lambda e, t_=tt, b_=bb, c_=c, q=qc: e.dma_start(
                    out=t_[:, b_:b_ + (q + 1) * 512].rearrange(
                        "p (tc i) -> p tc i", i=512),
                    in_=KS[0:q + 1, c_].rearrange("tc p i -> p tc i")))
                inc("y", d_k, 16)
            kg_ck[(qc, g)] = d_k.n

        def emit_vw_load(qc, g, w):
            vs = w % 2
            wt("y", c_pe, lastV_ck[vs])
            em("y", lambda e, w_=w, g_=g, s=vs: e.dma_start(
                out=m6[:, 1040 * s:1040 * (s + 1)].rearrange(
                    "p (kc i) -> p kc i", i=260),
                in_=VS[4 * w_:4 * (w_ + 1), :, 260 * g_:260 * (g_ + 1)]
                .rearrange("kc p i -> p kc i")))
            vw_ck[(g, w)] = inc("y", d_v, 16)

        def emit_x_load(qc, idx):
            j, oh2 = idx // 2, idx % 2
            s = idx % 2
            wt("y", c_dve, ck_pre1_slot[s])
            em("y", lambda e, q=qc, j_=j, o_=oh2, s_=s: e.dma_start(
                out=xs_sb[:, 512 * s_:512 * (s_ + 1)],
                in_=xh[q * 512 + 128 * j_:q * 512 + 128 * (j_ + 1),
                       512 * o_:512 * (o_ + 1)]))
            x_ck[(qc, idx)] = inc("y", d_x, 16)

        qt_cks = {}
        glast_qk = {g: 0 for g in range(NG)}

        def emit_qt_slice(qc, g):
            wt("y", c_pe, glast_qk[g])
            em("y", lambda e, q=qc, g_=g: e.dma_start(
                out=m1[:, 1024 * g_:1024 * (g_ + 1)],
                in_=QS[q, :, 1024 * g_:1024 * (g_ + 1)]))
            qt_cks[(qc, g)] = inc("y", d_q, 16)

        def emit_qc_head(qc):
            emit_kg_load(qc, 0)
            emit_kg_load(qc, 1)

        for qc in range(NT):
            nkc = 4 * (qc + 1)
            nwin = qc + 1
            if qc == 0:
                emit_qc_head(0)
                for g_ in range(NG):
                    emit_qt_slice(0, g_)
            first_qk_of_qc = True
            emit_x_load(qc, 0)
            emit_x_load(qc, 1)

            pending_epi = []
            pending_tailpv = None

            def make_epi_chunks(gp, pvlast, dep_wo):
                st_ = {}

                def chunkA():
                    wt("v", c_pe, max(pvlast[0], pvlast[1]))
                    em("v", lambda e: e.tensor_copy(dstf[0:1, :],
                                                    pF[0][64:65, :]))
                    em("v", lambda e: e.tensor_copy(dstf[32:33, :],
                                                    pF[1][64:65, :]))
                    vinc_safe()
                    wt("c", c_pe, max(pvlast[2], pvlast[3]))
                    if ck_state["last_rec"]:
                        wt("c", c_dve, ck_state["last_rec"])
                    else:
                        wt("c", c_pool, memset_done)
                    em("c", lambda e: e.copy(dstf[64:65, :], pF[2][64:65, :]))
                    em("c", lambda e: e.copy(dstf[96:97, :], pF[3][64:65, :]))
                    ev_c = ainc_safe()
                    wt("v", c_act, ev_c)
                    wt("v", c_pe, max(rr_last_bc))
                    em("v", lambda e: _recip_bf(nc, e, rrv4b[0:97, 0:256],
                                                dstf[0:97, 0:256]))
                    em("v", lambda e: e.tensor_copy(
                        dstf[0:1, 0:1], rrv4b[96:97, 255:256]))
                    vsp()
                    vsp()
                    st_["recL"] = vinc_safe()
                    em("v", lambda e: _recip_bf(nc, e, rrv4b[0:97, 256:512],
                                                dstf[0:97, 256:512]))
                    em("v", lambda e: e.tensor_copy(
                        dstf[0:1, 0:1], rrv4b[96:97, 511:512]))
                    vsp()
                    vsp()
                    st_["rec"] = vinc_safe()
                    ck_state["last_rec"] = st_["rec"]

                def chunkB():
                    wt("t", c_dve, st_["recL"])
                    bcs = []
                    for hi in range(4):
                        mm(pF[hi][64:128, 0:256],
                           sel_sb[0:97, 64 * hi:64 * hi + 64],
                           rrv4b[0:97, 0:256])
                        bcs.append(inc("t", c_pe))
                    wt("t", c_dve, st_["rec"])
                    for hi in range(4):
                        mm(pF[hi][64:128, 256:512],
                           sel_sb[0:97, 64 * hi:64 * hi + 64],
                           rrv4b[0:97, 256:512])
                        bcs.append(inc("t", c_pe))
                    st_["bc"] = [bcs[4], bcs[5], bcs[6], bcs[7]]
                    rr_last_bc[0] = bcs[7]
                    rr_last_bc[1] = bcs[7]

                def chunkC():
                    wt("v", c_pe, st_["bc"][1])
                    em("v", lambda e: e.tensor_copy(rbc_sb[:, 0:512],
                                                    pF[0][64:128, :]))
                    em("v", lambda e: e.tensor_copy(rbc_sb[:, 512:1024],
                                                    pF[1][64:128, :]))
                    vsp()
                    if dep_wo:
                        wt("v", c_pe, dep_wo)
                    for hi in (0, 1):
                        h = 4 * gp + hi
                        em("v", lambda e, p=hi, c_=h // 2, pr=h % 2: e.tensor_mul(
                            bass.AP(m2, 64 * pr * 4096 + 512 * c_,
                                    [[4096, 64], [1, 512]]),
                            pF[p][0:64, :], rbc_sb[:, 512 * p:512 * (p + 1)]))
                        ck_mrg_last[hi] = inc("v", c_dve)

                def chunkD():
                    wt("v", c_pe, st_["bc"][3])
                    em("v", lambda e: e.tensor_copy(rbc_sb[:, 0:512],
                                                    pF[2][64:128, :]))
                    em("v", lambda e: e.tensor_copy(rbc_sb[:, 512:1024],
                                                    pF[3][64:128, :]))
                    vsp()
                    for hi in (2, 3):
                        h = 4 * gp + hi
                        em("v", lambda e, p=hi, c_=h // 2, pr=h % 2: e.tensor_mul(
                            bass.AP(m2, 64 * pr * 4096 + 512 * c_,
                                    [[4096, 64], [1, 512]]),
                            pF[p][0:64, :],
                            rbc_sb[:, 512 * (p - 2):512 * (p - 1)]))
                        ck_mrg_last[hi] = inc("v", c_dve)

                return [("A", chunkA), ("B", chunkB),
                        ("C", chunkC), ("D", chunkD)]

            for g in range(NG):
                tA, bA, rA, tB, bB, rB = kg_views(g)
                first_qk_of_group = True
                exp_pair = {}
                msk_par = {}
                pair_sp = {}
                if pending_tailpv:
                    tail_to_emit = pending_tailpv
                    pending_tailpv = None
                else:
                    tail_to_emit = None
                for kc in range(nkc):
                    par = kc % 2
                    diag_k = (kc // 4 == qc)
                    for p in range(2):
                        need = spidx - 2
                        if need >= 0 and need in exp_sp_cks:
                            wt("t", c_act, exp_sp_cks[need])
                        if first_qk_of_group:
                            wt("t", d_k, kg_ck[(qc, g)])
                            wt("t", d_q, qt_cks[(qc, g)])
                            first_qk_of_group = False
                        sb_pair = pS2[spidx % 2]
                        if first_qk_of_qc:
                            wt("t", c_dve, lastpTR_read)
                            first_qk_of_qc = False
                        for sub in range(2):
                            hi = 2 * p + sub
                            h = 4 * g + hi
                            c = h // 2
                            hp = h % 2
                            if c == 2 * g:
                                kt, kb, krl = tA, bA, rA
                            else:
                                kt, kb, krl = tB, bB, rB
                            lhsT = bass.AP(kt, 64 * hp * krl + kb + 128 * kc,
                                           [[krl, 64], [1, 128]])
                            rhs = bass.AP(m1, 64 * hp * 4096 + 512 * c,
                                          [[4096, 64], [1, 512]])
                            mm(sb_pair[:, 512 * sub:512 * (sub + 1)], lhsT, rhs)
                            qk = inc("t", c_pe)
                            if kc == nkc - 1 and hi == 3:
                                kslotQK[g % 2] = qk
                                prev_qk_m1 = qk
                                glast_qk[g] = qk
                        pair_sp[(kc, p)] = spidx
                        qk_cks[spidx] = qk
                        spidx += 1
                    if kc == 1 and tail_to_emit:
                        tail_to_emit()
                        tail_to_emit = None
                    if kc == 1:
                        emit_vw_load(qc, g, 0)
                    elif kc == 2 and nwin >= 2:
                        emit_vw_load(qc, g, 1)
                    if kc == 2:
                        while pending_epi:
                            pending_epi.pop(0)[1]()
                    kcms = []
                    if kc == 2:
                        kcms = [0, 1]
                    elif kc > 2:
                        kcms = [kc - 1]
                    for kcm in kcms:
                        mpar = kcm % 2
                        w = kcm // 4
                        if (kcm // 4) == qc:
                            wt("t", c_dve, msk_par[mpar])
                        else:
                            wt("t", c_act, max(exp_pair[(mpar, 0)],
                                               exp_pair[(mpar, 1)]))
                        if kcm == 0:
                            wt("t", d_v, vw_ck[(g, 0)])
                            if ck_lw_last:
                                wt("t", c_act, ck_lw_last)
                        elif kcm % 4 == 0:
                            wt("t", d_v, vw_ck[(g, w)])
                        for hi in range(4):
                            if kcm == 0:
                                wt("t", c_dve, ck_mrg_last[hi])
                                if qc > 0 and g == 0:
                                    if hi == 0 and ck_pre2_slot[0]:
                                        wt("t", c_dve, ck_pre2_slot[0])
                                    if hi == 1 and ck_pre2_slot[1]:
                                        wt("t", c_dve, ck_pre2_slot[1])
                                    if hi == 2 and ck_hbf_last:
                                        wt("t", c_dve, ck_hbf_last)
                                    if hi == 3 and ck_dve_lg_rd:
                                        wt("t", c_dve, ck_dve_lg_rd)
                            vs = w % 2
                            lhsT = bass.AP(
                                m6, 1040 * vs + 260 * (kcm % 4) + 65 * hi,
                                [[4096, 128], [1, 65]])
                            mm(pF[hi][0:65, :], lhsT, PT4(hi, mpar),
                               start=(kcm == 0), stop=False)
                            pv = inc("t", c_pe)
                            ck_pv[(hi, kcm)] = pv
                            pt_last_par[mpar][hi // 2] = pv
                            if hi == 3:
                                lastV_ck[vs] = pv
                                if kcm % 4 == 3 and w + 2 < nwin:
                                    emit_vw_load(qc, g, w + 2)
                    for p in range(2):
                        sp = pair_sp[(kc, p)]
                        sb_pair = pS2[sp % 2]
                        wt("c", c_pe, qk_cks[sp])
                        if kc >= 2:
                            wt("c", c_pe, max(ck_pv[(2 * p, kc - 2)],
                                              ck_pv[(2 * p + 1, kc - 2)]))
                        else:
                            wt("c", c_pe, pt_last_par[par].get(p, 0))
                            if qc == 0 and g == 0 and p == 0 and kc == 0:
                                wt("c", c_dve, ck_cb)
                        em("c", lambda e, b=sb_pair, g_=g, p_=p, pa=par:
                           e.activation(
                               m5[:, 1024 * (2 * pa + p_):
                                  1024 * (2 * pa + p_) + 1024],
                               b[:, :], AF.Exp,
                               bias=CBP_sb[:, 2 * g_ + p_:2 * g_ + p_ + 1]))
                        ec = inc("c", c_act)
                        exp_sp_cks[sp] = ec
                        exp_pair[(par, p)] = ec
                        ck_exp_last = ec
                        if diag_k:
                            r = kc - 4 * qc
                            wt("v", c_act, ec)
                            for sub in range(2):
                                hi = 2 * p + sub
                                em("v", lambda e, p2=hi, pa=par, r_=r:
                                   e.tensor_mul(
                                       PT4(p2, pa), PT4(p2, pa),
                                       msk_sb[:, 384 - 128 * r_:
                                              384 - 128 * r_ + 512]))
                            msk_par[par] = inc("v", c_dve)
                    if kc == 1 and pending_epi and pending_epi[0][0] == "A":
                        pending_epi.pop(0)[1]()

                if g + 2 < NG:
                    emit_kg_load(qc, g + 2)
                if qc + 1 < NT:
                    emit_qt_slice(qc + 1, g)

                assert not pending_epi and pending_tailpv is None

                def make_tail(gq, gg, nk, expd, mskd, wock):
                    def tail():
                        pvl = {}
                        for kcm in (nk - 1,):
                            mpar = kcm % 2
                            w = kcm // 4
                            wt("t", c_dve, mskd[mpar])
                            for hi in range(4):
                                vs = w % 2
                                lhsT = bass.AP(
                                    m6, 1040 * vs + 260 * (kcm % 4) + 65 * hi,
                                    [[4096, 128], [1, 65]])
                                mm(pF[hi][0:65, :], lhsT, PT4(hi, mpar),
                                   start=False, stop=(kcm == nk - 1))
                                pv = inc("t", c_pe)
                                ck_pv[(hi, kcm)] = pv
                                pt_last_par[mpar][hi // 2] = pv
                                pvl[hi] = pv
                                if hi == 3:
                                    lastV_ck[vs] = pv
                        chunks = make_epi_chunks(
                            gg, pvl, wock if gg == 0 else 0)
                        pending_epi.extend(chunks)
                    return tail

                pending_tailpv = make_tail(qc, g, nkc, dict(exp_pair),
                                           dict(msk_par), ck_wo_last)

            if pending_tailpv:
                pending_tailpv()
                pending_tailpv = None
            while pending_epi:
                pending_epi.pop(0)[1]()
            if qc + 1 < NT:
                emit_qc_head(qc + 1)

            mrg_all = max(ck_mrg_last.values())
            for idx in range(8):
                j, oh2 = idx // 2, idx % 2
                bank = pF[2 + idx % 2]
                if idx == 0:
                    wt("t", c_dve, mrg_all)
                    if qc == 0:
                        wt("t", d_ld, ld_b)
                if idx == 1 and ck_lw_last:
                    wt("t", c_act, ck_lw_last)
                if idx >= 2:
                    wt("t", c_dve, ck_pre1_slot[idx % 2])
                for c in range(NCHUNK):
                    mm(bank[:, :],
                       m2[:, 512 * c + 128 * j:512 * c + 128 * (j + 1)],
                       w3[:, 1024 * c + 512 * oh2:1024 * c + 512 * (oh2 + 1)],
                       start=(c == 0), stop=(c == NCHUNK - 1))
                wo_ck = inc("t", c_pe)
                if idx == 7:
                    ck_wo_last = wo_ck
                wt("v", c_pe, wo_ck)
                wt("v", d_x, x_ck[(qc, idx)])
                em("v", lambda e, b=bank, j_=j, o_=oh2, s=idx % 2:
                   e.scalar_tensor_tensor(
                       B1[:, 1024 * j_ + 512 * o_:1024 * j_ + 512 * (o_ + 1)],
                       b[:, :], 1.0, xs_sb[:, 512 * s:512 * (s + 1)],
                       op0=ALU.bypass, op1=ALU.add,
                       accum_out=acc_sb[:, 2 * j_ + o_:2 * j_ + o_ + 1]))
                p1 = inc("v", c_dve)
                ck_pre1_slot[idx % 2] = p1
                ck_pre1[(qc, j, oh2)] = p1
                if idx + 2 < 8:
                    emit_x_load(qc, idx + 2)

            x1T_cks = {}
            sq_cks = {}
            var_cks = {}
            sd_cks = {}
            for j in range(4):
                L = 16 * j
                wt("v", c_dve, ck_pre1[(qc, j, 1)])
                em("v", lambda e, j_=j, L_=L: e.tensor_add(
                    lnp[:, L_:L_ + 1], acc_sb[:, 2 * j_:2 * j_ + 1],
                    acc_sb[:, 2 * j_ + 1:2 * j_ + 2]))
                vsp()
                em("v", lambda e, L_=L: e.tensor_scalar_mul(
                    lnp[:, L_ + 1:L_ + 2], lnp[:, L_:L_ + 1], 1.0 / D))
                inc("v", c_dve)
                wt("c", c_dve, ck_pre1[(qc, j, 1)])
                if lnp_guard[j]:
                    wt("c", c_dve, lnp_guard[j])
                em("c", lambda e, j_=j, L_=L: e.activation(
                    sqs_v, B1[:, 1024 * j_:1024 * j_ + 512], AF.Square,
                    accum_out=lnp[:, L_ + 7:L_ + 8]))
                em("c", lambda e, j_=j, L_=L: e.activation(
                    sqs_v, B1[:, 1024 * j_ + 512:1024 * (j_ + 1)], AF.Square,
                    accum_out=lnp[:, L_ + 8:L_ + 9]))
                sq_cks[j] = ainc_safe()
            for j in range(4):
                L = 16 * j
                wt("v", c_act, sq_cks[j])
                vsp()
                em("v", lambda e, L_=L: e.tensor_scalar(
                    lnp[:, L_ + 2:L_ + 3], lnp[:, L_ + 1:L_ + 2],
                    lnp[:, L_ + 1:L_ + 2], EPS,
                    op0=ALU.mult, op1=ALU.subtract))
                em("v", lambda e, L_=L: e.tensor_add(
                    lnp[:, L_ + 9:L_ + 10], lnp[:, L_ + 7:L_ + 8],
                    lnp[:, L_ + 8:L_ + 9]))
                vsp()
                em("v", lambda e, L_=L: e.tensor_scalar(
                    lnp[:, L_ + 3:L_ + 4], lnp[:, L_ + 9:L_ + 10], 1.0 / D,
                    lnp[:, L_ + 2:L_ + 3],
                    op0=ALU.mult, op1=ALU.subtract))
                var_cks[j] = vinc_safe()
                lnp_guard[j] = var_cks[j]
            for j in range(4):
                L = 16 * j
                wt("c", c_dve, var_cks[j])
                em("c", lambda e, L_=L: e.activation(
                    lnp[:, L_ + 4:L_ + 5], lnp[:, L_ + 3:L_ + 4], AF.Sqrt))
                sd_cks[j] = ainc_safe()
            for j in range(4):
                L = 16 * j
                wt("v", c_act, sd_cks[j])
                em("v", lambda e, L_=L: e.reciprocal(
                    lnp[:, L_ + 5:L_ + 6], lnp[:, L_ + 4:L_ + 5]))
                vsp()
                em("v", lambda e, L_=L: e.scalar_tensor_tensor(
                    lnp[:, L_ + 6:L_ + 7], lnp[:, L_ + 1:L_ + 2], -1.0,
                    lnp[:, L_ + 5:L_ + 6],
                    op0=ALU.mult, op1=ALU.mult))
                vsp()
                if j == 0 and ck_out:
                    wt("v", d_out, ck_out)
                em("v", lambda e, j_=j, L_=L: e.tensor_scalar(
                    B2[:, 1024 * j_:1024 * (j_ + 1)],
                    B1[:, 1024 * j_:1024 * (j_ + 1)],
                    lnp[:, L_ + 5:L_ + 6], lnp[:, L_ + 6:L_ + 7],
                    op0=ALU.mult, op1=ALU.add))
                vsp()
                x1_ck = inc("v", c_dve)
                if ck_x1bf_rd:
                    wt("v", c_pe, ck_x1bf_rd)
                em("v", lambda e, j_=j: e.tensor_copy(
                    m4[:, 1024 * j_:1024 * (j_ + 1)],
                    B2[:, 1024 * j_:1024 * (j_ + 1)]))
                x1bf_ck = inc("v", c_dve)
                wt("t", c_dve, x1bf_ck)
                wt("t", c_dve, lastpTR_read)
                if j == 0:
                    wt("t", c_act, ck_exp_last)
                for c in range(NCHUNK):
                    tr(TRA[:, 128 * c:128 * (c + 1)],
                       m4[:, 1024 * j + 128 * c:1024 * j + 128 * (c + 1)])
                x1tr_ck = inc("t", c_pe)
                if j == 3:
                    ck_x1bf_rd = x1tr_ck
                wt("v", c_pe, x1tr_ck)
                if ck_base_last:
                    wt("v", c_pe, ck_base_last)
                em("v", lambda e, j_=j: e.tensor_copy(
                    bass.AP(m3, 128 * j_, [[4096, 128], [512, 8], [1, 128]]),
                    TRA.rearrange("p (c i) -> p c i", i=128)))
                x1T_cks[j] = inc("v", c_dve)
                lastpTR_read = x1T_cks[j]

            pT_cks = {}
            for j in range(4):
                wt("t", c_dve, x1T_cks[j])
                if j == 0:
                    wt("t", c_dve, ck_pre1_slot[0])
                    if ck_pre2_slot[0]:
                        wt("t", c_dve, ck_pre2_slot[0])
                    if ck_hbf_last:
                        wt("t", c_dve, ck_hbf_last)
                if ck_dve_wp_rd:
                    wt("t", c_dve, ck_dve_wp_rd)
                for c in range(NCHUNK):
                    mm(pF[2][0:128, 0:256],
                       bass.AP(m3, 512 * c + 128 * j, [[4096, 128], [1, 128]]),
                       WpT_sb[:, 256 * c:256 * (c + 1)],
                       start=(c == 0), stop=(c == NCHUNK - 1))
                wp_ck = inc("t", c_pe)
                wt("v", c_pe, wp_ck)
                em("v", lambda e, j_=j: e.tensor_add(
                    projf[:, 256 * j_:256 * (j_ + 1)],
                    pF[2][0:128, 0:256], bpb_sb[:, :]))
                projadd_ck = inc("v", c_dve)
                ck_dve_wp_rd = projadd_ck
                wt("c", c_dve, projadd_ck)
                em("c", lambda e, j_=j: e.activation(
                    sqp_v, projf[:, 256 * j_:256 * (j_ + 1)], AF.Square,
                    accum_out=rtp[:, 0:1]))
                asp()
                asp()
                em("c", lambda e: e.activation(rtp[:, 1:2], rtp[:, 0:1], AF.Sqrt))
                pr_ck = ainc_safe()
                wt("v", c_act, pr_ck)
                em("v", lambda e: e.tensor_scalar_max(rtp[:, 1:2], rtp[:, 1:2], 1e-12))
                vsp()
                em("v", lambda e: e.reciprocal(rtp[:, 2:3], rtp[:, 1:2]))
                vsp()
                em("v", lambda e, j_=j: e.tensor_scalar_mul(
                    m6[:, 2304 + 256 * j_:2304 + 256 * (j_ + 1)],
                    projf[:, 256 * j_:256 * (j_ + 1)], rtp[:, 2:3]))
                pbf_ck = inc("v", c_dve)
                wt("t", c_dve, pbf_ck)
                wt("t", c_dve, lastpTR_read)
                for pc in range(2):
                    tr(TRA[:, 128 * pc:128 * (pc + 1)],
                       m6[:, 2304 + 256 * j + 128 * pc:
                          2304 + 256 * j + 128 * (pc + 1)])
                ptr_ck = inc("t", c_pe)
                wt("v", c_pe, ptr_ck)
                em("v", lambda e, j_=j: e.tensor_copy(
                    bass.AP(m7, 128 * j_, [[4096, 128], [512, 2], [1, 128]]),
                    TRA.rearrange("p (c i) -> p c i", i=128)[:, 0:2, :]))
                pT_cks[j] = inc("v", c_dve)
                lastpTR_read = pT_cks[j]

            g4_cks = {}
            for j in range(4):
                wt("t", c_dve, pT_cks[j])
                if j == 0:
                    wt("t", c_dve, ck_pre1_slot[1])
                    if ck_pre2_slot[1]:
                        wt("t", c_dve, ck_pre2_slot[1])
                if ck_dve_lg_rd:
                    wt("t", c_dve, ck_dve_lg_rd)
                if ck_lw_last:
                    wt("t", c_act, ck_lw_last)
                for pc in range(2):
                    mm(pF[3][0:128, 0:4],
                       bass.AP(m7, 512 * pc + 128 * j, [[4096, 128], [1, 128]]),
                       simn_sb[:, 4 * pc:4 * (pc + 1)],
                       start=(pc == 0), stop=(pc == 1))
                lg_ck = inc("t", c_pe)
                wt("v", c_pe, lg_ck)
                em("v", lambda e: e.reduce_max(
                    rtp[:, 3:4], pF[3][0:128, 0:4], AX.X, negate=True))
                nm_ck = vinc_safe()
                ck_dve_lg_rd = nm_ck
                wt("c", c_dve, nm_ck)
                em("c", lambda e: e.activation(
                    lw_sb[:, :], pF[3][0:128, 0:4], AF.Exp, bias=rtp[:, 3:4]))
                lw_ck = ainc_safe()
                ck_lw_last = lw_ck
                wt("v", c_act, lw_ck)
                em("v", lambda e: e.reduce_sum(rtp[:, 4:5], lw_sb[:, :], AX.X))
                em("v", lambda e: e.reduce_max(rtp[:, 6:7], lw_sb[:, :], AX.X))
                em("v", lambda e, j_=j: e.reciprocal(
                    rtp[:, 8 + j_:9 + j_], rtp[:, 4:5]))
                em("v", lambda e, j_=j: e.scalar_tensor_tensor(
                    g4_sb[:, 4 * j_:4 * (j_ + 1)], lw_sb[:, :], rtp[:, 6:7],
                    lw_sb[:, :], op0=ALU.is_ge, op1=ALU.mult))
                vsp()
                g4_cks[j] = inc("v", c_dve)

            hT_cks = {}
            for j in range(4):
                if ck_hbf_last:
                    wt("t", c_dve, ck_hbf_last)
                wt("t", c_dve, ck_dve_wp_rd)
                for c in range(NCHUNK):
                    mm(pF[2][0:128, 0:32],
                       bass.AP(m3, 512 * c + 128 * j, [[4096, 128], [1, 128]]),
                       WdT_sb[:, 32 * c:32 * (c + 1)],
                       start=(c == 0), stop=(c == NCHUNK - 1))
                dn_ck = inc("t", c_pe)
                wt("v", c_pe, dn_ck)
                wt("v", c_dve, g4_cks[j])
                for e_i in range(4):
                    em("v", lambda e, j_=j, ei=e_i: e.tensor_scalar(
                        m6[:, 3328 + 32 * j_ + 8 * ei:
                           3328 + 32 * j_ + 8 * (ei + 1)],
                        pF[2][0:128, 8 * ei:8 * (ei + 1)],
                        g4_sb[:, 4 * j_ + ei:4 * j_ + ei + 1],
                        rtp[:, 8 + j_:9 + j_],
                        op0=ALU.mult, op1=ALU.mult))
                hbf_ck = vinc_safe()
                ck_hbf_last = hbf_ck
                ck_state["hbf_for_bc"] = hbf_ck
                wt("t", c_dve, hbf_ck)
                wt("t", c_dve, lastpTR_read)
                em("t", lambda e, j_=j: e.transpose(
                    TRA[0:32, 0:128],
                    m6[:, 3328 + 32 * j_:3328 + 32 * (j_ + 1)],
                    id_sb[:, :]))
                htr_ck = inc("t", c_pe)
                wt("v", c_pe, htr_ck)
                em("v", lambda e, j_=j: e.tensor_copy(
                    bass.AP(m7, hT_off + 128 * j_, [[4096, 32], [1, 128]]),
                    TRA[0:32, 0:128]))
                hT_cks[j] = vinc_safe()
                lastpTR_read = hT_cks[j]

            for j in range(4):
                for oh2 in range(2):
                    bank = pF[0 + oh2]
                    wt("t", c_dve, hT_cks[j])
                    if j == 0:
                        wt("t", c_dve, max(ck_mrg_last.values()))
                        if oh2 == 0:
                            wt("t", c_dve, ck_dve_wp_rd)
                        else:
                            wt("t", c_dve, ck_dve_lg_rd)
                            wt("t", c_act, ck_lw_last)
                    if ck_pre2_slot[oh2]:
                        wt("t", c_dve, ck_pre2_slot[oh2])
                    for c in range(NCHUNK):
                        mm(bank[:, :],
                           bass.AP(m3, 512 * c + 128 * j, [[4096, 128], [1, 128]]),
                           w2[:, 1024 * c + 512 * oh2:1024 * c + 512 * (oh2 + 1)],
                           start=(c == 0), stop=False)
                    mm(bank[:, :],
                       bass.AP(m7, hT_off + 128 * j, [[4096, 32], [1, 128]]),
                       WuT_sb[0:32, 512 * oh2:512 * (oh2 + 1)],
                       start=False, stop=True)
                    bs_ck = inc("t", c_pe)
                    ck_base_last = bs_ck
                    wt("v", c_pe, bs_ck)
                    em("v", lambda e, b=bank, j_=j, o_=oh2: e.scalar_tensor_tensor(
                        B1[:, 1024 * j_ + 512 * o_:1024 * j_ + 512 * (o_ + 1)],
                        b[:, :], 1.0,
                        B2[:, 1024 * j_ + 512 * o_:1024 * j_ + 512 * (o_ + 1)],
                        op0=ALU.bypass, op1=ALU.add,
                        accum_out=acc_sb[:, 8 + 2 * j_ + o_:9 + 2 * j_ + o_]))
                    p2 = inc("v", c_dve)
                    ck_pre2_slot[oh2] = p2
                    ck_pre2_last = p2
                    pre2_cks[(j, oh2)] = p2

            sq2_cks = {}
            v2_cks = {}
            sd2_cks = {}
            for j in range(4):
                L = 16 * j
                pj = max(pre2_cks[(j, 0)], pre2_cks[(j, 1)])
                wt("v", c_dve, pj)
                em("v", lambda e, j_=j, L_=L: e.tensor_add(
                    lnp[:, L_:L_ + 1], acc_sb[:, 8 + 2 * j_:9 + 2 * j_],
                    acc_sb[:, 9 + 2 * j_:10 + 2 * j_]))
                vsp()
                em("v", lambda e, L_=L: e.tensor_scalar_mul(
                    lnp[:, L_ + 1:L_ + 2], lnp[:, L_:L_ + 1], 1.0 / D))
                inc("v", c_dve)
                wt("c", c_dve, pj)
                if lnp_guard[j]:
                    wt("c", c_dve, lnp_guard[j])
                em("c", lambda e, j_=j, L_=L: e.activation(
                    sqs_v, B1[:, 1024 * j_:1024 * j_ + 512], AF.Square,
                    accum_out=lnp[:, L_ + 7:L_ + 8]))
                em("c", lambda e, j_=j, L_=L: e.activation(
                    sqs_v, B1[:, 1024 * j_ + 512:1024 * (j_ + 1)], AF.Square,
                    accum_out=lnp[:, L_ + 8:L_ + 9]))
                sq2_cks[j] = ainc_safe()
            for j in range(4):
                L = 16 * j
                wt("v", c_act, sq2_cks[j])
                vsp()
                em("v", lambda e, L_=L: e.tensor_scalar(
                    lnp[:, L_ + 2:L_ + 3], lnp[:, L_ + 1:L_ + 2],
                    lnp[:, L_ + 1:L_ + 2], EPS,
                    op0=ALU.mult, op1=ALU.subtract))
                em("v", lambda e, L_=L: e.tensor_add(
                    lnp[:, L_ + 9:L_ + 10], lnp[:, L_ + 7:L_ + 8],
                    lnp[:, L_ + 8:L_ + 9]))
                vsp()
                em("v", lambda e, L_=L: e.tensor_scalar(
                    lnp[:, L_ + 3:L_ + 4], lnp[:, L_ + 9:L_ + 10], 1.0 / D,
                    lnp[:, L_ + 2:L_ + 3],
                    op0=ALU.mult, op1=ALU.subtract))
                v2_cks[j] = vinc_safe()
                lnp_guard[j] = v2_cks[j]
            for j in range(4):
                L = 16 * j
                wt("c", c_dve, v2_cks[j])
                em("c", lambda e, L_=L: e.activation(
                    lnp[:, L_ + 4:L_ + 5], lnp[:, L_ + 3:L_ + 4], AF.Sqrt))
                sd2_cks[j] = ainc_safe()
            for j in range(4):
                L = 16 * j
                wt("v", c_act, sd2_cks[j])
                em("v", lambda e, L_=L: e.reciprocal(
                    lnp[:, L_ + 5:L_ + 6], lnp[:, L_ + 4:L_ + 5]))
                vsp()
                em("v", lambda e, L_=L: e.scalar_tensor_tensor(
                    lnp[:, L_ + 6:L_ + 7], lnp[:, L_ + 1:L_ + 2], -1.0,
                    lnp[:, L_ + 5:L_ + 6],
                    op0=ALU.mult, op1=ALU.mult))
                vsp()
                em("v", lambda e, j_=j, L_=L: e.tensor_scalar(
                    B2[:, 1024 * j_:1024 * (j_ + 1)],
                    B1[:, 1024 * j_:1024 * (j_ + 1)],
                    lnp[:, L_ + 5:L_ + 6], lnp[:, L_ + 6:L_ + 7],
                    op0=ALU.mult, op1=ALU.add))
                vsp()
                ck_ln2_j = inc("v", c_dve)
                wt("g", c_dve, ck_ln2_j)
                em("g", lambda e, q=qc, j_=j: e.dma_start(
                    out=out_h[q * 512 + 128 * j_:q * 512 + 128 * (j_ + 1), :],
                    in_=B2[:, 1024 * j_:1024 * (j_ + 1)]))
                ck_out = inc("g", d_out, 16)

        em("y", lambda e: e.wait_ge(d_out.h, ck_out))

        def runner(lst):
            def f(e):
                with nc.allow_low_precision(reason="bf16 pipeline per problem tolerance"):
                    for fn in lst:
                        fn(e)
            return f

        with nc.Block() as blk:
            blk.tensor(runner(prog["t"]))
            blk.vector(runner(prog["v"]))
            blk.scalar(runner(prog["c"]))
            blk.gpsimd(runner(prog["g"]))
            blk.sync(runner(prog["y"]))

    return nc


def _recip_bf(nc, e, out, in_):
    with nc.allow_low_precision(reason="recip feeds bf16 matmul"):
        return e.reciprocal(out, in_)


def _make_host_constants():
    pos = np.arange(S_LEN, dtype=np.float32)
    freqs = (1.0 / (10000.0 ** (np.arange(0, HD, 2, dtype=np.float32) / HD))
             ).astype(np.float32)
    ang = pos[:, None] * freqs[None, :]
    cosf = np.repeat(np.cos(ang).astype(np.float32), 2, axis=1)
    sinf = np.repeat(np.sin(ang).astype(np.float32), 2, axis=1)
    cosm = np.ascontiguousarray(np.tile(cosf.T, (2, 1)))
    sinm = np.ascontiguousarray(np.tile(sinf.T, (2, 1)))

    ident = np.eye(128, dtype=ml_dtypes.bfloat16)
    rmat = np.zeros((128, 128), dtype=np.float32)
    for blk in (0, 64):
        for i in range(32):
            rmat[blk + 2 * i, blk + 2 * i + 1] = 1.0
            rmat[blk + 2 * i + 1, blk + 2 * i] = -1.0
    rmat = rmat.astype(ml_dtypes.bfloat16)

    hsum = np.zeros((128, 128), dtype=np.float32)
    for c in range(8):
        hsum[0:64, 16 * c + 2 * c] = 1.0
        hsum[64:128, 16 * c + 2 * c + 1] = 1.0
    hsum = hsum.astype(ml_dtypes.bfloat16)

    iu = np.arange(896)[None, :]
    ii = np.arange(128)[:, None]
    masks = (iu >= ii + 384).astype(ml_dtypes.bfloat16)

    onesb = np.ones((1, 128), dtype=ml_dtypes.bfloat16)
    onesf = np.ones((1, 128), dtype=np.float32)
    selm = np.zeros((128, 256), dtype=ml_dtypes.bfloat16)
    for h in range(4):
        selm[32 * h, 64 * h:64 * (h + 1)] = 1.0
    return cosm, sinm, ident, rmat, hsum, masks, onesb, onesf, selm


def kernel(L0_attn_x, L0_attn_wq, L0_attn_wk, L0_attn_wv, L0_attn_wo,
           L0_ffn_W, L0_ffn_Wp, L0_ffn_bp, L0_ffn_sim, L0_ffn_temp,
           L0_ffn_Wd0, L0_ffn_Wu0, L0_ffn_Wd1, L0_ffn_Wu1,
           L0_ffn_Wd2, L0_ffn_Wu2, L0_ffn_Wd3, L0_ffn_Wu3):
    x = np.asarray(L0_attn_x, dtype=np.float32)
    assert x.shape == (8, S_LEN, D)

    wqT = np.ascontiguousarray(np.asarray(L0_attn_wq, np.float32).T)
    wkT = np.ascontiguousarray(np.asarray(L0_attn_wk, np.float32).T)
    wvT = np.ascontiguousarray(np.asarray(L0_attn_wv, np.float32).T)
    woT = np.ascontiguousarray(np.asarray(L0_attn_wo, np.float32).T)
    WT = np.ascontiguousarray(np.asarray(L0_ffn_W, np.float32).T)
    WpT = np.ascontiguousarray(np.asarray(L0_ffn_Wp, np.float32).T)

    Wd = [np.asarray(w, np.float32) for w in
          (L0_ffn_Wd0, L0_ffn_Wd1, L0_ffn_Wd2, L0_ffn_Wd3)]
    Wu = [np.asarray(w, np.float32) for w in
          (L0_ffn_Wu0, L0_ffn_Wu1, L0_ffn_Wu2, L0_ffn_Wu3)]
    WdTp = np.zeros((D, 32), dtype=np.float32)
    WuTp = np.zeros((32, D), dtype=np.float32)
    for e_i, r_e in enumerate((1, 2, 4, 8)):
        WdTp[:, 8 * e_i:8 * e_i + r_e] = Wd[e_i].T
        WuTp[8 * e_i:8 * e_i + r_e, :] = Wu[e_i].T
    simT = np.ascontiguousarray(np.asarray(L0_ffn_sim, np.float32).T)
    tempv = np.asarray(L0_ffn_temp, np.float32).reshape(1, 1)
    bp_bc = np.ascontiguousarray(
        np.tile(np.asarray(L0_ffn_bp, np.float32)[None, :], (128, 1)))

    (cosm, sinm, ident, rmat, hsum, masks, onesb, onesf,
     selm) = _make_host_constants()

    shared = dict(wqT=wqT, wkT=wkT, wvT=wvT, woT=woT, WT=WT, WpT=WpT,
                  WdTp=WdTp, WuTp=WuTp, simT=simT, tempv=tempv, bp_bc=bp_bc,
                  cosm=cosm, sinm=sinm, identb=ident, rmatb=rmat,
                  headsum=hsum, masksb=masks, onesb=onesb, onesf=onesf,
                  selm=selm)

    in_maps = []
    for b in range(8):
        xb = np.ascontiguousarray(x[b])
        m = dict(shared)
        m["x"] = xb
        m["xT"] = np.ascontiguousarray(xb.T)
        in_maps.append(m)

    nc = build_program()
    res = run_bass_kernel_spmd(nc, in_maps, core_ids=list(range(8)))
    return np.stack([np.asarray(r["o"], dtype=np.float32) for r in res.results])


# version 36
# speedup vs baseline: 1.0053x; 1.0053x over previous
from contextlib import ExitStack

import numpy as np
import ml_dtypes

import concourse.bass as bass
import concourse.mybir as mybir
from concourse.bass_utils import run_bass_kernel_spmd

F32 = mybir.dt.float32
BF16 = mybir.dt.bfloat16
AF = mybir.ActivationFunctionType
ALU = mybir.AluOpType
AX = mybir.AxisListType

S_LEN = 4096
D = 1024
HD = 64
NT = 8
TT = 512
NCHUNK = 8
NG = 4
CLAMP_MAX = float(np.log(1.0 / 0.01))
EPS = 1e-5


class Ctr:
    def __init__(self, st, nc, name):
        self.h = st.enter_context(nc.semaphore(name))
        self.n = 0

    def tick(self, k=1):
        self.n += k
        return self.n


def build_program():
    nc = bass.Bass("TRN2", target_bir_lowering=False)

    xh = nc.dram_tensor("x", [S_LEN, D], F32, kind="ExternalInput")
    xTh = nc.dram_tensor("xT", [D, S_LEN], F32, kind="ExternalInput")
    wqh = nc.dram_tensor("wqT", [D, D], F32, kind="ExternalInput")
    wkh = nc.dram_tensor("wkT", [D, D], F32, kind="ExternalInput")
    wvh = nc.dram_tensor("wvT", [D, D], F32, kind="ExternalInput")
    woh = nc.dram_tensor("woT", [D, D], F32, kind="ExternalInput")
    Wh = nc.dram_tensor("WT", [D, D], F32, kind="ExternalInput")
    Wph = nc.dram_tensor("WpT", [D, 256], F32, kind="ExternalInput")
    Wdh = nc.dram_tensor("WdTp", [D, 32], F32, kind="ExternalInput")
    Wuh = nc.dram_tensor("WuTp", [32, D], F32, kind="ExternalInput")
    simh = nc.dram_tensor("simT", [4, 256], F32, kind="ExternalInput")
    temph = nc.dram_tensor("tempv", [1, 1], F32, kind="ExternalInput")
    bph = nc.dram_tensor("bp_bc", [128, 256], F32, kind="ExternalInput")
    cosmh = nc.dram_tensor("cosm", [128, S_LEN], F32, kind="ExternalInput")
    sinmh = nc.dram_tensor("sinm", [128, S_LEN], F32, kind="ExternalInput")
    identh = nc.dram_tensor("identb", [128, 128], BF16, kind="ExternalInput")
    rmath = nc.dram_tensor("rmatb", [128, 128], BF16, kind="ExternalInput")
    hsumh = nc.dram_tensor("headsum", [128, 128], BF16, kind="ExternalInput")
    maskh = nc.dram_tensor("masksb", [128, 896], BF16, kind="ExternalInput")
    onesbh = nc.dram_tensor("onesb", [1, 128], BF16, kind="ExternalInput")
    onesfh = nc.dram_tensor("onesf", [1, 128], F32, kind="ExternalInput")
    selh = nc.dram_tensor("selm", [128, 256], BF16, kind="ExternalInput")
    out_h = nc.dram_tensor("o", [S_LEN, D], F32, kind="ExternalOutput")

    QS = nc.dram_tensor("QS", [NT, 128, 4096], BF16, kind="Internal")
    KS = nc.dram_tensor("KS", [NT, NCHUNK, 128, 512], BF16, kind="Internal")
    VS = nc.dram_tensor("VS", [32, 128, 1040], BF16, kind="Internal")
    CHD = nc.dram_tensor("CHD", [1, 16], F32, kind="Internal")

    with ExitStack() as st:
        def sb(name, shape, dt):
            return st.enter_context(nc.sbuf_tensor(name, shape, dt))

        def mkpsum(name, shape, dt):
            return st.enter_context(nc.psum_tensor(name, shape, dt))

        id_sb = sb("id_sb", [128, 128], BF16)
        rm_sb = sb("rm_sb", [128, 128], BF16)
        hs_sb = sb("hs_sb", [128, 128], BF16)
        msk_sb = sb("msk_sb", [128, 896], BF16)
        ones_sb = sb("ones_sb", [1, 128], BF16)
        onesf_sb = sb("onesf_sb", [1, 128], F32)
        bpb_sb = sb("bpb_sb", [128, 256], F32)
        simT_sb = sb("simT_sb", [4, 256], F32)
        smq_sb = sb("smq_sb", [4, 256], F32)
        simbf_sb = sb("simbf_sb", [4, 256], BF16)
        temp_sb = sb("temp_sb", [1, 1], F32)
        simn_sb = sb("simn_sb", [128, 8], BF16)
        chrow_sb = sb("chrow_sb", [1, 16], F32)
        CB_sb = sb("CB_sb", [128, 16], F32)
        CBP_sb = sb("CBP_sb", [128, 8], F32)
        chp_sb = sb("chp_sb", [1, 8], F32)
        qmax2_sb = sb("qmax2_sb", [16, 1], F32)
        kmax2_sb = sb("kmax2_sb", [16, 1], F32)
        qmx_sb = sb("qmx_sb", [16, 1], F32)
        kmx_sb = sb("kmx_sb", [16, 1], F32)
        ch_sb = sb("ch_sb", [16, 1], F32)
        n2t_sb = sb("n2t_sb", [16, 2], F32)
        WpT_sb = sb("WpT_sb", [128, 2048], BF16)
        WdT_sb = sb("WdT_sb", [128, 256], BF16)
        WuT_sb = sb("WuT_sb", [32, 1024], BF16)
        lnp = sb("lnp", [128, 64], F32)
        rtp = sb("rtp", [128, 16], F32)
        lw_sb = sb("lw_sb", [128, 4], F32)
        g4_sb = sb("g4_sb", [128, 16], F32)
        acc_sb = sb("acc_sb", [128, 20], F32)
        spc_sb = sb("spc_sb", [128, 4], F32)
        rbc_sb = sb("rbc_sb", [64, 1024], BF16)
        dstf = sb("dstf", [128, 512], F32)
        rrv4b = sb("rrv4b", [128, 512], BF16)
        sel_sb = sb("sel_sb", [128, 256], BF16)
        projf = sb("projf", [128, 1024], F32)
        xs_sb = sb("xs_sb", [128, 1024], F32)
        VpT = sb("VpT", [128, 2080], BF16)
        sqA = sb("sqA", [128, 1024], BF16)
        t1_sb = sb("t1_sb", [128, 1024], BF16)
        t2_sb = sb("t2_sb", [128, 1024], BF16)

        w1 = sb("w1", [128, 8192], BF16)
        w2 = sb("w2", [128, 8192], BF16)
        w3 = sb("w3", [128, 8192], BF16)
        m1 = sb("m1", [128, 4096], BF16)
        m2 = sb("m2", [128, 4096], BF16)
        m3 = sb("m3", [128, 4096], BF16)
        m4 = sb("m4", [128, 4096], BF16)
        m5 = sb("m5", [128, 4096], BF16)
        m6 = sb("m6", [128, 4096], BF16)
        m7 = sb("m7", [128, 4096], BF16)
        m8 = sb("m8", [128, 4096], BF16)
        m9 = sb("m9", [128, 4096], BF16)
        B1 = sb("B1", [128, 4096], F32)
        B2 = sb("B2", [128, 4096], F32)

        pF = [mkpsum(f"pF{i}", [128, 512], F32) for i in range(4)]
        pSA = mkpsum("pSA", [128, 1024], F32)
        pSB = mkpsum("pSB", [128, 1024], F32)
        pS2 = [pSA, pSB]
        TRA = pSA[:, 0:512].bitcast(BF16)
        TRB = pSB[:, 512:1024].bitcast(BF16)

        c_pe = Ctr(st, nc, "c_pe")
        c_dve = Ctr(st, nc, "c_dve")
        c_act = Ctr(st, nc, "c_act")
        c_pool = Ctr(st, nc, "c_pool")
        d_ld = Ctr(st, nc, "d_ld")
        d_xt = Ctr(st, nc, "d_xt")
        d_scw = Ctr(st, nc, "d_scw")
        d_q = Ctr(st, nc, "d_q")
        d_k = Ctr(st, nc, "d_k")
        d_v = Ctr(st, nc, "d_v")
        d_x = Ctr(st, nc, "d_x")
        d_out = Ctr(st, nc, "d_out")
        c_ph = Ctr(st, nc, "c_ph")
        ENG = {"t": c_pe, "v": c_dve, "c": c_act, "g": c_pool}

        prog = {k: [] for k in "tvcgy"}

        def em(q, fn):
            prog[q].append(fn)

        def wt(q, ctr, val):
            assert val <= ctr.n, f"wait ahead of producer: {val} > {ctr.n} on {q}"
            if val > 0:
                em(q, lambda e, h=ctr.h, v=val: e.wait_ge(h, v))

        def inc(q, ctr, k=1):
            last = prog[q][-1]

            def fn(e, last=last, h=ctr.h, k=k):
                last(e).then_inc(h, k)
            prog[q][-1] = fn
            return ctr.tick(k)

        def vsp():
            em("v", lambda e: e.tensor_scalar_mul(
                spc_sb[:, 0:1], spc_sb[:, 1:2], 1.0))

        def vinc_safe():
            vsp()
            return inc("v", c_dve)

        def asp():
            em("c", lambda e: e.mul(spc_sb[:, 2:3], spc_sb[:, 3:4], 1.0))

        def ainc_safe():
            asp()
            return inc("c", c_act)

        def mm(out, lhsT, rhs, start=True, stop=True):
            em("t", lambda e, o=out, l=lhsT, r=rhs, s1=start, s2=stop:
               e.matmul(o, l, r, start=s1, stop=s2))

        def tr(out, in_):
            kp = in_.shape[0]
            em("t", lambda e, o=out, i=in_, k=kp: e.transpose(
                o, i, id_sb[0:k, 0:k]))

        for dst, src in [
            (id_sb[:, :], identh[:, :]), (rm_sb[:, :], rmath[:, :]),
            (hs_sb[:, :], hsumh[:, :]), (msk_sb[:, :], maskh[:, :]),
            (ones_sb[:, :], onesbh[:, :]), (onesf_sb[:, :], onesfh[:, :]),
            (sel_sb[:, :], selh[:, :]),
            (bpb_sb[:, :], bph[:, :]), (simT_sb[:, :], simh[:, :]),
            (temp_sb[:, :], temph[:, :]),
        ]:
            em("y", lambda e, d=dst, s=src: e.dma_start(out=d, in_=s))
            inc("y", d_ld, 16)
        ld_plain = d_ld.n

        for dst, src in [
            (w1[:, :].rearrange("p (c o) -> p c o", o=1024),
             wqh[:, :].rearrange("(c p) o -> p c o", p=128)),
            (w2[:, :].rearrange("p (c o) -> p c o", o=1024),
             wkh[:, :].rearrange("(c p) o -> p c o", p=128)),
            (w3[:, :].rearrange("p (c o) -> p c o", o=1024),
             wvh[:, :].rearrange("(c p) o -> p c o", p=128)),
            (m1[:, :], cosmh[:, :]),
            (m2[:, :], sinmh[:, :]),
            (WpT_sb[:, :].rearrange("p (c o) -> p c o", o=256),
             Wph[:, :].rearrange("(c p) o -> p c o", p=128)),
            (WdT_sb[:, :].rearrange("p (c o) -> p c o", o=32),
             Wdh[:, :].rearrange("(c p) o -> p c o", p=128)),
            (WuT_sb[:, :], Wuh[:, :]),
        ]:
            em("g", lambda e, d=dst, s=src: e.dma_start(out=d, in_=s))
            inc("g", d_ld, 16)
        ld_all = d_ld.n

        em("g", lambda e: e.memset(spc_sb[:, :], 0.0))
        em("g", lambda e: e.memset(qmax2_sb[:, :], 0.0))
        em("g", lambda e: e.memset(kmax2_sb[:, :], 0.0))
        em("g", lambda e: e.memset(
            bass.AP(VpT, 64, [[2080, 128], [65, 32], [1, 1]]), 1.0))
        em("g", lambda e: e.memset(dstf[:, :], 1.0))
        memset_done = inc("g", c_pool)

        em("g", lambda e: e.dma_start(
            out=m3[:, :].rearrange("p (c t) -> p c t", t=TT),
            in_=xTh[:, 0:TT].rearrange("(c p) t -> p c t", p=128)))
        inc("g", d_xt, 16)

        wt("c", d_ld, ld_all)
        em("c", lambda e: e.activation(smq_sb[:, :], simT_sb[:, :], AF.Square))
        sm1 = inc("c", c_act)
        wt("v", c_act, sm1)
        em("v", lambda e: e.reduce_sum(rtp[0:4, 0:1], smq_sb[:, :], AX.X))
        sm2 = vinc_safe()
        wt("c", c_dve, sm2)
        em("c", lambda e: e.activation(rtp[0:4, 1:2], rtp[0:4, 0:1], AF.Sqrt))
        sm3 = ainc_safe()
        wt("v", c_act, sm3)
        em("v", lambda e: e.tensor_scalar_max(rtp[0:4, 1:2], rtp[0:4, 1:2], 1e-12))
        vsp()
        em("v", lambda e: e.reciprocal(rtp[0:4, 2:3], rtp[0:4, 1:2]))
        em("v", lambda e: e.tensor_scalar_min(rtp[0:1, 3:4], temp_sb[:, :], CLAMP_MAX))
        sm4 = vinc_safe()
        wt("c", c_dve, sm4)
        em("c", lambda e: e.activation(rtp[0:1, 4:5], rtp[0:1, 3:4], AF.Exp))
        sm5 = ainc_safe()
        wt("t", c_act, sm5)
        mm(pSB[0:4, 0:1], onesf_sb[0:1, 0:4], rtp[0:1, 4:5])
        sm6 = inc("t", c_pe)
        wt("v", c_pe, sm6)
        em("v", lambda e: e.tensor_copy(rtp[0:4, 5:6], pSB[0:4, 0:1]))
        vsp()
        em("v", lambda e: e.tensor_mul(rtp[0:4, 6:7], rtp[0:4, 2:3], rtp[0:4, 5:6]))
        vsp()
        em("v", lambda e: e.tensor_scalar_mul(
            simbf_sb[:, :], simT_sb[:, :], rtp[0:4, 6:7]))
        sm7 = vinc_safe()
        wt("t", c_dve, sm7)
        wt("t", d_ld, ld_plain)
        tr(TRB[0:128, 0:4], simbf_sb[0:4, 0:128])
        tr(TRB[0:128, 4:8], simbf_sb[0:4, 128:256])
        sm8 = inc("t", c_pe)
        wt("v", c_pe, sm8)
        em("v", lambda e: e.tensor_copy(simn_sb[:, :], TRB[0:128, 0:8]))
        lastpTR_read = inc("v", c_dve)
        sm7x = lastpTR_read

        xt = [m3, m4]
        ck_qkv_last = {}
        ck_evac = []
        ck_qpl, ck_kpl, ck_vsb = {}, {}, {}
        ck_rope, ck_t2, ck_addidx, ck_sq, ck_t1 = {}, {}, {}, {}, {}
        ck_n2done = {}
        rec_vp = [0, 0]
        rec_qtb = 0
        rec_ktb = 0
        gidx = 0
        ridx = 0
        tidx = 0
        ck_hsq_last = 0
        ck_hsk_last = 0
        ck_n2done_prev = 0
        ck_vtr_last = {c: 0 for c in range(NCHUNK)}
        hs_byc = {}

        for tc in range(NT):
            par = tc % 2
            if tc + 1 < NT:
                if tc >= 1:
                    wt("g", c_pe, ck_qkv_last[tc - 1])
                em("g", lambda e, t=tc + 1: e.dma_start(
                    out=xt[t % 2][:, :].rearrange("p (c t) -> p c t", t=TT),
                    in_=xTh[:, t * TT:(t + 1) * TT].rearrange(
                        "(c p) t -> p c t", p=128)))
                inc("g", d_xt, 16)
            xt_ld = 16 * (tc + 1)

            first_mm_of_tile = True
            for w in range(3):
                wsb = [w1, w2, w3][w]
                for oc in range(NCHUNK):
                    bank = pF[gidx % 4]
                    if gidx >= 4:
                        engq, cnt = ck_evac[gidx - 4]
                        wt("t", ENG[engq], cnt)
                    if first_mm_of_tile:
                        wt("t", d_xt, xt_ld)
                        first_mm_of_tile = False
                    for c in range(NCHUNK):
                        mm(bank[:, :],
                           wsb[:, 1024 * c + 128 * oc:1024 * c + 128 * oc + 128],
                           xt[par][:, TT * c:TT * (c + 1)],
                           start=(c == 0), stop=(c == NCHUNK - 1))
                    g_ck = inc("t", c_pe)
                    if w == 2 and oc == NCHUNK - 1:
                        ck_qkv_last[tc] = g_ck
                    if w == 0:
                        wt("c", c_pe, g_ck)
                        if tc > 0:
                            wt("c", c_dve, ck_rope[("q", tc - 1, oc)])
                            wt("c", c_pe, ck_hsq_last)
                        em("c", lambda e, b=bank, o=oc: e.activation(
                            m5[:, TT * o:TT * (o + 1)], b[:, :], AF.Copy,
                            scale=0.125))
                        ck_qpl[(tc, oc)] = inc("c", c_act)
                        ck_evac.append(("c", ck_qpl[(tc, oc)]))
                    elif w == 1:
                        wt("c", c_pe, g_ck)
                        if tc > 0:
                            wt("c", c_dve, ck_rope[("k", tc - 1, oc)])
                            wt("c", c_pe, ck_hsk_last)
                        em("c", lambda e, b=bank, o=oc: e.activation(
                            m6[:, TT * o:TT * (o + 1)], b[:, :], AF.Copy,
                            scale=1.0))
                        ck_kpl[(tc, oc)] = inc("c", c_act)
                        ck_evac.append(("c", ck_kpl[(tc, oc)]))
                    else:
                        wt("v", c_pe, g_ck)
                        if tc > 0:
                            wt("v", c_pe, ck_vtr_last[oc])
                        em("v", lambda e, b=bank, o=oc: e.tensor_copy(
                            m9[:, TT * o:TT * (o + 1)], b[:, :]))
                        ck_vsb[(tc, oc)] = inc("v", c_dve)
                        ck_evac.append(("v", ck_vsb[(tc, oc)]))
                    gidx += 1

            for which, pl, tgt, n2bank, ckd in (
                    ("q", m5, m7, pSB[:, 0:512], ck_qpl),
                    ("k", m6, m8, pSB[:, 0:512], ck_kpl)):
                scw_gate = rec_qtb if which == "q" else rec_ktb
                for c in range(NCHUNK):
                    rb = pSA[:, 512 * (ridx % 2):512 * (ridx % 2 + 1)]
                    ep = ckd[(tc, c)]
                    wt("t", c_act, ep)
                    if ridx >= 2:
                        wt("t", c_dve, ck_t2[ridx - 2])
                    mm(rb, rm_sb[:, :], pl[:, TT * c:TT * (c + 1)])
                    rot_ck = inc("t", c_pe)
                    tslot = tidx % 2
                    wt("g", c_act, ep)
                    if tidx >= 2:
                        wt("g", c_dve, ck_addidx[tidx - 2])
                    em("g", lambda e, p=pl, c_=c, s=tslot, t=tc: e.tensor_tensor(
                        t1_sb[:, TT * s:TT * (s + 1)],
                        p[:, TT * c_:TT * (c_ + 1)],
                        m1[:, TT * t:TT * (t + 1)], ALU.mult))
                    ck_t1[tidx] = inc("g", c_pool)
                    wt("v", c_pe, rot_ck)
                    em("v", lambda e, b=rb, s=tslot, t=tc: e.tensor_tensor(
                        t2_sb[:, TT * s:TT * (s + 1)], b,
                        m2[:, TT * t:TT * (t + 1)], ALU.mult))
                    ck_t2[ridx] = inc("v", c_dve)
                    wt("v", c_pool, ck_t1[tidx])
                    wt("v", d_scw, scw_gate)
                    em("v", lambda e, t_=tgt, c_=c, s=tslot: e.tensor_add(
                        t_[:, TT * c_:TT * (c_ + 1)],
                        t1_sb[:, TT * s:TT * (s + 1)],
                        t2_sb[:, TT * s:TT * (s + 1)]))
                    ck_rope[(which, tc, c)] = inc("v", c_dve)
                    ck_addidx[tidx] = ck_rope[(which, tc, c)]
                    ridx += 1
                    tidx += 1
                    sslot = c % 2
                    wt("c", c_act, ep)
                    key2 = (which, c - 2) if c >= 2 else (
                        ("q", NCHUNK - 2 + c) if which == "k" else None)
                    if key2 is not None and (key2[1] >= 0):
                        prev = hs_byc.get(key2)
                        if prev:
                            wt("c", c_pe, prev)
                    em("c", lambda e, p=pl, c_=c, s=sslot: e.activation(
                        sqA[:, TT * s:TT * (s + 1)],
                        p[:, TT * c_:TT * (c_ + 1)], AF.Square))
                    ck_sq[(which, tc, c)] = inc("c", c_act)
                    wt("t", c_act, ck_sq[(which, tc, c)])
                    if c == 0:
                        wt("t", c_dve, ck_n2done_prev)
                        if tc == 0 and which == "q":
                            wt("t", c_dve, sm7x)
                    mm(n2bank[0:16, 0:512], hs_sb[:, 16 * c:16 * c + 16],
                       sqA[:, TT * sslot:TT * (sslot + 1)],
                       start=(c == 0), stop=(c == NCHUNK - 1))
                    hs_ck = inc("t", c_pe)
                    hs_byc[(which, c)] = hs_ck
                if which == "q":
                    ck_hsq_last = hs_ck
                else:
                    ck_hsk_last = hs_ck
                wt("v", c_pe, hs_ck)
                if tc == 0 and which == "q":
                    wt("v", c_pool, memset_done)
                tgt_max = qmax2_sb if which == "q" else kmax2_sb
                em("v", lambda e: e.reduce_max(
                    n2t_sb[0:16, 0:1], pSB[0:16, 0:512], AX.X))
                vsp()
                em("v", lambda e, t_=tgt_max: e.tensor_max(
                    t_[:, :], t_[:, :], n2t_sb[0:16, 0:1]))
                ck_n2done_prev = vinc_safe()
            ck_n2done[tc] = ck_n2done_prev

            for j in range(4):
                for c in range(NCHUNK):
                    wt("t", c_dve, ck_vsb[(tc, c)])
                    if c == 0:
                        wt("t", c_dve, lastpTR_read)
                        if tc == 0 and j == 0:
                            wt("t", c_pool, memset_done)
                    tr(TRB[:, 128 * c:128 * (c + 1)],
                       m9[:, TT * c + 128 * j:TT * c + 128 * (j + 1)])
                vtr_ck = inc("t", c_pe)
                for c in range(NCHUNK):
                    ck_vtr_last[c] = vtr_ck
                vslot = j % 2
                wt("v", c_pe, vtr_ck)
                wt("v", d_scw, rec_vp[vslot])
                em("v", lambda e, s=vslot: e.tensor_copy(
                    bass.AP(VpT, 1040 * s, [[2080, 128], [130, 8], [65, 2], [1, 64]]),
                    TRB.rearrange("p (c d i) -> p c d i", d=2, i=64)))
                vev = inc("v", c_dve)
                lastpTR_read = vev
                wt("y", c_dve, vev)
                em("y", lambda e, s=vslot, t=tc, j_=j: e.dma_start(
                    out=VS[4 * t + j_, :, :],
                    in_=VpT[:, 1040 * s:1040 * (s + 1)]))
                rec_vp[vslot] = inc("y", d_scw, 16)

            wt("y", c_dve, ck_rope[("q", tc, NCHUNK - 1)])
            em("y", lambda e, t=tc: e.dma_start(out=QS[t, :, :], in_=m7[:, :]))
            rec_qtb = inc("y", d_scw, 16)
            wt("y", c_dve, ck_rope[("k", tc, NCHUNK - 1)])
            em("y", lambda e, t=tc: e.dma_start(
                out=KS[t].rearrange("c p i -> p c i"),
                in_=m8[:, :].rearrange("p (c i) -> p c i", i=512)))
            rec_ktb = inc("y", d_scw, 16)

        wt("c", c_dve, ck_n2done[NT - 1])
        em("c", lambda e: e.activation(qmx_sb[:, :], qmax2_sb[:, :], AF.Sqrt))
        em("c", lambda e: e.activation(kmx_sb[:, :], kmax2_sb[:, :], AF.Sqrt))
        cbm = ainc_safe()
        wt("v", c_act, cbm)
        em("v", lambda e: e.tensor_mul(ch_sb[:, :], qmx_sb[:, :], kmx_sb[:, :]))
        cb2 = vinc_safe()
        wt("y", c_dve, cb2)
        em("y", lambda e: e.dma_start(
            out=CHD[:, :].rearrange("a b -> b a"), in_=ch_sb[:, :]))
        cb3 = inc("y", d_scw, 16)
        wt("y", d_scw, cb3)
        em("y", lambda e: e.dma_start(out=chrow_sb[:, :], in_=CHD[:, :]))
        cb4 = inc("y", d_ld, 16)
        wt("v", d_ld, cb4)
        em("v", lambda e: e.tensor_tensor(
            chp_sb[:, :],
            bass.AP(chrow_sb, 0, [[16, 1], [2, 8]]),
            bass.AP(chrow_sb, 1, [[16, 1], [2, 8]]), ALU.max))
        cbp1 = vinc_safe()
        wt("t", c_dve, cbp1)
        mm(pSB[0:128, 0:8], onesf_sb[:, :], chp_sb[:, :])
        cb5 = inc("t", c_pe)
        wt("v", c_pe, cb5)
        em("v", lambda e: e.tensor_scalar_mul(CBP_sb[:, :], pSB[0:128, 0:8], -1.0))
        ck_cb = vinc_safe()

        for q in "tvcgy":
            em(q, lambda e: e.nop())
            last = prog[q][-1]

            def fn(e, last=last):
                last(e).then_inc(c_ph.h, 1)
            prog[q][-1] = fn
        c_ph.tick(5)
        for q in "tvcgy":
            em(q, lambda e: e.wait_ge(c_ph.h, 5))

        em("g", lambda e: e.dma_start(
            out=w2[:, :].rearrange("p (c o) -> p c o", o=1024),
            in_=Wh[:, :].rearrange("(c p) o -> p c o", p=128)))
        inc("g", d_ld, 16)
        em("g", lambda e: e.dma_start(
            out=w3[:, :].rearrange("p (c o) -> p c o", o=1024),
            in_=woh[:, :].rearrange("(c p) o -> p c o", p=128)))
        ld_b = inc("g", d_ld, 16)

        def PT4(hi, par):
            return m5[:, 1024 * (2 * par + hi // 2) + 512 * (hi % 2):
                      1024 * (2 * par + hi // 2) + 512 * (hi % 2) + 512]

        rrv = [bass.AP(m7, 1024 + 512 * s, [[4096, 1], [1, 512]]) for s in range(2)]
        sqs_v = m7[:, 2048:2560]
        sqp_v = m7[:, 2560:2816]
        hT_off = 3072

        ck_pv = {}
        ck_mrg_last = {hi: 0 for hi in range(4)}
        ck_pre1 = {}
        ck_pre1_slot = {0: 0, 1: 0}
        ck_pre2_slot = {0: 0, 1: 0}
        ck_pre2_last = 0
        ck_out = 0
        ck_x1bf_rd = 0
        ck_base_last = 0
        ck_rbc_last = 0
        ck_hbf_last = 0
        ck_dve_wp_rd = 0
        ck_dve_lg_rd = 0
        ck_lw_last = 0
        ck_lnp78_rd = 0
        lnp_guard = {j: 0 for j in range(4)}
        pre2_cks = {}
        ck_wo_last = 0
        lastV_ck = [0, 0]
        kslotQK = [0, 0]
        pt_last_par = {0: {}, 1: {}}
        rr_last_bc = [0, 0]
        kg_ck = {}
        vw_ck = {}
        x_ck = {}
        spidx = 0
        exp_sp_cks = {}
        qk_cks = {}
        ck_state = {"last_rec": 0, "hbf_for_bc": 0}
        ck_exp_last = 0
        prev_qk_m1 = 0

        def kg_views(g):
            if g % 2 == 0:
                return (w1, 0, 8192, w1, 4096, 8192)
            return (m8, 0, 4096, m9, 0, 4096)

        def emit_kg_load(qc, g):
            tA, bA, rA, tB, bB, rB = kg_views(g)
            wt("y", c_pe, kslotQK[g % 2])
            for (tt, bb, c) in ((tA, bA, 2 * g), (tB, bB, 2 * g + 1)):
                em("y", lambda e, t_=tt, b_=bb, c_=c, q=qc: e.dma_start(
                    out=t_[:, b_:b_ + (q + 1) * 512].rearrange(
                        "p (tc i) -> p tc i", i=512),
                    in_=KS[0:q + 1, c_].rearrange("tc p i -> p tc i")))
                inc("y", d_k, 16)
            kg_ck[(qc, g)] = d_k.n

        def emit_vw_load(qc, g, w):
            vs = w % 2
            wt("y", c_pe, lastV_ck[vs])
            em("y", lambda e, w_=w, g_=g, s=vs: e.dma_start(
                out=m6[:, 1040 * s:1040 * (s + 1)].rearrange(
                    "p (kc i) -> p kc i", i=260),
                in_=VS[4 * w_:4 * (w_ + 1), :, 260 * g_:260 * (g_ + 1)]
                .rearrange("kc p i -> p kc i")))
            vw_ck[(g, w)] = inc("y", d_v, 16)

        def emit_x_load(qc, idx):
            j, oh2 = idx // 2, idx % 2
            s = idx % 2
            wt("y", c_dve, ck_pre1_slot[s])
            em("y", lambda e, q=qc, j_=j, o_=oh2, s_=s: e.dma_start(
                out=xs_sb[:, 512 * s_:512 * (s_ + 1)],
                in_=xh[q * 512 + 128 * j_:q * 512 + 128 * (j_ + 1),
                       512 * o_:512 * (o_ + 1)]))
            x_ck[(qc, idx)] = inc("y", d_x, 16)

        qt_cks = {}
        glast_qk = {g: 0 for g in range(NG)}

        def emit_qt_slice(qc, g):
            wt("y", c_pe, glast_qk[g])
            em("y", lambda e, q=qc, g_=g: e.dma_start(
                out=m1[:, 1024 * g_:1024 * (g_ + 1)],
                in_=QS[q, :, 1024 * g_:1024 * (g_ + 1)]))
            qt_cks[(qc, g)] = inc("y", d_q, 16)

        def emit_qc_head(qc):
            emit_kg_load(qc, 0)
            emit_kg_load(qc, 1)

        for qc in range(NT):
            nkc = 4 * (qc + 1)
            nwin = qc + 1
            if qc == 0:
                emit_qc_head(0)
                for g_ in range(NG):
                    emit_qt_slice(0, g_)
            first_qk_of_qc = True
            emit_x_load(qc, 0)
            emit_x_load(qc, 1)

            pending_epi = []
            pending_tailpv = None

            def make_epi_chunks(gp, pvlast, dep_wo):
                st_ = {}

                def chunkA():
                    wt("v", c_pe, max(pvlast.values()))
                    if not ck_state["last_rec"]:
                        wt("v", c_pool, memset_done)
                    em("v", lambda e: e.tensor_copy(dstf[0:1, :],
                                                    pF[0][64:65, :]))
                    em("v", lambda e: e.tensor_copy(dstf[32:33, :],
                                                    pF[1][64:65, :]))
                    em("v", lambda e: e.tensor_copy(dstf[64:65, :],
                                                    pF[2][64:65, :]))
                    em("v", lambda e: e.tensor_copy(dstf[96:97, :],
                                                    pF[3][64:65, :]))
                    vsp()
                    vsp()
                    wt("v", c_pe, max(rr_last_bc))
                    em("v", lambda e: _recip_bf(nc, e, rrv4b[0:97, 0:256],
                                                dstf[0:97, 0:256]))
                    em("v", lambda e: e.tensor_copy(
                        dstf[0:1, 0:1], rrv4b[96:97, 255:256]))
                    vsp()
                    vsp()
                    st_["recL"] = vinc_safe()
                    em("v", lambda e: _recip_bf(nc, e, rrv4b[0:97, 256:512],
                                                dstf[0:97, 256:512]))
                    em("v", lambda e: e.tensor_copy(
                        dstf[0:1, 0:1], rrv4b[96:97, 511:512]))
                    vsp()
                    vsp()
                    st_["rec"] = vinc_safe()
                    ck_state["last_rec"] = st_["rec"]

                def chunkB():
                    wt("t", c_dve, st_["recL"])
                    bcs = []
                    for hi in range(4):
                        mm(pF[hi][64:128, 0:256],
                           sel_sb[0:97, 64 * hi:64 * hi + 64],
                           rrv4b[0:97, 0:256])
                        bcs.append(inc("t", c_pe))
                    wt("t", c_dve, st_["rec"])
                    for hi in range(4):
                        mm(pF[hi][64:128, 256:512],
                           sel_sb[0:97, 64 * hi:64 * hi + 64],
                           rrv4b[0:97, 256:512])
                        bcs.append(inc("t", c_pe))
                    st_["bc"] = [bcs[4], bcs[5], bcs[6], bcs[7]]
                    rr_last_bc[0] = bcs[7]
                    rr_last_bc[1] = bcs[7]

                def chunkC():
                    wt("v", c_pe, st_["bc"][1])
                    em("v", lambda e: e.tensor_copy(rbc_sb[:, 0:512],
                                                    pF[0][64:128, :]))
                    em("v", lambda e: e.tensor_copy(rbc_sb[:, 512:1024],
                                                    pF[1][64:128, :]))
                    vsp()
                    if dep_wo:
                        wt("v", c_pe, dep_wo)
                    for hi in (0, 1):
                        h = 4 * gp + hi
                        em("v", lambda e, p=hi, c_=h // 2, pr=h % 2: e.tensor_mul(
                            bass.AP(m2, 64 * pr * 4096 + 512 * c_,
                                    [[4096, 64], [1, 512]]),
                            pF[p][0:64, :], rbc_sb[:, 512 * p:512 * (p + 1)]))
                        ck_mrg_last[hi] = inc("v", c_dve)

                def chunkD():
                    wt("v", c_pe, st_["bc"][3])
                    em("v", lambda e: e.tensor_copy(rbc_sb[:, 0:512],
                                                    pF[2][64:128, :]))
                    em("v", lambda e: e.tensor_copy(rbc_sb[:, 512:1024],
                                                    pF[3][64:128, :]))
                    vsp()
                    for hi in (2, 3):
                        h = 4 * gp + hi
                        em("v", lambda e, p=hi, c_=h // 2, pr=h % 2: e.tensor_mul(
                            bass.AP(m2, 64 * pr * 4096 + 512 * c_,
                                    [[4096, 64], [1, 512]]),
                            pF[p][0:64, :],
                            rbc_sb[:, 512 * (p - 2):512 * (p - 1)]))
                        ck_mrg_last[hi] = inc("v", c_dve)

                return [("A", chunkA), ("B", chunkB),
                        ("C", chunkC), ("D", chunkD)]

            for g in range(NG):
                tA, bA, rA, tB, bB, rB = kg_views(g)
                first_qk_of_group = True
                exp_pair = {}
                msk_par = {}
                pair_sp = {}
                if pending_tailpv:
                    tail_to_emit = pending_tailpv
                    pending_tailpv = None
                else:
                    tail_to_emit = None
                for kc in range(nkc):
                    par = kc % 2
                    diag_k = (kc // 4 == qc)
                    for p in range(2):
                        need = spidx - 2
                        if need >= 0 and need in exp_sp_cks:
                            wt("t", c_act, exp_sp_cks[need])
                        if first_qk_of_group:
                            wt("t", d_k, kg_ck[(qc, g)])
                            wt("t", d_q, qt_cks[(qc, g)])
                            first_qk_of_group = False
                        sb_pair = pS2[spidx % 2]
                        if first_qk_of_qc:
                            wt("t", c_dve, lastpTR_read)
                            first_qk_of_qc = False
                        for sub in range(2):
                            hi = 2 * p + sub
                            h = 4 * g + hi
                            c = h // 2
                            hp = h % 2
                            if c == 2 * g:
                                kt, kb, krl = tA, bA, rA
                            else:
                                kt, kb, krl = tB, bB, rB
                            lhsT = bass.AP(kt, 64 * hp * krl + kb + 128 * kc,
                                           [[krl, 64], [1, 128]])
                            rhs = bass.AP(m1, 64 * hp * 4096 + 512 * c,
                                          [[4096, 64], [1, 512]])
                            mm(sb_pair[:, 512 * sub:512 * (sub + 1)], lhsT, rhs)
                            qk = inc("t", c_pe)
                            if kc == nkc - 1 and hi == 3:
                                kslotQK[g % 2] = qk
                                prev_qk_m1 = qk
                                glast_qk[g] = qk
                        pair_sp[(kc, p)] = spidx
                        qk_cks[spidx] = qk
                        spidx += 1
                    if kc == 1 and tail_to_emit:
                        tail_to_emit()
                        tail_to_emit = None
                    if kc == 1:
                        emit_vw_load(qc, g, 0)
                    elif kc == 2 and nwin >= 2:
                        emit_vw_load(qc, g, 1)
                    if kc == 2:
                        while pending_epi:
                            pending_epi.pop(0)[1]()
                    kcms = []
                    if kc == 2:
                        kcms = [0, 1]
                    elif kc > 2:
                        kcms = [kc - 1]
                    for kcm in kcms:
                        mpar = kcm % 2
                        w = kcm // 4
                        if (kcm // 4) == qc:
                            wt("t", c_dve, msk_par[mpar])
                        else:
                            wt("t", c_act, max(exp_pair[(mpar, 0)],
                                               exp_pair[(mpar, 1)]))
                        if kcm == 0:
                            wt("t", d_v, vw_ck[(g, 0)])
                            if ck_lw_last:
                                wt("t", c_act, ck_lw_last)
                        elif kcm % 4 == 0:
                            wt("t", d_v, vw_ck[(g, w)])
                        for hi in range(4):
                            if kcm == 0:
                                wt("t", c_dve, ck_mrg_last[hi])
                                if qc > 0 and g == 0:
                                    if hi == 0 and ck_pre2_slot[0]:
                                        wt("t", c_dve, ck_pre2_slot[0])
                                    if hi == 1 and ck_pre2_slot[1]:
                                        wt("t", c_dve, ck_pre2_slot[1])
                                    if hi == 2 and ck_hbf_last:
                                        wt("t", c_dve, ck_hbf_last)
                                    if hi == 3 and ck_dve_lg_rd:
                                        wt("t", c_dve, ck_dve_lg_rd)
                            vs = w % 2
                            lhsT = bass.AP(
                                m6, 1040 * vs + 260 * (kcm % 4) + 65 * hi,
                                [[4096, 128], [1, 65]])
                            mm(pF[hi][0:65, :], lhsT, PT4(hi, mpar),
                               start=(kcm == 0), stop=False)
                            pv = inc("t", c_pe)
                            ck_pv[(hi, kcm)] = pv
                            pt_last_par[mpar][hi // 2] = pv
                            if hi == 3:
                                lastV_ck[vs] = pv
                                if kcm % 4 == 3 and w + 2 < nwin:
                                    emit_vw_load(qc, g, w + 2)
                    for p in range(2):
                        sp = pair_sp[(kc, p)]
                        sb_pair = pS2[sp % 2]
                        wt("c", c_pe, qk_cks[sp])
                        if kc >= 2:
                            wt("c", c_pe, max(ck_pv[(2 * p, kc - 2)],
                                              ck_pv[(2 * p + 1, kc - 2)]))
                        else:
                            wt("c", c_pe, pt_last_par[par].get(p, 0))
                            if qc == 0 and g == 0 and p == 0 and kc == 0:
                                wt("c", c_dve, ck_cb)
                        em("c", lambda e, b=sb_pair, g_=g, p_=p, pa=par:
                           e.activation(
                               m5[:, 1024 * (2 * pa + p_):
                                  1024 * (2 * pa + p_) + 1024],
                               b[:, :], AF.Exp,
                               bias=CBP_sb[:, 2 * g_ + p_:2 * g_ + p_ + 1]))
                        ec = inc("c", c_act)
                        exp_sp_cks[sp] = ec
                        exp_pair[(par, p)] = ec
                        ck_exp_last = ec
                        if diag_k:
                            r = kc - 4 * qc
                            wt("v", c_act, ec)
                            for sub in range(2):
                                hi = 2 * p + sub
                                em("v", lambda e, p2=hi, pa=par, r_=r:
                                   e.tensor_mul(
                                       PT4(p2, pa), PT4(p2, pa),
                                       msk_sb[:, 384 - 128 * r_:
                                              384 - 128 * r_ + 512]))
                            msk_par[par] = inc("v", c_dve)
                    if kc == 1 and pending_epi and pending_epi[0][0] == "A":
                        pending_epi.pop(0)[1]()

                if g + 2 < NG:
                    emit_kg_load(qc, g + 2)
                if qc + 1 < NT:
                    emit_qt_slice(qc + 1, g)

                assert not pending_epi and pending_tailpv is None

                def make_tail(gq, gg, nk, expd, mskd, wock):
                    def tail():
                        pvl = {}
                        for kcm in (nk - 1,):
                            mpar = kcm % 2
                            w = kcm // 4
                            wt("t", c_dve, mskd[mpar])
                            for hi in range(4):
                                vs = w % 2
                                lhsT = bass.AP(
                                    m6, 1040 * vs + 260 * (kcm % 4) + 65 * hi,
                                    [[4096, 128], [1, 65]])
                                mm(pF[hi][0:65, :], lhsT, PT4(hi, mpar),
                                   start=False, stop=(kcm == nk - 1))
                                pv = inc("t", c_pe)
                                ck_pv[(hi, kcm)] = pv
                                pt_last_par[mpar][hi // 2] = pv
                                pvl[hi] = pv
                                if hi == 3:
                                    lastV_ck[vs] = pv
                        chunks = make_epi_chunks(
                            gg, pvl, wock if gg == 0 else 0)
                        pending_epi.extend(chunks)
                    return tail

                pending_tailpv = make_tail(qc, g, nkc, dict(exp_pair),
                                           dict(msk_par), ck_wo_last)

            if pending_tailpv:
                pending_tailpv()
                pending_tailpv = None
            while pending_epi:
                pending_epi.pop(0)[1]()
            if qc + 1 < NT:
                emit_qc_head(qc + 1)

            mrg_all = max(ck_mrg_last.values())
            for idx in range(8):
                j, oh2 = idx // 2, idx % 2
                bank = pF[2 + idx % 2]
                if idx == 0:
                    wt("t", c_dve, mrg_all)
                    if qc == 0:
                        wt("t", d_ld, ld_b)
                if idx == 1 and ck_lw_last:
                    wt("t", c_act, ck_lw_last)
                if idx >= 2:
                    wt("t", c_dve, ck_pre1_slot[idx % 2])
                for c in range(NCHUNK):
                    mm(bank[:, :],
                       m2[:, 512 * c + 128 * j:512 * c + 128 * (j + 1)],
                       w3[:, 1024 * c + 512 * oh2:1024 * c + 512 * (oh2 + 1)],
                       start=(c == 0), stop=(c == NCHUNK - 1))
                wo_ck = inc("t", c_pe)
                if idx == 7:
                    ck_wo_last = wo_ck
                wt("v", c_pe, wo_ck)
                wt("v", d_x, x_ck[(qc, idx)])
                em("v", lambda e, b=bank, j_=j, o_=oh2, s=idx % 2:
                   e.scalar_tensor_tensor(
                       B1[:, 1024 * j_ + 512 * o_:1024 * j_ + 512 * (o_ + 1)],
                       b[:, :], 1.0, xs_sb[:, 512 * s:512 * (s + 1)],
                       op0=ALU.bypass, op1=ALU.add,
                       accum_out=acc_sb[:, 2 * j_ + o_:2 * j_ + o_ + 1]))
                p1 = inc("v", c_dve)
                ck_pre1_slot[idx % 2] = p1
                ck_pre1[(qc, j, oh2)] = p1
                if idx + 2 < 8:
                    emit_x_load(qc, idx + 2)

            x1T_cks = {}
            sq_cks = {}
            var_cks = {}
            sd_cks = {}
            for j in range(4):
                L = 16 * j
                wt("v", c_dve, ck_pre1[(qc, j, 1)])
                em("v", lambda e, j_=j, L_=L: e.tensor_add(
                    lnp[:, L_:L_ + 1], acc_sb[:, 2 * j_:2 * j_ + 1],
                    acc_sb[:, 2 * j_ + 1:2 * j_ + 2]))
                vsp()
                em("v", lambda e, L_=L: e.tensor_scalar_mul(
                    lnp[:, L_ + 1:L_ + 2], lnp[:, L_:L_ + 1], 1.0 / D))
                inc("v", c_dve)
                wt("c", c_dve, ck_pre1[(qc, j, 1)])
                if lnp_guard[j]:
                    wt("c", c_dve, lnp_guard[j])
                em("c", lambda e, j_=j, L_=L: e.activation(
                    sqs_v, B1[:, 1024 * j_:1024 * j_ + 512], AF.Square,
                    accum_out=lnp[:, L_ + 7:L_ + 8]))
                em("c", lambda e, j_=j, L_=L: e.activation(
                    sqs_v, B1[:, 1024 * j_ + 512:1024 * (j_ + 1)], AF.Square,
                    accum_out=lnp[:, L_ + 8:L_ + 9]))
                sq_cks[j] = ainc_safe()
            for j in range(4):
                L = 16 * j
                wt("v", c_act, sq_cks[j])
                vsp()
                em("v", lambda e, L_=L: e.tensor_scalar(
                    lnp[:, L_ + 2:L_ + 3], lnp[:, L_ + 1:L_ + 2],
                    lnp[:, L_ + 1:L_ + 2], EPS,
                    op0=ALU.mult, op1=ALU.subtract))
                em("v", lambda e, L_=L: e.tensor_add(
                    lnp[:, L_ + 9:L_ + 10], lnp[:, L_ + 7:L_ + 8],
                    lnp[:, L_ + 8:L_ + 9]))
                vsp()
                em("v", lambda e, L_=L: e.tensor_scalar(
                    lnp[:, L_ + 3:L_ + 4], lnp[:, L_ + 9:L_ + 10], 1.0 / D,
                    lnp[:, L_ + 2:L_ + 3],
                    op0=ALU.mult, op1=ALU.subtract))
                var_cks[j] = vinc_safe()
                lnp_guard[j] = var_cks[j]
            for j in range(4):
                L = 16 * j
                wt("c", c_dve, var_cks[j])
                em("c", lambda e, L_=L: e.activation(
                    lnp[:, L_ + 4:L_ + 5], lnp[:, L_ + 3:L_ + 4], AF.Sqrt))
                sd_cks[j] = ainc_safe()
            for j in range(4):
                L = 16 * j
                wt("v", c_act, sd_cks[j])
                em("v", lambda e, L_=L: e.reciprocal(
                    lnp[:, L_ + 5:L_ + 6], lnp[:, L_ + 4:L_ + 5]))
                vsp()
                em("v", lambda e, L_=L: e.scalar_tensor_tensor(
                    lnp[:, L_ + 6:L_ + 7], lnp[:, L_ + 1:L_ + 2], -1.0,
                    lnp[:, L_ + 5:L_ + 6],
                    op0=ALU.mult, op1=ALU.mult))
                vsp()
                if j == 0 and ck_out:
                    wt("v", d_out, ck_out)
                em("v", lambda e, j_=j, L_=L: e.tensor_scalar(
                    B2[:, 1024 * j_:1024 * (j_ + 1)],
                    B1[:, 1024 * j_:1024 * (j_ + 1)],
                    lnp[:, L_ + 5:L_ + 6], lnp[:, L_ + 6:L_ + 7],
                    op0=ALU.mult, op1=ALU.add))
                vsp()
                x1_ck = inc("v", c_dve)
                if ck_x1bf_rd:
                    wt("v", c_pe, ck_x1bf_rd)
                em("v", lambda e, j_=j: e.tensor_copy(
                    m4[:, 1024 * j_:1024 * (j_ + 1)],
                    B2[:, 1024 * j_:1024 * (j_ + 1)]))
                x1bf_ck = inc("v", c_dve)
                wt("t", c_dve, x1bf_ck)
                wt("t", c_dve, lastpTR_read)
                if j == 0:
                    wt("t", c_act, ck_exp_last)
                for c in range(NCHUNK):
                    tr(TRA[:, 128 * c:128 * (c + 1)],
                       m4[:, 1024 * j + 128 * c:1024 * j + 128 * (c + 1)])
                x1tr_ck = inc("t", c_pe)
                if j == 3:
                    ck_x1bf_rd = x1tr_ck
                wt("v", c_pe, x1tr_ck)
                if ck_base_last:
                    wt("v", c_pe, ck_base_last)
                em("v", lambda e, j_=j: e.tensor_copy(
                    bass.AP(m3, 128 * j_, [[4096, 128], [512, 8], [1, 128]]),
                    TRA.rearrange("p (c i) -> p c i", i=128)))
                x1T_cks[j] = inc("v", c_dve)
                lastpTR_read = x1T_cks[j]

            pT_cks = {}
            for j in range(4):
                wt("t", c_dve, x1T_cks[j])
                if j == 0:
                    wt("t", c_dve, ck_pre1_slot[0])
                    if ck_pre2_slot[0]:
                        wt("t", c_dve, ck_pre2_slot[0])
                    if ck_hbf_last:
                        wt("t", c_dve, ck_hbf_last)
                if ck_dve_wp_rd:
                    wt("t", c_dve, ck_dve_wp_rd)
                for c in range(NCHUNK):
                    mm(pF[2][0:128, 0:256],
                       bass.AP(m3, 512 * c + 128 * j, [[4096, 128], [1, 128]]),
                       WpT_sb[:, 256 * c:256 * (c + 1)],
                       start=(c == 0), stop=(c == NCHUNK - 1))
                wp_ck = inc("t", c_pe)
                wt("v", c_pe, wp_ck)
                em("v", lambda e, j_=j: e.tensor_add(
                    projf[:, 256 * j_:256 * (j_ + 1)],
                    pF[2][0:128, 0:256], bpb_sb[:, :]))
                projadd_ck = inc("v", c_dve)
                ck_dve_wp_rd = projadd_ck
                wt("c", c_dve, projadd_ck)
                em("c", lambda e, j_=j: e.activation(
                    sqp_v, projf[:, 256 * j_:256 * (j_ + 1)], AF.Square,
                    accum_out=rtp[:, 0:1]))
                asp()
                asp()
                em("c", lambda e: e.activation(rtp[:, 1:2], rtp[:, 0:1], AF.Sqrt))
                pr_ck = ainc_safe()
                wt("v", c_act, pr_ck)
                em("v", lambda e: e.tensor_scalar_max(rtp[:, 1:2], rtp[:, 1:2], 1e-12))
                vsp()
                em("v", lambda e: e.reciprocal(rtp[:, 2:3], rtp[:, 1:2]))
                vsp()
                em("v", lambda e, j_=j: e.tensor_scalar_mul(
                    m6[:, 2304 + 256 * j_:2304 + 256 * (j_ + 1)],
                    projf[:, 256 * j_:256 * (j_ + 1)], rtp[:, 2:3]))
                pbf_ck = inc("v", c_dve)
                wt("t", c_dve, pbf_ck)
                wt("t", c_dve, lastpTR_read)
                for pc in range(2):
                    tr(TRA[:, 128 * pc:128 * (pc + 1)],
                       m6[:, 2304 + 256 * j + 128 * pc:
                          2304 + 256 * j + 128 * (pc + 1)])
                ptr_ck = inc("t", c_pe)
                wt("v", c_pe, ptr_ck)
                em("v", lambda e, j_=j: e.tensor_copy(
                    bass.AP(m7, 128 * j_, [[4096, 128], [512, 2], [1, 128]]),
                    TRA.rearrange("p (c i) -> p c i", i=128)[:, 0:2, :]))
                pT_cks[j] = inc("v", c_dve)
                lastpTR_read = pT_cks[j]

            g4_cks = {}
            for j in range(4):
                wt("t", c_dve, pT_cks[j])
                if j == 0:
                    wt("t", c_dve, ck_pre1_slot[1])
                    if ck_pre2_slot[1]:
                        wt("t", c_dve, ck_pre2_slot[1])
                if ck_dve_lg_rd:
                    wt("t", c_dve, ck_dve_lg_rd)
                if ck_lw_last:
                    wt("t", c_act, ck_lw_last)
                for pc in range(2):
                    mm(pF[3][0:128, 0:4],
                       bass.AP(m7, 512 * pc + 128 * j, [[4096, 128], [1, 128]]),
                       simn_sb[:, 4 * pc:4 * (pc + 1)],
                       start=(pc == 0), stop=(pc == 1))
                lg_ck = inc("t", c_pe)
                wt("v", c_pe, lg_ck)
                em("v", lambda e: e.reduce_max(
                    rtp[:, 3:4], pF[3][0:128, 0:4], AX.X, negate=True))
                nm_ck = vinc_safe()
                ck_dve_lg_rd = nm_ck
                wt("c", c_dve, nm_ck)
                em("c", lambda e: e.activation(
                    lw_sb[:, :], pF[3][0:128, 0:4], AF.Exp, bias=rtp[:, 3:4]))
                lw_ck = ainc_safe()
                ck_lw_last = lw_ck
                wt("v", c_act, lw_ck)
                em("v", lambda e: e.reduce_sum(rtp[:, 4:5], lw_sb[:, :], AX.X))
                em("v", lambda e: e.reduce_max(rtp[:, 6:7], lw_sb[:, :], AX.X))
                em("v", lambda e, j_=j: e.reciprocal(
                    rtp[:, 8 + j_:9 + j_], rtp[:, 4:5]))
                em("v", lambda e, j_=j: e.scalar_tensor_tensor(
                    g4_sb[:, 4 * j_:4 * (j_ + 1)], lw_sb[:, :], rtp[:, 6:7],
                    lw_sb[:, :], op0=ALU.is_ge, op1=ALU.mult))
                vsp()
                g4_cks[j] = inc("v", c_dve)

            hT_cks = {}
            for j in range(4):
                if ck_hbf_last:
                    wt("t", c_dve, ck_hbf_last)
                wt("t", c_dve, ck_dve_wp_rd)
                for c in range(NCHUNK):
                    mm(pF[2][0:128, 0:32],
                       bass.AP(m3, 512 * c + 128 * j, [[4096, 128], [1, 128]]),
                       WdT_sb[:, 32 * c:32 * (c + 1)],
                       start=(c == 0), stop=(c == NCHUNK - 1))
                dn_ck = inc("t", c_pe)
                wt("v", c_pe, dn_ck)
                wt("v", c_dve, g4_cks[j])
                for e_i in range(4):
                    em("v", lambda e, j_=j, ei=e_i: e.tensor_scalar(
                        m6[:, 3328 + 32 * j_ + 8 * ei:
                           3328 + 32 * j_ + 8 * (ei + 1)],
                        pF[2][0:128, 8 * ei:8 * (ei + 1)],
                        g4_sb[:, 4 * j_ + ei:4 * j_ + ei + 1],
                        rtp[:, 8 + j_:9 + j_],
                        op0=ALU.mult, op1=ALU.mult))
                hbf_ck = vinc_safe()
                ck_hbf_last = hbf_ck
                ck_state["hbf_for_bc"] = hbf_ck
                wt("t", c_dve, hbf_ck)
                wt("t", c_dve, lastpTR_read)
                em("t", lambda e, j_=j: e.transpose(
                    TRA[0:32, 0:128],
                    m6[:, 3328 + 32 * j_:3328 + 32 * (j_ + 1)],
                    id_sb[:, :]))
                htr_ck = inc("t", c_pe)
                wt("v", c_pe, htr_ck)
                em("v", lambda e, j_=j: e.tensor_copy(
                    bass.AP(m7, hT_off + 128 * j_, [[4096, 32], [1, 128]]),
                    TRA[0:32, 0:128]))
                hT_cks[j] = vinc_safe()
                lastpTR_read = hT_cks[j]

            for j in range(4):
                for oh2 in range(2):
                    bank = pF[0 + oh2]
                    wt("t", c_dve, hT_cks[j])
                    if j == 0:
                        wt("t", c_dve, max(ck_mrg_last.values()))
                        if oh2 == 0:
                            wt("t", c_dve, ck_dve_wp_rd)
                        else:
                            wt("t", c_dve, ck_dve_lg_rd)
                            wt("t", c_act, ck_lw_last)
                    if ck_pre2_slot[oh2]:
                        wt("t", c_dve, ck_pre2_slot[oh2])
                    for c in range(NCHUNK):
                        mm(bank[:, :],
                           bass.AP(m3, 512 * c + 128 * j, [[4096, 128], [1, 128]]),
                           w2[:, 1024 * c + 512 * oh2:1024 * c + 512 * (oh2 + 1)],
                           start=(c == 0), stop=False)
                    mm(bank[:, :],
                       bass.AP(m7, hT_off + 128 * j, [[4096, 32], [1, 128]]),
                       WuT_sb[0:32, 512 * oh2:512 * (oh2 + 1)],
                       start=False, stop=True)
                    bs_ck = inc("t", c_pe)
                    ck_base_last = bs_ck
                    wt("v", c_pe, bs_ck)
                    em("v", lambda e, b=bank, j_=j, o_=oh2: e.scalar_tensor_tensor(
                        B1[:, 1024 * j_ + 512 * o_:1024 * j_ + 512 * (o_ + 1)],
                        b[:, :], 1.0,
                        B2[:, 1024 * j_ + 512 * o_:1024 * j_ + 512 * (o_ + 1)],
                        op0=ALU.bypass, op1=ALU.add,
                        accum_out=acc_sb[:, 8 + 2 * j_ + o_:9 + 2 * j_ + o_]))
                    p2 = inc("v", c_dve)
                    ck_pre2_slot[oh2] = p2
                    ck_pre2_last = p2
                    pre2_cks[(j, oh2)] = p2

            sq2_cks = {}
            v2_cks = {}
            sd2_cks = {}
            for j in range(4):
                L = 16 * j
                pj = max(pre2_cks[(j, 0)], pre2_cks[(j, 1)])
                wt("v", c_dve, pj)
                em("v", lambda e, j_=j, L_=L: e.tensor_add(
                    lnp[:, L_:L_ + 1], acc_sb[:, 8 + 2 * j_:9 + 2 * j_],
                    acc_sb[:, 9 + 2 * j_:10 + 2 * j_]))
                vsp()
                em("v", lambda e, L_=L: e.tensor_scalar_mul(
                    lnp[:, L_ + 1:L_ + 2], lnp[:, L_:L_ + 1], 1.0 / D))
                inc("v", c_dve)
                wt("c", c_dve, pj)
                if lnp_guard[j]:
                    wt("c", c_dve, lnp_guard[j])
                em("c", lambda e, j_=j, L_=L: e.activation(
                    sqs_v, B1[:, 1024 * j_:1024 * j_ + 512], AF.Square,
                    accum_out=lnp[:, L_ + 7:L_ + 8]))
                em("c", lambda e, j_=j, L_=L: e.activation(
                    sqs_v, B1[:, 1024 * j_ + 512:1024 * (j_ + 1)], AF.Square,
                    accum_out=lnp[:, L_ + 8:L_ + 9]))
                sq2_cks[j] = ainc_safe()
            for j in range(4):
                L = 16 * j
                wt("v", c_act, sq2_cks[j])
                vsp()
                em("v", lambda e, L_=L: e.tensor_scalar(
                    lnp[:, L_ + 2:L_ + 3], lnp[:, L_ + 1:L_ + 2],
                    lnp[:, L_ + 1:L_ + 2], EPS,
                    op0=ALU.mult, op1=ALU.subtract))
                em("v", lambda e, L_=L: e.tensor_add(
                    lnp[:, L_ + 9:L_ + 10], lnp[:, L_ + 7:L_ + 8],
                    lnp[:, L_ + 8:L_ + 9]))
                vsp()
                em("v", lambda e, L_=L: e.tensor_scalar(
                    lnp[:, L_ + 3:L_ + 4], lnp[:, L_ + 9:L_ + 10], 1.0 / D,
                    lnp[:, L_ + 2:L_ + 3],
                    op0=ALU.mult, op1=ALU.subtract))
                v2_cks[j] = vinc_safe()
                lnp_guard[j] = v2_cks[j]
            for j in range(4):
                L = 16 * j
                wt("c", c_dve, v2_cks[j])
                em("c", lambda e, L_=L: e.activation(
                    lnp[:, L_ + 4:L_ + 5], lnp[:, L_ + 3:L_ + 4], AF.Sqrt))
                sd2_cks[j] = ainc_safe()
            for j in range(4):
                L = 16 * j
                wt("v", c_act, sd2_cks[j])
                em("v", lambda e, L_=L: e.reciprocal(
                    lnp[:, L_ + 5:L_ + 6], lnp[:, L_ + 4:L_ + 5]))
                vsp()
                em("v", lambda e, L_=L: e.scalar_tensor_tensor(
                    lnp[:, L_ + 6:L_ + 7], lnp[:, L_ + 1:L_ + 2], -1.0,
                    lnp[:, L_ + 5:L_ + 6],
                    op0=ALU.mult, op1=ALU.mult))
                vsp()
                em("v", lambda e, j_=j, L_=L: e.tensor_scalar(
                    B2[:, 1024 * j_:1024 * (j_ + 1)],
                    B1[:, 1024 * j_:1024 * (j_ + 1)],
                    lnp[:, L_ + 5:L_ + 6], lnp[:, L_ + 6:L_ + 7],
                    op0=ALU.mult, op1=ALU.add))
                vsp()
                ck_ln2_j = inc("v", c_dve)
                wt("g", c_dve, ck_ln2_j)
                em("g", lambda e, q=qc, j_=j: e.dma_start(
                    out=out_h[q * 512 + 128 * j_:q * 512 + 128 * (j_ + 1), :],
                    in_=B2[:, 1024 * j_:1024 * (j_ + 1)]))
                ck_out = inc("g", d_out, 16)

        em("y", lambda e: e.wait_ge(d_out.h, ck_out))

        def runner(lst):
            def f(e):
                with nc.allow_low_precision(reason="bf16 pipeline per problem tolerance"):
                    for fn in lst:
                        fn(e)
            return f

        with nc.Block() as blk:
            blk.tensor(runner(prog["t"]))
            blk.vector(runner(prog["v"]))
            blk.scalar(runner(prog["c"]))
            blk.gpsimd(runner(prog["g"]))
            blk.sync(runner(prog["y"]))

    return nc


def _recip_bf(nc, e, out, in_):
    with nc.allow_low_precision(reason="recip feeds bf16 matmul"):
        return e.reciprocal(out, in_)


def _make_host_constants():
    pos = np.arange(S_LEN, dtype=np.float32)
    freqs = (1.0 / (10000.0 ** (np.arange(0, HD, 2, dtype=np.float32) / HD))
             ).astype(np.float32)
    ang = pos[:, None] * freqs[None, :]
    cosf = np.repeat(np.cos(ang).astype(np.float32), 2, axis=1)
    sinf = np.repeat(np.sin(ang).astype(np.float32), 2, axis=1)
    cosm = np.ascontiguousarray(np.tile(cosf.T, (2, 1)))
    sinm = np.ascontiguousarray(np.tile(sinf.T, (2, 1)))

    ident = np.eye(128, dtype=ml_dtypes.bfloat16)
    rmat = np.zeros((128, 128), dtype=np.float32)
    for blk in (0, 64):
        for i in range(32):
            rmat[blk + 2 * i, blk + 2 * i + 1] = 1.0
            rmat[blk + 2 * i + 1, blk + 2 * i] = -1.0
    rmat = rmat.astype(ml_dtypes.bfloat16)

    hsum = np.zeros((128, 128), dtype=np.float32)
    for c in range(8):
        hsum[0:64, 16 * c + 2 * c] = 1.0
        hsum[64:128, 16 * c + 2 * c + 1] = 1.0
    hsum = hsum.astype(ml_dtypes.bfloat16)

    iu = np.arange(896)[None, :]
    ii = np.arange(128)[:, None]
    masks = (iu >= ii + 384).astype(ml_dtypes.bfloat16)

    onesb = np.ones((1, 128), dtype=ml_dtypes.bfloat16)
    onesf = np.ones((1, 128), dtype=np.float32)
    selm = np.zeros((128, 256), dtype=ml_dtypes.bfloat16)
    for h in range(4):
        selm[32 * h, 64 * h:64 * (h + 1)] = 1.0
    return cosm, sinm, ident, rmat, hsum, masks, onesb, onesf, selm


def kernel(L0_attn_x, L0_attn_wq, L0_attn_wk, L0_attn_wv, L0_attn_wo,
           L0_ffn_W, L0_ffn_Wp, L0_ffn_bp, L0_ffn_sim, L0_ffn_temp,
           L0_ffn_Wd0, L0_ffn_Wu0, L0_ffn_Wd1, L0_ffn_Wu1,
           L0_ffn_Wd2, L0_ffn_Wu2, L0_ffn_Wd3, L0_ffn_Wu3):
    x = np.asarray(L0_attn_x, dtype=np.float32)
    assert x.shape == (8, S_LEN, D)

    wqT = np.ascontiguousarray(np.asarray(L0_attn_wq, np.float32).T)
    wkT = np.ascontiguousarray(np.asarray(L0_attn_wk, np.float32).T)
    wvT = np.ascontiguousarray(np.asarray(L0_attn_wv, np.float32).T)
    woT = np.ascontiguousarray(np.asarray(L0_attn_wo, np.float32).T)
    WT = np.ascontiguousarray(np.asarray(L0_ffn_W, np.float32).T)
    WpT = np.ascontiguousarray(np.asarray(L0_ffn_Wp, np.float32).T)

    Wd = [np.asarray(w, np.float32) for w in
          (L0_ffn_Wd0, L0_ffn_Wd1, L0_ffn_Wd2, L0_ffn_Wd3)]
    Wu = [np.asarray(w, np.float32) for w in
          (L0_ffn_Wu0, L0_ffn_Wu1, L0_ffn_Wu2, L0_ffn_Wu3)]
    WdTp = np.zeros((D, 32), dtype=np.float32)
    WuTp = np.zeros((32, D), dtype=np.float32)
    for e_i, r_e in enumerate((1, 2, 4, 8)):
        WdTp[:, 8 * e_i:8 * e_i + r_e] = Wd[e_i].T
        WuTp[8 * e_i:8 * e_i + r_e, :] = Wu[e_i].T
    simT = np.ascontiguousarray(np.asarray(L0_ffn_sim, np.float32).T)
    tempv = np.asarray(L0_ffn_temp, np.float32).reshape(1, 1)
    bp_bc = np.ascontiguousarray(
        np.tile(np.asarray(L0_ffn_bp, np.float32)[None, :], (128, 1)))

    (cosm, sinm, ident, rmat, hsum, masks, onesb, onesf,
     selm) = _make_host_constants()

    shared = dict(wqT=wqT, wkT=wkT, wvT=wvT, woT=woT, WT=WT, WpT=WpT,
                  WdTp=WdTp, WuTp=WuTp, simT=simT, tempv=tempv, bp_bc=bp_bc,
                  cosm=cosm, sinm=sinm, identb=ident, rmatb=rmat,
                  headsum=hsum, masksb=masks, onesb=onesb, onesf=onesf,
                  selm=selm)

    in_maps = []
    for b in range(8):
        xb = np.ascontiguousarray(x[b])
        m = dict(shared)
        m["x"] = xb
        m["xT"] = np.ascontiguousarray(xb.T)
        in_maps.append(m)

    nc = build_program()
    res = run_bass_kernel_spmd(nc, in_maps, core_ids=list(range(8)))
    return np.stack([np.asarray(r["o"], dtype=np.float32) for r in res.results])


# version 58 (final)
# speedup vs baseline: 1.0227x; 1.0173x over previous
from contextlib import ExitStack

import numpy as np
import ml_dtypes

import concourse.bass as bass
import concourse.mybir as mybir
from concourse.bass_utils import run_bass_kernel_spmd

F32 = mybir.dt.float32
BF16 = mybir.dt.bfloat16
AF = mybir.ActivationFunctionType
ALU = mybir.AluOpType
AX = mybir.AxisListType

S_LEN = 4096
D = 1024
HD = 64
NT = 8
TT = 512
NCHUNK = 8
NG = 4
CLAMP_MAX = float(np.log(1.0 / 0.01))
EPS = 1e-5


class Ctr:
    def __init__(self, st, nc, name):
        self.h = st.enter_context(nc.semaphore(name))
        self.n = 0

    def tick(self, k=1):
        self.n += k
        return self.n


def build_program():
    nc = bass.Bass("TRN2", target_bir_lowering=False)

    xh = nc.dram_tensor("x", [S_LEN, D], F32, kind="ExternalInput")
    xTh = nc.dram_tensor("xT", [D, S_LEN], F32, kind="ExternalInput")
    wqh = nc.dram_tensor("wqT", [D, D], F32, kind="ExternalInput")
    wkh = nc.dram_tensor("wkT", [D, D], F32, kind="ExternalInput")
    wvh = nc.dram_tensor("wvT", [D, D], F32, kind="ExternalInput")
    woh = nc.dram_tensor("woT", [D, D], F32, kind="ExternalInput")
    Wh = nc.dram_tensor("WT", [D, D], F32, kind="ExternalInput")
    Wph = nc.dram_tensor("WpT", [D, 256], F32, kind="ExternalInput")
    Wdh = nc.dram_tensor("WdTp", [D, 32], F32, kind="ExternalInput")
    Wuh = nc.dram_tensor("WuTp", [32, D], F32, kind="ExternalInput")
    simh = nc.dram_tensor("simT", [4, 256], F32, kind="ExternalInput")
    temph = nc.dram_tensor("tempv", [1, 1], F32, kind="ExternalInput")
    bph = nc.dram_tensor("bp_bc", [128, 256], F32, kind="ExternalInput")
    cosmh = nc.dram_tensor("cosm", [128, S_LEN], F32, kind="ExternalInput")
    sinmh = nc.dram_tensor("sinm", [128, S_LEN], F32, kind="ExternalInput")
    identh = nc.dram_tensor("identb", [128, 128], BF16, kind="ExternalInput")
    rmath = nc.dram_tensor("rmatb", [128, 128], BF16, kind="ExternalInput")
    hsumh = nc.dram_tensor("headsum", [128, 128], BF16, kind="ExternalInput")
    maskh = nc.dram_tensor("masksb", [128, 896], BF16, kind="ExternalInput")
    onesbh = nc.dram_tensor("onesb", [1, 128], BF16, kind="ExternalInput")
    onesfh = nc.dram_tensor("onesf", [1, 128], F32, kind="ExternalInput")
    selh = nc.dram_tensor("selm", [128, 256], BF16, kind="ExternalInput")
    out_h = nc.dram_tensor("o", [S_LEN, D], F32, kind="ExternalOutput")

    QS = nc.dram_tensor("QS", [NT, 128, 4096], BF16, kind="Internal")
    KS = nc.dram_tensor("KS", [NT, NCHUNK, 128, 512], BF16, kind="Internal")
    VS = nc.dram_tensor("VS", [32, 128, 1040], BF16, kind="Internal")
    CHD = nc.dram_tensor("CHD", [1, 16], F32, kind="Internal")

    with ExitStack() as st:
        def sb(name, shape, dt):
            return st.enter_context(nc.sbuf_tensor(name, shape, dt))

        def mkpsum(name, shape, dt):
            return st.enter_context(nc.psum_tensor(name, shape, dt))

        id_sb = sb("id_sb", [128, 128], BF16)
        rm_sb = sb("rm_sb", [128, 128], BF16)
        hs_sb = sb("hs_sb", [128, 128], BF16)
        msk_sb = sb("msk_sb", [128, 896], BF16)
        ones_sb = sb("ones_sb", [1, 128], BF16)
        onesf_sb = sb("onesf_sb", [1, 128], F32)
        bpb_sb = sb("bpb_sb", [128, 256], F32)
        simT_sb = sb("simT_sb", [4, 256], F32)
        smq_sb = sb("smq_sb", [4, 256], F32)
        simbf_sb = sb("simbf_sb", [4, 256], BF16)
        temp_sb = sb("temp_sb", [1, 1], F32)
        simn_sb = sb("simn_sb", [128, 8], BF16)
        chrow_sb = sb("chrow_sb", [1, 16], F32)
        CB_sb = sb("CB_sb", [128, 16], F32)
        CBP_sb = sb("CBP_sb", [128, 8], F32)
        chp_sb = sb("chp_sb", [1, 8], F32)
        qmax2_sb = sb("qmax2_sb", [16, 1], F32)
        kmax2_sb = sb("kmax2_sb", [16, 1], F32)
        qmx_sb = sb("qmx_sb", [16, 1], F32)
        kmx_sb = sb("kmx_sb", [16, 1], F32)
        ch_sb = sb("ch_sb", [16, 1], F32)
        n2t_sb = sb("n2t_sb", [16, 2], F32)
        WpT_sb = sb("WpT_sb", [128, 2048], BF16)
        WdT_sb = sb("WdT_sb", [128, 256], BF16)
        WuT_sb = sb("WuT_sb", [32, 1024], BF16)
        lnp = sb("lnp", [128, 64], F32)
        rtp = sb("rtp", [128, 16], F32)
        lw_sb = sb("lw_sb", [128, 4], F32)
        g4_sb = sb("g4_sb", [128, 16], F32)
        acc_sb = sb("acc_sb", [128, 20], F32)
        spc_sb = sb("spc_sb", [128, 4], F32)
        rbc_sb = sb("rbc_sb", [64, 1024], BF16)
        dstf = sb("dstf", [128, 512], F32)
        rrv4b = sb("rrv4b", [128, 512], BF16)
        sel_sb = sb("sel_sb", [128, 256], BF16)
        projf = sb("projf", [128, 1024], F32)
        xs_sb = sb("xs_sb", [128, 1024], F32)
        VpT = sb("VpT", [128, 2080], BF16)
        sqA = sb("sqA", [128, 1024], BF16)
        t1_sb = sb("t1_sb", [128, 1024], BF16)
        t2_sb = sb("t2_sb", [128, 1024], BF16)

        w1 = sb("w1", [128, 8192], BF16)
        w2 = sb("w2", [128, 8192], BF16)
        w3 = sb("w3", [128, 8192], BF16)
        m1 = sb("m1", [128, 4096], BF16)
        m2 = sb("m2", [128, 4096], BF16)
        m3 = sb("m3", [128, 4096], BF16)
        m4 = sb("m4", [128, 4096], BF16)
        m5 = sb("m5", [128, 4096], BF16)
        m6 = sb("m6", [128, 4096], BF16)
        m7 = sb("m7", [128, 4096], BF16)
        m8 = sb("m8", [128, 4096], BF16)
        m9 = sb("m9", [128, 4096], BF16)
        B1 = sb("B1", [128, 4096], F32)
        B2 = sb("B2", [128, 4096], F32)

        pF = [mkpsum(f"pF{i}", [128, 512], F32) for i in range(4)]
        pSA = mkpsum("pSA", [128, 1024], F32)
        pSB = mkpsum("pSB", [128, 1024], F32)
        pS2 = [pSA, pSB]
        TRA = pSA[:, 0:512].bitcast(BF16)
        TRB = pSB[:, 512:1024].bitcast(BF16)

        c_pe = Ctr(st, nc, "c_pe")
        c_dve = Ctr(st, nc, "c_dve")
        c_act = Ctr(st, nc, "c_act")
        c_pool = Ctr(st, nc, "c_pool")
        d_ld = Ctr(st, nc, "d_ld")
        d_xt = Ctr(st, nc, "d_xt")
        d_scw = Ctr(st, nc, "d_scw")
        d_q = Ctr(st, nc, "d_q")
        d_k = Ctr(st, nc, "d_k")
        d_v = Ctr(st, nc, "d_v")
        d_x = Ctr(st, nc, "d_x")
        d_out = Ctr(st, nc, "d_out")
        c_ph = Ctr(st, nc, "c_ph")
        ENG = {"t": c_pe, "v": c_dve, "c": c_act, "g": c_pool}

        prog = {k: [] for k in "tvcgy"}

        def em(q, fn):
            prog[q].append(fn)

        def wt(q, ctr, val):
            assert val <= ctr.n, f"wait ahead of producer: {val} > {ctr.n} on {q}"
            if val > 0:
                em(q, lambda e, h=ctr.h, v=val: e.wait_ge(h, v))

        def inc(q, ctr, k=1):
            last = prog[q][-1]

            def fn(e, last=last, h=ctr.h, k=k):
                last(e).then_inc(h, k)
            prog[q][-1] = fn
            return ctr.tick(k)

        def vsp():
            em("v", lambda e: e.tensor_scalar_mul(
                spc_sb[:, 0:1], spc_sb[:, 1:2], 1.0))

        def vinc_safe():
            vsp()
            return inc("v", c_dve)

        def asp():
            em("c", lambda e: e.mul(spc_sb[:, 2:3], spc_sb[:, 3:4], 1.0))

        def ainc_safe():
            asp()
            return inc("c", c_act)

        def mm(out, lhsT, rhs, start=True, stop=True):
            em("t", lambda e, o=out, l=lhsT, r=rhs, s1=start, s2=stop:
               e.matmul(o, l, r, start=s1, stop=s2))

        def tr(out, in_):
            kp = in_.shape[0]
            em("t", lambda e, o=out, i=in_, k=kp: e.transpose(
                o, i, id_sb[0:k, 0:k]))

        for dst, src in [
            (id_sb[:, :], identh[:, :]), (rm_sb[:, :], rmath[:, :]),
            (hs_sb[:, :], hsumh[:, :]), (msk_sb[:, :], maskh[:, :]),
            (ones_sb[:, :], onesbh[:, :]), (onesf_sb[:, :], onesfh[:, :]),
            (sel_sb[:, :], selh[:, :]),
            (bpb_sb[:, :], bph[:, :]), (simT_sb[:, :], simh[:, :]),
            (temp_sb[:, :], temph[:, :]),
        ]:
            em("y", lambda e, d=dst, s=src: e.dma_start(out=d, in_=s))
            inc("y", d_ld, 16)
        ld_plain = d_ld.n

        for dst, src in [
            (w1[:, :].rearrange("p (c o) -> p c o", o=1024),
             wqh[:, :].rearrange("(c p) o -> p c o", p=128)),
            (w2[:, :].rearrange("p (c o) -> p c o", o=1024),
             wkh[:, :].rearrange("(c p) o -> p c o", p=128)),
            (w3[:, :].rearrange("p (c o) -> p c o", o=1024),
             wvh[:, :].rearrange("(c p) o -> p c o", p=128)),
            (m1[:, :], cosmh[:, :]),
            (m2[:, :], sinmh[:, :]),
            (WpT_sb[:, :].rearrange("p (c o) -> p c o", o=256),
             Wph[:, :].rearrange("(c p) o -> p c o", p=128)),
            (WdT_sb[:, :].rearrange("p (c o) -> p c o", o=32),
             Wdh[:, :].rearrange("(c p) o -> p c o", p=128)),
            (WuT_sb[:, :], Wuh[:, :]),
        ]:
            em("g", lambda e, d=dst, s=src: e.dma_start(out=d, in_=s))
            inc("g", d_ld, 16)
        ld_all = d_ld.n

        em("g", lambda e: e.memset(spc_sb[:, :], 0.0))
        em("g", lambda e: e.memset(qmax2_sb[:, :], 0.0))
        em("g", lambda e: e.memset(kmax2_sb[:, :], 0.0))
        em("g", lambda e: e.memset(
            bass.AP(VpT, 64, [[2080, 128], [65, 32], [1, 1]]), 1.0))
        em("g", lambda e: e.memset(dstf[:, :], 1.0))
        memset_done = inc("g", c_pool)

        em("g", lambda e: e.dma_start(
            out=m3[:, :].rearrange("p (c t) -> p c t", t=TT),
            in_=xTh[:, 0:TT].rearrange("(c p) t -> p c t", p=128)))
        inc("g", d_xt, 16)

        wt("c", d_ld, ld_all)
        em("c", lambda e: e.activation(smq_sb[:, :], simT_sb[:, :], AF.Square))
        sm1 = inc("c", c_act)
        wt("v", c_act, sm1)
        em("v", lambda e: e.reduce_sum(rtp[0:4, 0:1], smq_sb[:, :], AX.X))
        sm2 = vinc_safe()
        wt("c", c_dve, sm2)
        em("c", lambda e: e.activation(rtp[0:4, 1:2], rtp[0:4, 0:1], AF.Sqrt))
        sm3 = ainc_safe()
        wt("v", c_act, sm3)
        em("v", lambda e: e.tensor_scalar_max(rtp[0:4, 1:2], rtp[0:4, 1:2], 1e-12))
        vsp()
        em("v", lambda e: e.reciprocal(rtp[0:4, 2:3], rtp[0:4, 1:2]))
        em("v", lambda e: e.tensor_scalar_min(rtp[0:1, 3:4], temp_sb[:, :], CLAMP_MAX))
        sm4 = vinc_safe()
        wt("c", c_dve, sm4)
        em("c", lambda e: e.activation(rtp[0:1, 4:5], rtp[0:1, 3:4], AF.Exp))
        sm5 = ainc_safe()
        wt("t", c_act, sm5)
        mm(pSB[0:4, 0:1], onesf_sb[0:1, 0:4], rtp[0:1, 4:5])
        sm6 = inc("t", c_pe)
        wt("v", c_pe, sm6)
        em("v", lambda e: e.tensor_copy(rtp[0:4, 5:6], pSB[0:4, 0:1]))
        vsp()
        em("v", lambda e: e.tensor_mul(rtp[0:4, 6:7], rtp[0:4, 2:3], rtp[0:4, 5:6]))
        vsp()
        em("v", lambda e: e.tensor_scalar_mul(
            simbf_sb[:, :], simT_sb[:, :], rtp[0:4, 6:7]))
        sm7 = vinc_safe()
        wt("t", c_dve, sm7)
        wt("t", d_ld, ld_plain)
        tr(TRB[0:128, 0:4], simbf_sb[0:4, 0:128])
        tr(TRB[0:128, 4:8], simbf_sb[0:4, 128:256])
        sm8 = inc("t", c_pe)
        wt("v", c_pe, sm8)
        em("v", lambda e: e.tensor_copy(simn_sb[:, :], TRB[0:128, 0:8]))
        lastpTR_read = inc("v", c_dve)
        sm7x = lastpTR_read

        xt = [m3, m4]
        ck_qkv_last = {}
        ck_evac = []
        ck_qpl, ck_kpl, ck_vsb = {}, {}, {}
        ck_rope, ck_t2, ck_addidx, ck_sq, ck_t1 = {}, {}, {}, {}, {}
        ck_n2done = {}
        rec_vp = [0, 0]
        rec_qtb = 0
        rec_ktb = 0
        gidx = 0
        ridx = 0
        tidx = 0
        ck_hsq_last = 0
        ck_hsk_last = 0
        ck_n2done_prev = 0
        ck_vtr_last = {c: 0 for c in range(NCHUNK)}
        hs_byc = {}

        for tc in range(NT):
            par = tc % 2
            if tc + 1 < NT:
                if tc >= 1:
                    wt("g", c_pe, ck_qkv_last[tc - 1])
                em("g", lambda e, t=tc + 1: e.dma_start(
                    out=xt[t % 2][:, :].rearrange("p (c t) -> p c t", t=TT),
                    in_=xTh[:, t * TT:(t + 1) * TT].rearrange(
                        "(c p) t -> p c t", p=128)))
                inc("g", d_xt, 16)
            xt_ld = 16 * (tc + 1)

            first_mm_of_tile = True
            for w in range(3):
                wsb = [w1, w2, w3][w]
                for oc in range(NCHUNK):
                    bank = pF[gidx % 4]
                    if gidx >= 4:
                        engq, cnt = ck_evac[gidx - 4]
                        wt("t", ENG[engq], cnt)
                    if first_mm_of_tile:
                        wt("t", d_xt, xt_ld)
                        first_mm_of_tile = False
                    for c in range(NCHUNK):
                        mm(bank[:, :],
                           wsb[:, 1024 * c + 128 * oc:1024 * c + 128 * oc + 128],
                           xt[par][:, TT * c:TT * (c + 1)],
                           start=(c == 0), stop=(c == NCHUNK - 1))
                    g_ck = inc("t", c_pe)
                    if w == 2 and oc == NCHUNK - 1:
                        ck_qkv_last[tc] = g_ck
                    if w == 0:
                        wt("c", c_pe, g_ck)
                        if tc > 0:
                            wt("c", c_dve, ck_rope[("q", tc - 1, oc)])
                            wt("c", c_pe, ck_hsq_last)
                        em("c", lambda e, b=bank, o=oc: e.activation(
                            m5[:, TT * o:TT * (o + 1)], b[:, :], AF.Copy,
                            scale=0.125))
                        ck_qpl[(tc, oc)] = inc("c", c_act)
                        ck_evac.append(("c", ck_qpl[(tc, oc)]))
                    elif w == 1:
                        wt("c", c_pe, g_ck)
                        if tc > 0:
                            wt("c", c_dve, ck_rope[("k", tc - 1, oc)])
                            wt("c", c_pe, ck_hsk_last)
                        em("c", lambda e, b=bank, o=oc: e.activation(
                            m6[:, TT * o:TT * (o + 1)], b[:, :], AF.Copy,
                            scale=1.0))
                        ck_kpl[(tc, oc)] = inc("c", c_act)
                        ck_evac.append(("c", ck_kpl[(tc, oc)]))
                    else:
                        wt("v", c_pe, g_ck)
                        if tc > 0:
                            wt("v", c_pe, ck_vtr_last[oc])
                        em("v", lambda e, b=bank, o=oc: e.tensor_copy(
                            m9[:, TT * o:TT * (o + 1)], b[:, :]))
                        ck_vsb[(tc, oc)] = inc("v", c_dve)
                        ck_evac.append(("v", ck_vsb[(tc, oc)]))
                    gidx += 1

            for which, pl, tgt, n2bank, ckd in (
                    ("q", m5, m7, pSB[:, 0:512], ck_qpl),
                    ("k", m6, m8, pSB[:, 0:512], ck_kpl)):
                scw_gate = rec_qtb if which == "q" else rec_ktb
                for c in range(NCHUNK):
                    rb = pSA[:, 512 * (ridx % 2):512 * (ridx % 2 + 1)]
                    ep = ckd[(tc, c)]
                    wt("t", c_act, ep)
                    if ridx >= 2:
                        wt("t", c_dve, ck_t2[ridx - 2])
                    mm(rb, rm_sb[:, :], pl[:, TT * c:TT * (c + 1)])
                    rot_ck = inc("t", c_pe)
                    tslot = tidx % 2
                    wt("g", c_act, ep)
                    if tidx >= 2:
                        wt("g", c_dve, ck_addidx[tidx - 2])
                    em("g", lambda e, p=pl, c_=c, s=tslot, t=tc: e.tensor_tensor(
                        t1_sb[:, TT * s:TT * (s + 1)],
                        p[:, TT * c_:TT * (c_ + 1)],
                        m1[:, TT * t:TT * (t + 1)], ALU.mult))
                    ck_t1[tidx] = inc("g", c_pool)
                    wt("v", c_pe, rot_ck)
                    em("v", lambda e, b=rb, s=tslot, t=tc: e.tensor_tensor(
                        t2_sb[:, TT * s:TT * (s + 1)], b,
                        m2[:, TT * t:TT * (t + 1)], ALU.mult))
                    ck_t2[ridx] = inc("v", c_dve)
                    wt("v", c_pool, ck_t1[tidx])
                    wt("v", d_scw, scw_gate)
                    em("v", lambda e, t_=tgt, c_=c, s=tslot: e.tensor_add(
                        t_[:, TT * c_:TT * (c_ + 1)],
                        t1_sb[:, TT * s:TT * (s + 1)],
                        t2_sb[:, TT * s:TT * (s + 1)]))
                    ck_rope[(which, tc, c)] = inc("v", c_dve)
                    ck_addidx[tidx] = ck_rope[(which, tc, c)]
                    ridx += 1
                    tidx += 1
                    sslot = c % 2
                    wt("c", c_act, ep)
                    key2 = (which, c - 2) if c >= 2 else (
                        ("q", NCHUNK - 2 + c) if which == "k" else None)
                    if key2 is not None and (key2[1] >= 0):
                        prev = hs_byc.get(key2)
                        if prev:
                            wt("c", c_pe, prev)
                    em("c", lambda e, p=pl, c_=c, s=sslot: e.activation(
                        sqA[:, TT * s:TT * (s + 1)],
                        p[:, TT * c_:TT * (c_ + 1)], AF.Square))
                    ck_sq[(which, tc, c)] = inc("c", c_act)
                    wt("t", c_act, ck_sq[(which, tc, c)])
                    if c == 0:
                        wt("t", c_dve, ck_n2done_prev)
                        if tc == 0 and which == "q":
                            wt("t", c_dve, sm7x)
                    mm(n2bank[0:16, 0:512], hs_sb[:, 16 * c:16 * c + 16],
                       sqA[:, TT * sslot:TT * (sslot + 1)],
                       start=(c == 0), stop=(c == NCHUNK - 1))
                    hs_ck = inc("t", c_pe)
                    hs_byc[(which, c)] = hs_ck
                if which == "q":
                    ck_hsq_last = hs_ck
                else:
                    ck_hsk_last = hs_ck
                wt("v", c_pe, hs_ck)
                if tc == 0 and which == "q":
                    wt("v", c_pool, memset_done)
                tgt_max = qmax2_sb if which == "q" else kmax2_sb
                em("v", lambda e: e.reduce_max(
                    n2t_sb[0:16, 0:1], pSB[0:16, 0:512], AX.X))
                vsp()
                em("v", lambda e, t_=tgt_max: e.tensor_max(
                    t_[:, :], t_[:, :], n2t_sb[0:16, 0:1]))
                ck_n2done_prev = vinc_safe()
            ck_n2done[tc] = ck_n2done_prev

            for j in range(4):
                for c in range(NCHUNK):
                    wt("t", c_dve, ck_vsb[(tc, c)])
                    if c == 0:
                        wt("t", c_dve, lastpTR_read)
                        if tc == 0 and j == 0:
                            wt("t", c_pool, memset_done)
                    tr(TRB[:, 128 * c:128 * (c + 1)],
                       m9[:, TT * c + 128 * j:TT * c + 128 * (j + 1)])
                vtr_ck = inc("t", c_pe)
                for c in range(NCHUNK):
                    ck_vtr_last[c] = vtr_ck
                vslot = j % 2
                wt("v", c_pe, vtr_ck)
                wt("v", d_scw, rec_vp[vslot])
                em("v", lambda e, s=vslot: e.tensor_copy(
                    bass.AP(VpT, 1040 * s, [[2080, 128], [130, 8], [65, 2], [1, 64]]),
                    TRB.rearrange("p (c d i) -> p c d i", d=2, i=64)))
                vev = inc("v", c_dve)
                lastpTR_read = vev
                wt("y", c_dve, vev)
                em("y", lambda e, s=vslot, t=tc, j_=j: e.dma_start(
                    out=VS[4 * t + j_, :, :],
                    in_=VpT[:, 1040 * s:1040 * (s + 1)]))
                rec_vp[vslot] = inc("y", d_scw, 16)

            wt("y", c_dve, ck_rope[("q", tc, NCHUNK - 1)])
            em("y", lambda e, t=tc: e.dma_start(out=QS[t, :, :], in_=m7[:, :]))
            rec_qtb = inc("y", d_scw, 16)
            wt("y", c_dve, ck_rope[("k", tc, NCHUNK - 1)])
            em("y", lambda e, t=tc: e.dma_start(
                out=KS[t].rearrange("c p i -> p c i"),
                in_=m8[:, :].rearrange("p (c i) -> p c i", i=512)))
            rec_ktb = inc("y", d_scw, 16)

        wt("c", c_dve, ck_n2done[NT - 1])
        em("c", lambda e: e.activation(qmx_sb[:, :], qmax2_sb[:, :], AF.Sqrt))
        em("c", lambda e: e.activation(kmx_sb[:, :], kmax2_sb[:, :], AF.Sqrt))
        cbm = ainc_safe()
        wt("v", c_act, cbm)
        em("v", lambda e: e.tensor_mul(ch_sb[:, :], qmx_sb[:, :], kmx_sb[:, :]))
        cb2 = vinc_safe()
        wt("y", c_dve, cb2)
        em("y", lambda e: e.dma_start(
            out=CHD[:, :].rearrange("a b -> b a"), in_=ch_sb[:, :]))
        cb3 = inc("y", d_scw, 16)
        wt("y", d_scw, cb3)
        em("y", lambda e: e.dma_start(out=chrow_sb[:, :], in_=CHD[:, :]))
        cb4 = inc("y", d_ld, 16)
        wt("v", d_ld, cb4)
        em("v", lambda e: e.tensor_tensor(
            chp_sb[:, :],
            bass.AP(chrow_sb, 0, [[16, 1], [2, 8]]),
            bass.AP(chrow_sb, 1, [[16, 1], [2, 8]]), ALU.max))
        cbp1 = vinc_safe()
        wt("t", c_dve, cbp1)
        mm(pSB[0:128, 0:8], onesf_sb[:, :], chp_sb[:, :])
        cb5 = inc("t", c_pe)
        wt("v", c_pe, cb5)
        em("v", lambda e: e.tensor_scalar_mul(CBP_sb[:, :], pSB[0:128, 0:8], -1.0))
        ck_cb = vinc_safe()

        for q in "tvcgy":
            em(q, lambda e: e.nop())
            last = prog[q][-1]

            def fn(e, last=last):
                last(e).then_inc(c_ph.h, 1)
            prog[q][-1] = fn
        c_ph.tick(5)
        for q in "tvcgy":
            em(q, lambda e: e.wait_ge(c_ph.h, 5))

        em("g", lambda e: e.dma_start(
            out=w2[:, :].rearrange("p (c o) -> p c o", o=1024),
            in_=Wh[:, :].rearrange("(c p) o -> p c o", p=128)))
        inc("g", d_ld, 16)
        em("g", lambda e: e.dma_start(
            out=w3[:, :].rearrange("p (c o) -> p c o", o=1024),
            in_=woh[:, :].rearrange("(c p) o -> p c o", p=128)))
        ld_b = inc("g", d_ld, 16)

        def PT4(hi, par):
            return m5[:, 1024 * (2 * par + hi // 2) + 512 * (hi % 2):
                      1024 * (2 * par + hi // 2) + 512 * (hi % 2) + 512]

        rrv = [bass.AP(m7, 1024 + 512 * s, [[4096, 1], [1, 512]]) for s in range(2)]
        sqs_v = m7[:, 2048:2560]
        sqp_v = m7[:, 2560:2816]
        hT_off = 3072

        ck_pv = {}
        ck_mrg_last = {hi: 0 for hi in range(4)}
        ck_pre1 = {}
        ck_pre1_slot = {0: 0, 1: 0}
        ck_pre2_slot = {0: 0, 1: 0}
        ck_pre2_last = 0
        ck_out = 0
        ck_x1bf_rd = 0
        ck_base_last = 0
        ck_rbc_last = 0
        ck_hbf_last = 0
        ck_dve_wp_rd = 0
        ck_dve_lg_rd = 0
        ck_lw_last = 0
        ck_lnp78_rd = 0
        lnp_guard = {j: 0 for j in range(4)}
        pre2_cks = {}
        ck_wo_last = 0
        lastV_ck = [0, 0]
        kslotQK = [0, 0]
        pt_last_par = {0: {}, 1: {}}
        rr_last_bc = [0, 0]
        kg_ck = {}
        kg_ckA = {}
        vw_ck = {}
        x_ck = {}
        spidx = 0
        exp_sp_cks = {}
        qk_cks = {}
        ck_state = {"last_rec": 0, "hbf_for_bc": 0}
        ck_exp_last = 0
        prev_qk_m1 = 0

        def kg_views(g):
            if g % 2 == 0:
                return (w1, 0, 8192, w1, 4096, 8192)
            return (m8, 0, 4096, m9, 0, 4096)

        def emit_kg_load(qc, g):
            tA, bA, rA, tB, bB, rB = kg_views(g)
            wt("y", c_pe, kslotQK[g % 2])
            for (tt, bb, c) in ((tA, bA, 2 * g), (tB, bB, 2 * g + 1)):
                em("y", lambda e, t_=tt, b_=bb, c_=c, q=qc: e.dma_start(
                    out=t_[:, b_:b_ + (q + 1) * 512].rearrange(
                        "p (tc i) -> p tc i", i=512),
                    in_=KS[0:q + 1, c_].rearrange("tc p i -> p tc i")))
                inc("y", d_k, 16)
                if c == 2 * g:
                    kg_ckA[(qc, g)] = d_k.n
            kg_ck[(qc, g)] = d_k.n

        def emit_vw_load(qc, g, w):
            vs = w % 2
            wt("y", c_pe, lastV_ck[vs])
            em("y", lambda e, w_=w, g_=g, s=vs: e.dma_start(
                out=m6[:, 1040 * s:1040 * (s + 1)].rearrange(
                    "p (kc i) -> p kc i", i=260),
                in_=VS[4 * w_:4 * (w_ + 1), :, 260 * g_:260 * (g_ + 1)]
                .rearrange("kc p i -> p kc i")))
            vw_ck[(g, w)] = inc("y", d_v, 16)

        def emit_x_load(qc, idx):
            j, oh2 = idx // 2, idx % 2
            s = idx % 2
            wt("y", c_dve, ck_pre1_slot[s])
            em("y", lambda e, q=qc, j_=j, o_=oh2, s_=s: e.dma_start(
                out=xs_sb[:, 512 * s_:512 * (s_ + 1)],
                in_=xh[q * 512 + 128 * j_:q * 512 + 128 * (j_ + 1),
                       512 * o_:512 * (o_ + 1)]))
            x_ck[(qc, idx)] = inc("y", d_x, 16)

        qt_cks = {}
        glast_qk = {g: 0 for g in range(NG)}

        def emit_qt_slice(qc, g):
            wt("y", c_pe, glast_qk[g])
            em("y", lambda e, q=qc, g_=g: e.dma_start(
                out=m1[:, 1024 * g_:1024 * (g_ + 1)],
                in_=QS[q, :, 1024 * g_:1024 * (g_ + 1)]))
            qt_cks[(qc, g)] = inc("y", d_q, 16)

        def emit_qc_head(qc):
            emit_kg_load(qc, 0)
            emit_kg_load(qc, 1)

        for qc in range(NT):
            nkc = 4 * (qc + 1)
            nwin = qc + 1
            if qc == 0:
                emit_qc_head(0)
                for g_ in range(NG):
                    emit_qt_slice(0, g_)
            first_qk_of_qc = True
            emit_x_load(qc, 0)
            emit_x_load(qc, 1)

            pending_epi = []
            pending_tailpv = None

            def make_epi_chunks(gp, pvlast, dep_wo):
                st_ = {}

                def chunkA():
                    wt("v", c_pe, max(pvlast.values()))
                    if not ck_state["last_rec"]:
                        wt("v", c_pool, memset_done)
                    em("v", lambda e: e.tensor_copy(dstf[0:1, :],
                                                    pF[0][64:65, :]))
                    em("v", lambda e: e.tensor_copy(dstf[32:33, :],
                                                    pF[1][64:65, :]))
                    em("v", lambda e: e.tensor_copy(dstf[64:65, :],
                                                    pF[2][64:65, :]))
                    em("v", lambda e: e.tensor_copy(dstf[96:97, :],
                                                    pF[3][64:65, :]))
                    st_["evd"] = vinc_safe()
                    vsp()
                    wt("v", c_pe, max(rr_last_bc))
                    em("v", lambda e: _recip_bf(nc, e, rrv4b[0:97, 0:256],
                                                dstf[0:97, 0:256]))
                    em("v", lambda e: e.tensor_copy(
                        dstf[0:1, 0:1], rrv4b[96:97, 255:256]))
                    vsp()
                    vsp()
                    st_["recL"] = vinc_safe()
                    em("v", lambda e: _recip_bf(nc, e, rrv4b[0:97, 256:512],
                                                dstf[0:97, 256:512]))
                    em("v", lambda e: e.tensor_copy(
                        dstf[0:1, 0:1], rrv4b[96:97, 511:512]))
                    vsp()
                    vsp()
                    st_["rec"] = vinc_safe()
                    ck_state["last_rec"] = st_["rec"]

                def chunkB():
                    wt("t", c_dve, st_["evd"])
                    for w_ in range(8):
                        mm(pF[w_ % 4][64:128, :],
                           sel_sb[0:97, 64 * (w_ % 4):64 * (w_ % 4) + 64],
                           rrv4b[0:97, :])
                        inc("t", c_pe)
                    wt("t", c_dve, st_["recL"])
                    bcs = []
                    for hi in range(4):
                        mm(pF[hi][64:128, 0:256],
                           sel_sb[0:97, 64 * hi:64 * hi + 64],
                           rrv4b[0:97, 0:256])
                        bcs.append(inc("t", c_pe))
                    wt("t", c_dve, st_["rec"])
                    for hi in range(4):
                        mm(pF[hi][64:128, 256:512],
                           sel_sb[0:97, 64 * hi:64 * hi + 64],
                           rrv4b[0:97, 256:512])
                        bcs.append(inc("t", c_pe))
                    st_["bc"] = [bcs[4], bcs[5], bcs[6], bcs[7]]
                    rr_last_bc[0] = bcs[7]
                    rr_last_bc[1] = bcs[7]

                def chunkC():
                    wt("v", c_pe, st_["bc"][1])
                    em("v", lambda e: e.tensor_copy(rbc_sb[:, 0:512],
                                                    pF[0][64:128, :]))
                    em("v", lambda e: e.tensor_copy(rbc_sb[:, 512:1024],
                                                    pF[1][64:128, :]))
                    vsp()
                    if dep_wo:
                        wt("v", c_pe, dep_wo)
                    for hi in (0, 1):
                        h = 4 * gp + hi
                        em("v", lambda e, p=hi, c_=h // 2, pr=h % 2: e.tensor_mul(
                            bass.AP(m2, 64 * pr * 4096 + 512 * c_,
                                    [[4096, 64], [1, 512]]),
                            pF[p][0:64, :], rbc_sb[:, 512 * p:512 * (p + 1)]))
                        ck_mrg_last[hi] = inc("v", c_dve)

                def chunkD():
                    wt("v", c_pe, st_["bc"][3])
                    em("v", lambda e: e.tensor_copy(rbc_sb[:, 0:512],
                                                    pF[2][64:128, :]))
                    em("v", lambda e: e.tensor_copy(rbc_sb[:, 512:1024],
                                                    pF[3][64:128, :]))
                    vsp()
                    for hi in (2, 3):
                        h = 4 * gp + hi
                        em("v", lambda e, p=hi, c_=h // 2, pr=h % 2: e.tensor_mul(
                            bass.AP(m2, 64 * pr * 4096 + 512 * c_,
                                    [[4096, 64], [1, 512]]),
                            pF[p][0:64, :],
                            rbc_sb[:, 512 * (p - 2):512 * (p - 1)]))
                        ck_mrg_last[hi] = inc("v", c_dve)

                return [("A", chunkA), ("B", chunkB),
                        ("C", chunkC), ("D", chunkD)]

            for g in range(NG):
                tA, bA, rA, tB, bB, rB = kg_views(g)
                first_qk_of_group = True
                exp_pair = {}
                msk_par = {}
                pair_sp = {}
                if pending_tailpv:
                    tail_to_emit = pending_tailpv
                    pending_tailpv = None
                else:
                    tail_to_emit = None
                for kc in range(nkc):
                    par = kc % 2
                    diag_k = (kc // 4 == qc)
                    for p in range(2):
                        need = spidx - 2
                        if need >= 0 and need in exp_sp_cks:
                            wt("t", c_act, exp_sp_cks[need])
                        if first_qk_of_group:
                            wt("t", d_k, kg_ckA[(qc, g)])
                            wt("t", d_q, qt_cks[(qc, g)])
                            first_qk_of_group = False
                        if kc == 0 and p == 1:
                            wt("t", d_k, kg_ck[(qc, g)])
                        sb_pair = pS2[spidx % 2]
                        if first_qk_of_qc:
                            wt("t", c_dve, lastpTR_read)
                            first_qk_of_qc = False
                        for sub in range(2):
                            hi = 2 * p + sub
                            h = 4 * g + hi
                            c = h // 2
                            hp = h % 2
                            if c == 2 * g:
                                kt, kb, krl = tA, bA, rA
                            else:
                                kt, kb, krl = tB, bB, rB
                            lhsT = bass.AP(kt, 64 * hp * krl + kb + 128 * kc,
                                           [[krl, 64], [1, 128]])
                            rhs = bass.AP(m1, 64 * hp * 4096 + 512 * c,
                                          [[4096, 64], [1, 512]])
                            mm(sb_pair[:, 512 * sub:512 * (sub + 1)], lhsT, rhs)
                            qk = inc("t", c_pe)
                            if kc == nkc - 1 and hi == 3:
                                kslotQK[g % 2] = qk
                                prev_qk_m1 = qk
                                glast_qk[g] = qk
                        pair_sp[(kc, p)] = spidx
                        qk_cks[spidx] = qk
                        spidx += 1
                    if kc == 0 and tail_to_emit:
                        tail_to_emit()
                        tail_to_emit = None
                    if kc == 0:
                        emit_vw_load(qc, g, 0)
                    elif kc == 1 and nwin >= 2:
                        emit_vw_load(qc, g, 1)
                    if kc == 2:
                        while pending_epi:
                            pending_epi.pop(0)[1]()
                    kcms = []
                    if kc == 2:
                        kcms = [0, 1]
                    elif kc > 2:
                        kcms = [kc - 1]
                    for kcm in kcms:
                        mpar = kcm % 2
                        w = kcm // 4
                        if (kcm // 4) == qc:
                            wt("t", c_dve, msk_par[mpar])
                        else:
                            wt("t", c_act, max(exp_pair[(mpar, 0)],
                                               exp_pair[(mpar, 1)]))
                        if kcm == 0:
                            wt("t", d_v, vw_ck[(g, 0)])
                            if ck_lw_last:
                                wt("t", c_act, ck_lw_last)
                        elif kcm % 4 == 0:
                            wt("t", d_v, vw_ck[(g, w)])
                        for hi in range(4):
                            if kcm == 0:
                                wt("t", c_dve, ck_mrg_last[hi])
                                if qc > 0 and g == 0:
                                    if hi == 0 and ck_pre2_slot[0]:
                                        wt("t", c_dve, ck_pre2_slot[0])
                                    if hi == 1 and ck_pre2_slot[1]:
                                        wt("t", c_dve, ck_pre2_slot[1])
                                    if hi == 2 and ck_hbf_last:
                                        wt("t", c_dve, ck_hbf_last)
                                    if hi == 3 and ck_dve_lg_rd:
                                        wt("t", c_dve, ck_dve_lg_rd)
                            vs = w % 2
                            lhsT = bass.AP(
                                m6, 1040 * vs + 260 * (kcm % 4) + 65 * hi,
                                [[4096, 128], [1, 65]])
                            mm(pF[hi][0:65, :], lhsT, PT4(hi, mpar),
                               start=(kcm == 0), stop=False)
                            pv = inc("t", c_pe)
                            ck_pv[(hi, kcm)] = pv
                            pt_last_par[mpar][hi // 2] = pv
                            if hi == 3:
                                lastV_ck[vs] = pv
                                if kcm % 4 == 3 and w + 2 < nwin:
                                    emit_vw_load(qc, g, w + 2)
                    for p in range(2):
                        sp = pair_sp[(kc, p)]
                        sb_pair = pS2[sp % 2]
                        wt("c", c_pe, qk_cks[sp])
                        if kc >= 2:
                            wt("c", c_pe, max(ck_pv[(2 * p, kc - 2)],
                                              ck_pv[(2 * p + 1, kc - 2)]))
                        else:
                            wt("c", c_pe, pt_last_par[par].get(p, 0))
                            if qc == 0 and g == 0 and p == 0 and kc == 0:
                                wt("c", c_dve, ck_cb)
                        em("c", lambda e, b=sb_pair, g_=g, p_=p, pa=par:
                           e.activation(
                               m5[:, 1024 * (2 * pa + p_):
                                  1024 * (2 * pa + p_) + 1024],
                               b[:, :], AF.Exp,
                               bias=CBP_sb[:, 2 * g_ + p_:2 * g_ + p_ + 1]))
                        ec = inc("c", c_act)
                        exp_sp_cks[sp] = ec
                        exp_pair[(par, p)] = ec
                        ck_exp_last = ec
                        if diag_k:
                            r = kc - 4 * qc
                            wt("v", c_act, ec)
                            for sub in range(2):
                                hi = 2 * p + sub
                                em("v", lambda e, p2=hi, pa=par, r_=r:
                                   e.tensor_mul(
                                       PT4(p2, pa), PT4(p2, pa),
                                       msk_sb[:, 384 - 128 * r_:
                                              384 - 128 * r_ + 512]))
                            msk_par[par] = vinc_safe()
                    if kc == 0 and pending_epi and pending_epi[0][0] == "A":
                        pending_epi.pop(0)[1]()

                if g + 2 < NG:
                    emit_kg_load(qc, g + 2)
                if qc + 1 < NT:
                    emit_qt_slice(qc + 1, g)

                assert not pending_epi and pending_tailpv is None

                def make_tail(gq, gg, nk, expd, mskd, wock):
                    def tail():
                        pvl = {}
                        for kcm in (nk - 1,):
                            mpar = kcm % 2
                            w = kcm // 4
                            wt("t", c_dve, mskd[mpar])
                            for hi in range(4):
                                vs = w % 2
                                lhsT = bass.AP(
                                    m6, 1040 * vs + 260 * (kcm % 4) + 65 * hi,
                                    [[4096, 128], [1, 65]])
                                mm(pF[hi][0:65, :], lhsT, PT4(hi, mpar),
                                   start=False, stop=(kcm == nk - 1))
                                pv = inc("t", c_pe)
                                ck_pv[(hi, kcm)] = pv
                                pt_last_par[mpar][hi // 2] = pv
                                pvl[hi] = pv
                                if hi == 3:
                                    lastV_ck[vs] = pv
                        chunks = make_epi_chunks(
                            gg, pvl, wock if gg == 0 else 0)
                        pending_epi.extend(chunks)
                    return tail

                pending_tailpv = make_tail(qc, g, nkc, dict(exp_pair),
                                           dict(msk_par), ck_wo_last)

            if pending_tailpv:
                pending_tailpv()
                pending_tailpv = None
            while pending_epi:
                pending_epi.pop(0)[1]()
            if qc + 1 < NT:
                emit_qc_head(qc + 1)

            mrg_all = max(ck_mrg_last.values())
            for idx in range(8):
                j, oh2 = idx // 2, idx % 2
                bank = pF[2 + idx % 2]
                if idx == 0:
                    wt("t", c_dve, mrg_all)
                    if qc == 0:
                        wt("t", d_ld, ld_b)
                if idx == 1 and ck_lw_last:
                    wt("t", c_act, ck_lw_last)
                if idx >= 2:
                    wt("t", c_dve, ck_pre1_slot[idx % 2])
                for c in range(NCHUNK):
                    mm(bank[:, :],
                       m2[:, 512 * c + 128 * j:512 * c + 128 * (j + 1)],
                       w3[:, 1024 * c + 512 * oh2:1024 * c + 512 * (oh2 + 1)],
                       start=(c == 0), stop=(c == NCHUNK - 1))
                wo_ck = inc("t", c_pe)
                if idx == 7:
                    ck_wo_last = wo_ck
                wt("v", c_pe, wo_ck)
                wt("v", d_x, x_ck[(qc, idx)])
                em("v", lambda e, b=bank, j_=j, o_=oh2, s=idx % 2:
                   e.scalar_tensor_tensor(
                       B1[:, 1024 * j_ + 512 * o_:1024 * j_ + 512 * (o_ + 1)],
                       b[:, :], 1.0, xs_sb[:, 512 * s:512 * (s + 1)],
                       op0=ALU.bypass, op1=ALU.add,
                       accum_out=acc_sb[:, 2 * j_ + o_:2 * j_ + o_ + 1]))
                p1 = inc("v", c_dve)
                ck_pre1_slot[idx % 2] = p1
                ck_pre1[(qc, j, oh2)] = p1
                if idx + 2 < 8:
                    emit_x_load(qc, idx + 2)

            x1T_cks = {}
            sq_cks = {}
            var_cks = {}
            sd_cks = {}
            for j in range(4):
                L = 16 * j
                wt("v", c_dve, ck_pre1[(qc, j, 1)])
                em("v", lambda e, j_=j, L_=L: e.tensor_add(
                    lnp[:, L_:L_ + 1], acc_sb[:, 2 * j_:2 * j_ + 1],
                    acc_sb[:, 2 * j_ + 1:2 * j_ + 2]))
                vsp()
                em("v", lambda e, L_=L: e.tensor_scalar_mul(
                    lnp[:, L_ + 1:L_ + 2], lnp[:, L_:L_ + 1], 1.0 / D))
                inc("v", c_dve)
                wt("c", c_dve, ck_pre1[(qc, j, 1)])
                if lnp_guard[j]:
                    wt("c", c_dve, lnp_guard[j])
                em("c", lambda e, j_=j, L_=L: e.activation(
                    sqs_v, B1[:, 1024 * j_:1024 * j_ + 512], AF.Square,
                    accum_out=lnp[:, L_ + 7:L_ + 8]))
                em("c", lambda e, j_=j, L_=L: e.activation(
                    sqs_v, B1[:, 1024 * j_ + 512:1024 * (j_ + 1)], AF.Square,
                    accum_out=lnp[:, L_ + 8:L_ + 9]))
                sq_cks[j] = ainc_safe()
            for j in range(4):
                L = 16 * j
                wt("v", c_act, sq_cks[j])
                vsp()
                em("v", lambda e, L_=L: e.tensor_scalar(
                    lnp[:, L_ + 2:L_ + 3], lnp[:, L_ + 1:L_ + 2],
                    lnp[:, L_ + 1:L_ + 2], EPS,
                    op0=ALU.mult, op1=ALU.subtract))
                em("v", lambda e, L_=L: e.tensor_add(
                    lnp[:, L_ + 9:L_ + 10], lnp[:, L_ + 7:L_ + 8],
                    lnp[:, L_ + 8:L_ + 9]))
                vsp()
                em("v", lambda e, L_=L: e.tensor_scalar(
                    lnp[:, L_ + 3:L_ + 4], lnp[:, L_ + 9:L_ + 10], 1.0 / D,
                    lnp[:, L_ + 2:L_ + 3],
                    op0=ALU.mult, op1=ALU.subtract))
                var_cks[j] = vinc_safe()
                lnp_guard[j] = var_cks[j]
            for j in range(4):
                L = 16 * j
                wt("c", c_dve, var_cks[j])
                em("c", lambda e, L_=L: e.activation(
                    lnp[:, L_ + 4:L_ + 5], lnp[:, L_ + 3:L_ + 4], AF.Sqrt))
                sd_cks[j] = ainc_safe()
            for j in range(4):
                L = 16 * j
                wt("v", c_act, sd_cks[j])
                em("v", lambda e, L_=L: e.reciprocal(
                    lnp[:, L_ + 5:L_ + 6], lnp[:, L_ + 4:L_ + 5]))
                vsp()
                em("v", lambda e, L_=L: e.scalar_tensor_tensor(
                    lnp[:, L_ + 6:L_ + 7], lnp[:, L_ + 1:L_ + 2], -1.0,
                    lnp[:, L_ + 5:L_ + 6],
                    op0=ALU.mult, op1=ALU.mult))
                vsp()
                if j == 0 and ck_out:
                    wt("v", d_out, ck_out)
                em("v", lambda e, j_=j, L_=L: e.tensor_scalar(
                    B2[:, 1024 * j_:1024 * (j_ + 1)],
                    B1[:, 1024 * j_:1024 * (j_ + 1)],
                    lnp[:, L_ + 5:L_ + 6], lnp[:, L_ + 6:L_ + 7],
                    op0=ALU.mult, op1=ALU.add))
                vsp()
                x1_ck = inc("v", c_dve)
                if ck_x1bf_rd:
                    wt("v", c_pe, ck_x1bf_rd)
                em("v", lambda e, j_=j: e.tensor_copy(
                    m4[:, 1024 * j_:1024 * (j_ + 1)],
                    B2[:, 1024 * j_:1024 * (j_ + 1)]))
                x1bf_ck = inc("v", c_dve)
                wt("t", c_dve, x1bf_ck)
                wt("t", c_dve, lastpTR_read)
                if j == 0:
                    wt("t", c_act, ck_exp_last)
                for c in range(NCHUNK):
                    tr(TRA[:, 128 * c:128 * (c + 1)],
                       m4[:, 1024 * j + 128 * c:1024 * j + 128 * (c + 1)])
                x1tr_ck = inc("t", c_pe)
                if j == 3:
                    ck_x1bf_rd = x1tr_ck
                wt("v", c_pe, x1tr_ck)
                if ck_base_last:
                    wt("v", c_pe, ck_base_last)
                em("v", lambda e, j_=j: e.tensor_copy(
                    bass.AP(m3, 128 * j_, [[4096, 128], [512, 8], [1, 128]]),
                    TRA.rearrange("p (c i) -> p c i", i=128)))
                x1T_cks[j] = inc("v", c_dve)
                lastpTR_read = x1T_cks[j]

            pT_cks = {}
            for j in range(4):
                wt("t", c_dve, x1T_cks[j])
                if j == 0:
                    wt("t", c_dve, ck_pre1_slot[0])
                    if ck_pre2_slot[0]:
                        wt("t", c_dve, ck_pre2_slot[0])
                    if ck_hbf_last:
                        wt("t", c_dve, ck_hbf_last)
                if ck_dve_wp_rd:
                    wt("t", c_dve, ck_dve_wp_rd)
                for c in range(NCHUNK):
                    mm(pF[2][0:128, 0:256],
                       bass.AP(m3, 512 * c + 128 * j, [[4096, 128], [1, 128]]),
                       WpT_sb[:, 256 * c:256 * (c + 1)],
                       start=(c == 0), stop=(c == NCHUNK - 1))
                wp_ck = inc("t", c_pe)
                wt("v", c_pe, wp_ck)
                em("v", lambda e, j_=j: e.tensor_add(
                    projf[:, 256 * j_:256 * (j_ + 1)],
                    pF[2][0:128, 0:256], bpb_sb[:, :]))
                projadd_ck = inc("v", c_dve)
                ck_dve_wp_rd = projadd_ck
                wt("c", c_dve, projadd_ck)
                em("c", lambda e, j_=j: e.activation(
                    sqp_v, projf[:, 256 * j_:256 * (j_ + 1)], AF.Square,
                    accum_out=rtp[:, 0:1]))
                asp()
                asp()
                em("c", lambda e: e.activation(rtp[:, 1:2], rtp[:, 0:1], AF.Sqrt))
                pr_ck = ainc_safe()
                wt("v", c_act, pr_ck)
                em("v", lambda e: e.tensor_scalar_max(rtp[:, 1:2], rtp[:, 1:2], 1e-12))
                vsp()
                em("v", lambda e: e.reciprocal(rtp[:, 2:3], rtp[:, 1:2]))
                vsp()
                em("v", lambda e, j_=j: e.tensor_scalar_mul(
                    m6[:, 2304 + 256 * j_:2304 + 256 * (j_ + 1)],
                    projf[:, 256 * j_:256 * (j_ + 1)], rtp[:, 2:3]))
                pbf_ck = inc("v", c_dve)
                wt("t", c_dve, pbf_ck)
                wt("t", c_dve, lastpTR_read)
                for pc in range(2):
                    tr(TRA[:, 128 * pc:128 * (pc + 1)],
                       m6[:, 2304 + 256 * j + 128 * pc:
                          2304 + 256 * j + 128 * (pc + 1)])
                ptr_ck = inc("t", c_pe)
                wt("v", c_pe, ptr_ck)
                em("v", lambda e, j_=j: e.tensor_copy(
                    bass.AP(m7, 128 * j_, [[4096, 128], [512, 2], [1, 128]]),
                    TRA.rearrange("p (c i) -> p c i", i=128)[:, 0:2, :]))
                pT_cks[j] = inc("v", c_dve)
                lastpTR_read = pT_cks[j]

            g4_cks = {}
            for j in range(4):
                wt("t", c_dve, pT_cks[j])
                if j == 0:
                    wt("t", c_dve, ck_pre1_slot[1])
                    if ck_pre2_slot[1]:
                        wt("t", c_dve, ck_pre2_slot[1])
                if ck_dve_lg_rd:
                    wt("t", c_dve, ck_dve_lg_rd)
                if ck_lw_last:
                    wt("t", c_act, ck_lw_last)
                for pc in range(2):
                    mm(pF[3][0:128, 0:4],
                       bass.AP(m7, 512 * pc + 128 * j, [[4096, 128], [1, 128]]),
                       simn_sb[:, 4 * pc:4 * (pc + 1)],
                       start=(pc == 0), stop=(pc == 1))
                lg_ck = inc("t", c_pe)
                wt("v", c_pe, lg_ck)
                em("v", lambda e: e.reduce_max(
                    rtp[:, 3:4], pF[3][0:128, 0:4], AX.X, negate=True))
                nm_ck = vinc_safe()
                ck_dve_lg_rd = nm_ck
                wt("c", c_dve, nm_ck)
                em("c", lambda e: e.activation(
                    lw_sb[:, :], pF[3][0:128, 0:4], AF.Exp, bias=rtp[:, 3:4]))
                lw_ck = ainc_safe()
                ck_lw_last = lw_ck
                wt("v", c_act, lw_ck)
                em("v", lambda e: e.reduce_sum(rtp[:, 4:5], lw_sb[:, :], AX.X))
                em("v", lambda e: e.reduce_max(rtp[:, 6:7], lw_sb[:, :], AX.X))
                em("v", lambda e, j_=j: e.reciprocal(
                    rtp[:, 8 + j_:9 + j_], rtp[:, 4:5]))
                em("v", lambda e, j_=j: e.scalar_tensor_tensor(
                    g4_sb[:, 4 * j_:4 * (j_ + 1)], lw_sb[:, :], rtp[:, 6:7],
                    lw_sb[:, :], op0=ALU.is_ge, op1=ALU.mult))
                vsp()
                g4_cks[j] = inc("v", c_dve)

            hT_cks = {}
            for j in range(4):
                if ck_hbf_last:
                    wt("t", c_dve, ck_hbf_last)
                wt("t", c_dve, ck_dve_wp_rd)
                for c in range(NCHUNK):
                    mm(pF[2][0:128, 0:32],
                       bass.AP(m3, 512 * c + 128 * j, [[4096, 128], [1, 128]]),
                       WdT_sb[:, 32 * c:32 * (c + 1)],
                       start=(c == 0), stop=(c == NCHUNK - 1))
                dn_ck = inc("t", c_pe)
                wt("v", c_pe, dn_ck)
                wt("v", c_dve, g4_cks[j])
                for e_i in range(4):
                    em("v", lambda e, j_=j, ei=e_i: e.tensor_scalar(
                        m6[:, 3328 + 32 * j_ + 8 * ei:
                           3328 + 32 * j_ + 8 * (ei + 1)],
                        pF[2][0:128, 8 * ei:8 * (ei + 1)],
                        g4_sb[:, 4 * j_ + ei:4 * j_ + ei + 1],
                        rtp[:, 8 + j_:9 + j_],
                        op0=ALU.mult, op1=ALU.mult))
                hbf_ck = vinc_safe()
                ck_hbf_last = hbf_ck
                ck_state["hbf_for_bc"] = hbf_ck
                wt("t", c_dve, hbf_ck)
                wt("t", c_dve, lastpTR_read)
                em("t", lambda e, j_=j: e.transpose(
                    TRA[0:32, 0:128],
                    m6[:, 3328 + 32 * j_:3328 + 32 * (j_ + 1)],
                    id_sb[:, :]))
                htr_ck = inc("t", c_pe)
                wt("v", c_pe, htr_ck)
                em("v", lambda e, j_=j: e.tensor_copy(
                    bass.AP(m7, hT_off + 128 * j_, [[4096, 32], [1, 128]]),
                    TRA[0:32, 0:128]))
                hT_cks[j] = vinc_safe()
                lastpTR_read = hT_cks[j]

            for j in range(4):
                for oh2 in range(2):
                    bank = pF[0 + oh2]
                    wt("t", c_dve, hT_cks[j])
                    if j == 0:
                        wt("t", c_dve, max(ck_mrg_last.values()))
                        if oh2 == 0:
                            wt("t", c_dve, ck_dve_wp_rd)
                        else:
                            wt("t", c_dve, ck_dve_lg_rd)
                            wt("t", c_act, ck_lw_last)
                    if ck_pre2_slot[oh2]:
                        wt("t", c_dve, ck_pre2_slot[oh2])
                    for c in range(NCHUNK):
                        mm(bank[:, :],
                           bass.AP(m3, 512 * c + 128 * j, [[4096, 128], [1, 128]]),
                           w2[:, 1024 * c + 512 * oh2:1024 * c + 512 * (oh2 + 1)],
                           start=(c == 0), stop=False)
                    mm(bank[:, :],
                       bass.AP(m7, hT_off + 128 * j, [[4096, 32], [1, 128]]),
                       WuT_sb[0:32, 512 * oh2:512 * (oh2 + 1)],
                       start=False, stop=True)
                    bs_ck = inc("t", c_pe)
                    ck_base_last = bs_ck
                    wt("v", c_pe, bs_ck)
                    em("v", lambda e, b=bank, j_=j, o_=oh2: e.scalar_tensor_tensor(
                        B1[:, 1024 * j_ + 512 * o_:1024 * j_ + 512 * (o_ + 1)],
                        b[:, :], 1.0,
                        B2[:, 1024 * j_ + 512 * o_:1024 * j_ + 512 * (o_ + 1)],
                        op0=ALU.bypass, op1=ALU.add,
                        accum_out=acc_sb[:, 8 + 2 * j_ + o_:9 + 2 * j_ + o_]))
                    p2 = inc("v", c_dve)
                    ck_pre2_slot[oh2] = p2
                    ck_pre2_last = p2
                    pre2_cks[(j, oh2)] = p2

            sq2_cks = {}
            v2_cks = {}
            sd2_cks = {}
            for j in range(4):
                L = 16 * j
                pj = max(pre2_cks[(j, 0)], pre2_cks[(j, 1)])
                wt("v", c_dve, pj)
                em("v", lambda e, j_=j, L_=L: e.tensor_add(
                    lnp[:, L_:L_ + 1], acc_sb[:, 8 + 2 * j_:9 + 2 * j_],
                    acc_sb[:, 9 + 2 * j_:10 + 2 * j_]))
                vsp()
                em("v", lambda e, L_=L: e.tensor_scalar_mul(
                    lnp[:, L_ + 1:L_ + 2], lnp[:, L_:L_ + 1], 1.0 / D))
                inc("v", c_dve)
                wt("c", c_dve, pj)
                if lnp_guard[j]:
                    wt("c", c_dve, lnp_guard[j])
                em("c", lambda e, j_=j, L_=L: e.activation(
                    sqs_v, B1[:, 1024 * j_:1024 * j_ + 512], AF.Square,
                    accum_out=lnp[:, L_ + 7:L_ + 8]))
                em("c", lambda e, j_=j, L_=L: e.activation(
                    sqs_v, B1[:, 1024 * j_ + 512:1024 * (j_ + 1)], AF.Square,
                    accum_out=lnp[:, L_ + 8:L_ + 9]))
                sq2_cks[j] = ainc_safe()
            for j in range(4):
                L = 16 * j
                wt("v", c_act, sq2_cks[j])
                vsp()
                em("v", lambda e, L_=L: e.tensor_scalar(
                    lnp[:, L_ + 2:L_ + 3], lnp[:, L_ + 1:L_ + 2],
                    lnp[:, L_ + 1:L_ + 2], EPS,
                    op0=ALU.mult, op1=ALU.subtract))
                em("v", lambda e, L_=L: e.tensor_add(
                    lnp[:, L_ + 9:L_ + 10], lnp[:, L_ + 7:L_ + 8],
                    lnp[:, L_ + 8:L_ + 9]))
                vsp()
                em("v", lambda e, L_=L: e.tensor_scalar(
                    lnp[:, L_ + 3:L_ + 4], lnp[:, L_ + 9:L_ + 10], 1.0 / D,
                    lnp[:, L_ + 2:L_ + 3],
                    op0=ALU.mult, op1=ALU.subtract))
                v2_cks[j] = vinc_safe()
                lnp_guard[j] = v2_cks[j]
            for j in range(4):
                L = 16 * j
                wt("c", c_dve, v2_cks[j])
                em("c", lambda e, L_=L: e.activation(
                    lnp[:, L_ + 4:L_ + 5], lnp[:, L_ + 3:L_ + 4], AF.Sqrt))
                sd2_cks[j] = ainc_safe()
            for j in range(4):
                L = 16 * j
                wt("v", c_act, sd2_cks[j])
                em("v", lambda e, L_=L: e.reciprocal(
                    lnp[:, L_ + 5:L_ + 6], lnp[:, L_ + 4:L_ + 5]))
                vsp()
                em("v", lambda e, L_=L: e.scalar_tensor_tensor(
                    lnp[:, L_ + 6:L_ + 7], lnp[:, L_ + 1:L_ + 2], -1.0,
                    lnp[:, L_ + 5:L_ + 6],
                    op0=ALU.mult, op1=ALU.mult))
                vsp()
                em("v", lambda e, j_=j, L_=L: e.tensor_scalar(
                    B2[:, 1024 * j_:1024 * (j_ + 1)],
                    B1[:, 1024 * j_:1024 * (j_ + 1)],
                    lnp[:, L_ + 5:L_ + 6], lnp[:, L_ + 6:L_ + 7],
                    op0=ALU.mult, op1=ALU.add))
                vsp()
                ck_ln2_j = inc("v", c_dve)
                wt("g", c_dve, ck_ln2_j)
                em("g", lambda e, q=qc, j_=j: e.dma_start(
                    out=out_h[q * 512 + 128 * j_:q * 512 + 128 * (j_ + 1), :],
                    in_=B2[:, 1024 * j_:1024 * (j_ + 1)]))
                ck_out = inc("g", d_out, 16)

        em("y", lambda e: e.wait_ge(d_out.h, ck_out))

        def runner(lst):
            def f(e):
                with nc.allow_low_precision(reason="bf16 pipeline per problem tolerance"):
                    for fn in lst:
                        fn(e)
            return f

        with nc.Block() as blk:
            blk.tensor(runner(prog["t"]))
            blk.vector(runner(prog["v"]))
            blk.scalar(runner(prog["c"]))
            blk.gpsimd(runner(prog["g"]))
            blk.sync(runner(prog["y"]))

    return nc


def _recip_bf(nc, e, out, in_):
    with nc.allow_low_precision(reason="recip feeds bf16 matmul"):
        return e.reciprocal(out, in_)


def _make_host_constants():
    pos = np.arange(S_LEN, dtype=np.float32)
    freqs = (1.0 / (10000.0 ** (np.arange(0, HD, 2, dtype=np.float32) / HD))
             ).astype(np.float32)
    ang = pos[:, None] * freqs[None, :]
    cosf = np.repeat(np.cos(ang).astype(np.float32), 2, axis=1)
    sinf = np.repeat(np.sin(ang).astype(np.float32), 2, axis=1)
    cosm = np.ascontiguousarray(np.tile(cosf.T, (2, 1)))
    sinm = np.ascontiguousarray(np.tile(sinf.T, (2, 1)))

    ident = np.eye(128, dtype=ml_dtypes.bfloat16)
    rmat = np.zeros((128, 128), dtype=np.float32)
    for blk in (0, 64):
        for i in range(32):
            rmat[blk + 2 * i, blk + 2 * i + 1] = 1.0
            rmat[blk + 2 * i + 1, blk + 2 * i] = -1.0
    rmat = rmat.astype(ml_dtypes.bfloat16)

    hsum = np.zeros((128, 128), dtype=np.float32)
    for c in range(8):
        hsum[0:64, 16 * c + 2 * c] = 1.0
        hsum[64:128, 16 * c + 2 * c + 1] = 1.0
    hsum = hsum.astype(ml_dtypes.bfloat16)

    iu = np.arange(896)[None, :]
    ii = np.arange(128)[:, None]
    masks = (iu >= ii + 384).astype(ml_dtypes.bfloat16)

    onesb = np.ones((1, 128), dtype=ml_dtypes.bfloat16)
    onesf = np.ones((1, 128), dtype=np.float32)
    selm = np.zeros((128, 256), dtype=ml_dtypes.bfloat16)
    for h in range(4):
        selm[32 * h, 64 * h:64 * (h + 1)] = 1.0
    return cosm, sinm, ident, rmat, hsum, masks, onesb, onesf, selm


def kernel(L0_attn_x, L0_attn_wq, L0_attn_wk, L0_attn_wv, L0_attn_wo,
           L0_ffn_W, L0_ffn_Wp, L0_ffn_bp, L0_ffn_sim, L0_ffn_temp,
           L0_ffn_Wd0, L0_ffn_Wu0, L0_ffn_Wd1, L0_ffn_Wu1,
           L0_ffn_Wd2, L0_ffn_Wu2, L0_ffn_Wd3, L0_ffn_Wu3):
    x = np.asarray(L0_attn_x, dtype=np.float32)
    assert x.shape == (8, S_LEN, D)

    wqT = np.ascontiguousarray(np.asarray(L0_attn_wq, np.float32).T)
    wkT = np.ascontiguousarray(np.asarray(L0_attn_wk, np.float32).T)
    wvT = np.ascontiguousarray(np.asarray(L0_attn_wv, np.float32).T)
    woT = np.ascontiguousarray(np.asarray(L0_attn_wo, np.float32).T)
    WT = np.ascontiguousarray(np.asarray(L0_ffn_W, np.float32).T)
    WpT = np.ascontiguousarray(np.asarray(L0_ffn_Wp, np.float32).T)

    Wd = [np.asarray(w, np.float32) for w in
          (L0_ffn_Wd0, L0_ffn_Wd1, L0_ffn_Wd2, L0_ffn_Wd3)]
    Wu = [np.asarray(w, np.float32) for w in
          (L0_ffn_Wu0, L0_ffn_Wu1, L0_ffn_Wu2, L0_ffn_Wu3)]
    WdTp = np.zeros((D, 32), dtype=np.float32)
    WuTp = np.zeros((32, D), dtype=np.float32)
    for e_i, r_e in enumerate((1, 2, 4, 8)):
        WdTp[:, 8 * e_i:8 * e_i + r_e] = Wd[e_i].T
        WuTp[8 * e_i:8 * e_i + r_e, :] = Wu[e_i].T
    simT = np.ascontiguousarray(np.asarray(L0_ffn_sim, np.float32).T)
    tempv = np.asarray(L0_ffn_temp, np.float32).reshape(1, 1)
    bp_bc = np.ascontiguousarray(
        np.tile(np.asarray(L0_ffn_bp, np.float32)[None, :], (128, 1)))

    (cosm, sinm, ident, rmat, hsum, masks, onesb, onesf,
     selm) = _make_host_constants()

    shared = dict(wqT=wqT, wkT=wkT, wvT=wvT, woT=woT, WT=WT, WpT=WpT,
                  WdTp=WdTp, WuTp=WuTp, simT=simT, tempv=tempv, bp_bc=bp_bc,
                  cosm=cosm, sinm=sinm, identb=ident, rmatb=rmat,
                  headsum=hsum, masksb=masks, onesb=onesb, onesf=onesf,
                  selm=selm)

    in_maps = []
    for b in range(8):
        xb = np.ascontiguousarray(x[b])
        m = dict(shared)
        m["x"] = xb
        m["xT"] = np.ascontiguousarray(xb.T)
        in_maps.append(m)

    nc = build_program()
    res = run_bass_kernel_spmd(nc, in_maps, core_ids=list(range(8)))
    return np.stack([np.asarray(r["o"], dtype=np.float32) for r in res.results])
